# version 07
# baseline (speedup 1.0000x reference)
import numpy as np
import concourse.bass as bass
import concourse.mybir as mybir
from concourse.bass_utils import run_bass_kernel_spmd

F32 = mybir.dt.float32
BF16 = mybir.dt.bfloat16
AF = mybir.ActivationFunctionType
ALU = mybir.AluOpType
AX = mybir.AxisListType
AP = bass.AP

NB = 4
QTB = 16
NGQ = 32
SOFT_SCALE = float(2.0 ** -10)
LN_EPS = 1e-5
QN = 4096 * 1024


class Sch:

    def __init__(self):
        self.L = {k: [] for k in ("sync", "gpsimd", "tensor", "vector", "scalar")}
        self.c = {}

    def w(self, eng, sem, v):
        if v > 0:
            self.L[eng].append(lambda e, s=sem, vv=v: e.wait_ge(s, vv))

    def add(self, eng, f, inc=None, n=1):
        if inc is None:
            self.L[eng].append(f)
            return None
        tot = self.c[id(inc)] = self.c.get(id(inc), 0) + n
        self.L[eng].append(lambda e, ff=f, s=inc, nn=n: ff(e).then_inc(s, nn))
        return tot

    def v(self, sem):
        return self.c.get(id(sem), 0)

    def add_sync(self, eng, f, inc):
        tot = self.add(eng, f, inc)
        self.w(eng, inc, tot)
        return tot


def build_nc():
    nc = bass.Bass("TRN2", target_bir_lowering=False)

    xh = nc.dram_tensor("x", [2048, 1024], F32, kind="ExternalInput")
    wqh = nc.dram_tensor("Wq", [1024, 1024], F32, kind="ExternalInput")
    bqh = nc.dram_tensor("bq", [1024], F32, kind="ExternalInput")
    wkh = nc.dram_tensor("Wk", [1024, 1024], F32, kind="ExternalInput")
    bkh = nc.dram_tensor("bk", [1024], F32, kind="ExternalInput")
    wvh = nc.dram_tensor("Wv", [1024, 1024], F32, kind="ExternalInput")
    bvh = nc.dram_tensor("bv", [1024], F32, kind="ExternalInput")
    woh = nc.dram_tensor("Wo", [1024, 1024], F32, kind="ExternalInput")
    boh = nc.dram_tensor("bo", [1024], F32, kind="ExternalInput")
    w1h = nc.dram_tensor("W1", [4096, 1024], F32, kind="ExternalInput")
    b1h = nc.dram_tensor("b1", [4096], F32, kind="ExternalInput")
    w2h = nc.dram_tensor("W2", [1024, 4096], F32, kind="ExternalInput")
    b2h = nc.dram_tensor("b2", [1024], F32, kind="ExternalInput")
    eyeh = nc.dram_tensor("eye", [128, 128], F32, kind="ExternalInput")
    yh = nc.dram_tensor("y", [2048, 1024], F32, kind="ExternalOutput")

    wqt1h = nc.dram_tensor("wqt1", [4096, 1024], BF16, kind="Internal")
    wq2h = nc.dram_tensor("wq2", [1024, 4096], BF16, kind="Internal")
    x1scrh = nc.dram_tensor("x1scr", [2048, 1024], F32, kind="Internal")
    x1bfh = nc.dram_tensor("x1bfscr", [2048, 1024], BF16, kind="Internal")
    recscrh = nc.dram_tensor("recscr", [512], F32, kind="Internal")
    gscrh = nc.dram_tensor("gscr", [256], F32, kind="Internal")
    thrscrh = nc.dram_tensor("thrscr", [8], F32, kind="Internal")

    BASE = 16512
    off = [BASE]

    def alloc(name, shape, dtype, at=None):
        if at is None:
            at = off[0]
            nbytes = shape[-1] * (2 if dtype == BF16 else 4)
            for d in shape[1:-1]:
                nbytes *= d
            off[0] = at + ((nbytes + 31) // 32) * 32
        assert off[0] <= 229344, f"sbuf overflow at {name}: {off[0]}"
        return nc.alloc_sbuf_tensor_at(name, shape, dtype, offset=at)

    eye = alloc("eye", [128, 128], BF16)
    ones = alloc("ones", [1, 128], BF16)
    ones512 = alloc("ones512", [1, 512], BF16)
    bq_row = alloc("bq_row", [1, 1024], BF16)
    bk_row = alloc("bk_row", [1, 1024], BF16)
    bv_row = alloc("bv_row", [1, 1024], BF16)
    bo_row = alloc("bo_row", [1, 1024], BF16)
    b2_row = alloc("b2_row", [1, 1024], BF16)
    thrbc = alloc("thrbc", [128, 8], F32)
    gv = alloc("gv", [1, 16], F32)
    acc1 = alloc("acc1", [128, 1], F32)
    acc2 = alloc("acc2", [128, 1], F32)
    gtmp = alloc("gtmp", [128, 1], F32)
    negsum = alloc("negsum", [128, 1], F32)
    negmean = alloc("negmean", [128, 1], F32)
    varsum = alloc("varsum", [128, 1], F32)
    vareps = alloc("vareps", [128, 1], F32)
    uln = alloc("uln", [128, 1], F32)
    rstd = alloc("rstd", [128, 1], F32)
    warm = alloc("warm", [1, 256], BF16)

    A0 = off[0]

    Wq_sb = alloc("Wq_sb", [128, 8192], BF16)
    Wk_sb = alloc("Wk_sb", [128, 8192], BF16)
    Wv_sb = alloc("Wv_sb", [128, 8192], BF16)
    Wo_sb = alloc("Wo_sb", [128, 8192], BF16)
    Qd = alloc("Qd", [128, 8192], BF16)
    Kd = alloc("Kd", [128, 8192], BF16)
    Vs = alloc("Vs", [64, 8192], BF16)
    A2 = alloc("A2", [128, 8192], BF16)
    VsT = alloc("VsT", [128, 4160], BF16)
    xT = alloc("xT", [128, 4096], BF16)
    x_tc = alloc("x_tc", [128, 2048], F32)
    x_bf = alloc("x_bf", [128, 1024], BF16)
    Qtm = alloc("Qtm", [128, 2048], BF16)
    PT = alloc("PT", [128, 4096], BF16)
    rec_sb = alloc("rec_sb", [1, 512], F32)
    rec_bc = alloc("rec_bc", [64, 512], F32)
    s1 = alloc("s1", [128, 2048], F32)
    x1 = alloc("x1", [128, 2048], F32)
    ATTN_END = off[0]

    off[0] = A0
    WqT1 = alloc("WqT1", [128, 32768], BF16)
    WqT2 = alloc("WqT2", [128, 32768], BF16)
    hT = alloc("hT", [128, 16384], BF16)
    x1T = alloc("x1T", [128, 4096], BF16)
    x1f = alloc("x1f", [128, 2048], F32)
    s2 = alloc("s2", [128, 1024], F32)
    outb = alloc("outb", [128, 1024], F32)
    b1_rows = alloc("b1_rows", [1, 4096], BF16)
    FFN_END = off[0]
    off[0] = ATTN_END
    grow = alloc("grow", [1, 256], F32)
    Wch = alloc("Wch", [128, 1024], F32)
    tq1 = alloc("tq1", [128, 512], BF16)
    tq2 = alloc("tq2", [128, 512], BF16)
    qout = alloc("qout", [128, 512], BF16)
    QNT_END = off[0]
    assert QNT_END <= 229344 and FFN_END <= 229344, (ATTN_END, FFN_END, QNT_END)
    off[0] = max(ATTN_END, FFN_END, QNT_END)

    sch = Sch()

    from contextlib import ExitStack
    with ExitStack() as es:
        E = es.enter_context
        ps_sb = E(nc.psum_tensor("ps_sb", [128, 2048], F32))
        ps_pv = [E(nc.psum_tensor(f"ps_pv{i}", [128, 512], F32)) for i in range(2)]
        ps_tr = E(nc.psum_tensor("ps_tr", [128, 2048], BF16))

        sS = E(nc.semaphore("sS"))
        sG = E(nc.semaphore("sG"))
        sQS = E(nc.semaphore("sQS"))
        sFoldP = [E(nc.semaphore(f"sFold{i}")) for i in range(2)]
        sFIN = E(nc.semaphore("sFIN"))
        sXL = [E(nc.semaphore(f"sXL{i}")) for i in range(2)]
        sX1W = [E(nc.semaphore(f"sX1W{i}")) for i in range(2)]
        sXBF = [E(nc.semaphore(f"sXBF{i}")) for i in range(2)]
        sWQT = E(nc.semaphore("sWQT"))
        sX1T = E(nc.semaphore("sX1T"))
        sX1F = [E(nc.semaphore(f"sX1F{i}")) for i in range(2)]
        sOUT = E(nc.semaphore("sOUT"))
        sXB1 = E(nc.semaphore("sXB1"))
        sTS = E(nc.semaphore("sTS"))
        sEXP = E(nc.semaphore("sEXP"))
        sPVC = E(nc.semaphore("sPVC"))
        sREC = E(nc.semaphore("sREC"))
        sVN = E(nc.semaphore("sVN"))
        sVE = E(nc.semaphore("sVE"))
        sTT = E(nc.semaphore("sTT"))
        sSC = E(nc.semaphore("sSC"))
        blk = E(nc.Block())

        wp = {}

        def W(eng, key):
            if key in wp:
                sem, v = wp[key]
                sch.w(eng, sem, v)

        def mark(key, sem):
            wp[key] = (sem, sch.v(sem))

        g = "gpsimd"; s = "sync"; t = "tensor"; v = "vector"; sc = "scalar"

        ew = sch.add(g, lambda e: e.dma_start(eye[:, :], eyeh.ap()[:, :]), sG, 16)
        wp[("w_eye",)] = (sG, ew)
        for nm, row, bh in (("bq", bq_row, bqh), ("bk", bk_row, bkh),
                            ("bv", bv_row, bvh), ("bo", bo_row, boh),
                            ("b2", b2_row, b2h)):
            bw = sch.add(g, lambda e, d=row, sr=bh: e.dma_start(
                AP(d, 0, [[1024, 1], [1, 1024]]),
                AP(sr, 0, [[1024, 1], [1, 1024]])), sG, 16)
            wp[("w_" + nm,)] = (sG, bw)
        for nm, W_sb, wh in (("wq", Wq_sb, wqh), ("wk", Wk_sb, wkh), ("wv", Wv_sb, wvh)):
            ww = sch.add(g, lambda e, d=W_sb, sr=wh: e.dma_start(
                AP(d, 0, [[8192, 128], [1024, 8], [1, 1024]]),
                AP(sr, 0, [[1024, 128], [131072, 8], [1, 1024]])), sG, 16)
            wp[("w_" + nm,)] = (sG, ww)
        sch.add(g, lambda e: e.dma_start(
            AP(Wo_sb, 0, [[8192, 64], [1024, 8], [1, 1024]]),
            AP(woh, 0, [[1024, 64], [131072, 8], [1, 1024]])), sG, 16)
        ww = sch.add(g, lambda e: e.dma_start(
            AP(Wo_sb, 64 * 8192, [[8192, 64], [1024, 8], [1, 1024]]),
            AP(woh, 65536, [[1024, 64], [131072, 8], [1, 1024]])), sG, 16)
        wp[("w_wo",)] = (sG, ww)
        mark("prologue_g", sG)

        sch.add(v, lambda e: e.memset(ones[0:1, :], 1.0), sVE)
        sch.add(v, lambda e: e.memset(ones512[0:1, :], 1.0), sVE)
        sch.add(v, lambda e: e.memset(VsT[:, :], 1.0), sVE)
        sch.add(v, lambda e: e.memset(acc1[:, :], 0.0), sVE)
        sch.add(v, lambda e: e.memset(acc2[:, :], 0.0), sVE)
        mark("prologue_v", sVE)

        W(sc, "prologue_v")
        sch.add(sc, lambda e: e.activation(warm[0:1, 0:128], ones[0:1, :], AF.Exp), sSC)
        sch.add(sc, lambda e: e.activation(warm[0:1, 128:256], ones[0:1, :], AF.Ln), sSC)

        qsteps = []

        def gamma_step(widx, ch):
            def fs(e, widx=widx, ch=ch):
                if widx == 0:
                    return e.dma_start(Wch[:, :], AP(w1h, ch * 131072,
                                                    [[1024, 128], [1, 1024]]))
                a, cg = ch % 8, ch // 8
                return e.dma_start(Wch[:, :], AP(w2h, a * 524288 + cg * 1024,
                                                 [[4096, 128], [1, 1024]]))
            def fv(ops, widx=widx, ch=ch):
                acc = acc1 if widx == 0 else acc2
                ops.append((v, "sync", lambda e: e.reduce_sum(gtmp[:, :], Wch[:, :], AX.X,
                                                      apply_absolute_value=True), sVE))
                ops.append((v, "plain", lambda e, a=acc: e.tensor_add(a[:, :], a[:, :], gtmp[:, :]), sVE))
            return ("gamma", fs, fv)

        def finalize_step():
            return ("finalize", None, None)

        def quant_step(widx, ch):
            def fs(e, widx=widx, ch=ch):
                if widx == 0:
                    return e.dma_start(Wch[:, :], AP(w1h, ch * 131072,
                                                    [[1024, 128], [1, 1024]]))
                a, cg = ch % 8, ch // 8
                return e.dma_start(Wch[:, :], AP(w2h, a * 524288 + cg * 1024,
                                                 [[4096, 128], [1, 1024]]))
            def fo(e, hh, widx=widx, ch=ch):
                if widx == 0:
                    return e.dma_start(AP(wqt1h, ch * 131072 + hh * 512,
                                          [[1024, 128], [1, 512]]), qout[:, :])
                a, cg = ch % 8, ch // 8
                return e.dma_start(AP(wq2h, a * 524288 + cg * 1024 + hh * 512,
                                      [[4096, 128], [1, 512]]), qout[:, :])
            return ("quant", fs, fo, widx)

        for widx in (0, 1):
            for ch in range(32):
                qsteps.append(gamma_step(widx, ch))
        qsteps.append(finalize_step())
        for widx in (0, 1):
            for ch in range(32):
                qsteps.append(quant_step(widx, ch))
        qstate = {"i": 0, "thr_ready": None, "last_q_dma": None, "last_qv": None}

        def emit_qstep():
            i = qstate["i"]
            if i >= len(qsteps):
                return
            qstate["i"] = i + 1
            st = qsteps[i]
            if st[0] == "gamma":
                if qstate["last_qv"]:
                    W(s, qstate["last_qv"])
                ld = sch.add(s, st[1], sS, 16)
                sch.w(v, sS, ld)
                ops = []
                st[2](ops)
                for eng, kind, f, sem in ops:
                    (sch.add_sync if kind == "sync" else sch.add)(eng, f, sem)
                mark(f"qv_{i}", sVE)
                qstate["last_qv"] = f"qv_{i}" 
            elif st[0] == "finalize":
                sch.w(g, sVE, sch.v(sVE))
                sch.add(g, lambda e: e.dma_start(AP(gscrh, 0, [[1, 128], [1, 1]]),
                                                 acc1[:, :]), sFIN, 16)
                sch.add(g, lambda e: e.dma_start(AP(gscrh, 128, [[1, 128], [1, 1]]),
                                                 acc2[:, :]), sFIN, 16)
                sch.w(g, sFIN, sch.v(sFIN))
                gld = sch.add(g, lambda e: e.dma_start(
                    grow[0:1, :], AP(gscrh, 0, [[256, 1], [1, 256]])), sFIN, 16)
                sch.w(v, sFIN, gld)
                sch.add(v, lambda e: e.reduce_sum(gv[0:1, 0:1], grow[0:1, 0:128], AX.X), sVE)
                sch.add_sync(v, lambda e: e.reduce_sum(gv[0:1, 1:2], grow[0:1, 128:256], AX.X), sVE)
                c1, c2 = 0.5 / QN, 0.5 * 1e-5
                sch.add(v, lambda e: e.tensor_scalar(gv[0:1, 4:5], gv[0:1, 0:1], c1, c2, ALU.mult, ALU.add), sVE)
                sch.add(v, lambda e: e.tensor_scalar(gv[0:1, 5:6], gv[0:1, 0:1], -c1, -c2, ALU.mult, ALU.add), sVE)
                sch.add(v, lambda e: e.tensor_scalar(gv[0:1, 6:7], gv[0:1, 1:2], c1, c2, ALU.mult, ALU.add), sVE)
                thr_v = sch.add(v, lambda e: e.tensor_scalar(gv[0:1, 7:8], gv[0:1, 1:2], -c1, -c2, ALU.mult, ALU.add), sVE)
                sch.w(g, sVE, thr_v)
                sch.add(g, lambda e: e.dma_start(AP(thrscrh, 0, [[8, 1], [1, 4]]),
                                                 gv[0:1, 4:8]), sFIN, 16)
                sch.w(g, sFIN, sch.v(sFIN))
                thr_g = sch.add(g, lambda e: e.dma_start(
                    thrbc[:, 0:4], AP(thrscrh, 0, [[0, 128], [1, 4]])), sFIN, 16)
                qstate["thr_ready"] = thr_g
            else:
                widx = st[3]
                if qstate["last_qv"]:
                    W(s, qstate["last_qv"])
                ld = sch.add(s, st[1], sS, 16)
                sch.w(v, sS, ld)
                if qstate["thr_ready"] is not None:
                    sch.w(v, sFIN, qstate["thr_ready"])
                    qstate["thr_ready"] = None
                cp, cn = 2 * widx, 2 * widx + 1
                for hh in range(2):
                    if qstate["last_q_dma"] is not None:
                        sch.w(v, sQS, qstate["last_q_dma"])
                    sch.add(v, lambda e, cp=cp, hh=hh: e.tensor_scalar(
                        tq1[:, :], Wch[:, hh * 512:hh * 512 + 512],
                        thrbc[:, cp:cp + 1], None, ALU.is_gt), sVE)
                    sch.add_sync(v, lambda e, cn=cn, hh=hh: e.tensor_scalar(
                        tq2[:, :], Wch[:, hh * 512:hh * 512 + 512],
                        thrbc[:, cn:cn + 1], None, ALU.is_lt), sVE)
                    qv = sch.add_sync(v, lambda e: e.tensor_sub(
                        qout[:, :], tq1[:, :], tq2[:, :]), sVE)
                    sch.w(s, sVE, qv)
                    qstate["last_q_dma"] = sch.add(
                        s, lambda e, st=st, hh=hh: st[2](e, hh), sQS, 16)
                mark(f"qv_{i}", sVE)
                qstate["last_qv"] = f"qv_{i}"

        gidx_of = lambda b, qt, gg: (b * QTB + qt) * NGQ + gg

        xtq = []
        xtrn = [0]
        wp.setdefault(("trch", 0, "prev"), None)
        wp.setdefault(("trch", 1, "prev"), None)

        def Wtr(eng, key):
            val = wp.get(key)
            if val:
                sch.w(eng, val[0], val[1])

        def xbf_prefetch(b):
            for dc2 in range(2):
                if b > 0:
                    W(g, ("xtr", b - 1, 6 + dc2))
                ld = sch.add(g, lambda e, b=b, dc=dc2: e.dma_start(
                    AP(x_bf, (dc % 2) * 512, [[1024, 128], [128, 4], [1, 128]]),
                    AP(xh, b * 524288 + dc * 128, [[1024, 128], [131072, 4], [1, 128]])),
                    sXBF[dc2 % 2], 16)
                wp[("xbf", b, dc2)] = (sXBF[dc2 % 2], ld)

        xbf_prefetch(0)
        for b in range(NB):

            def make_xtr_items(b):
                items = []
                for dc in range(8):
                    for tc4 in range(4):
                        def em(dc=dc, tc4=tc4, b=b):
                            i = xtrn[0]; ch = i % 2
                            xtrn[0] += 1
                            W(t, ("xbf", b, dc))
                            if b == 0 and dc == 0 and tc4 == 0:
                                W(t, ("w_eye",))
                                W(t, "prologue_v")
                            if b > 0 and dc == 0 and tc4 == 0:
                                W(t, ("qkv_done", b - 1))
                            Wtr(t, ("trch", ch, "prev"))
                            tm = sch.add(t, lambda e, dc=dc, tc=tc4, ch=ch: e.transpose(
                                ps_tr[:, ch * 1024:ch * 1024 + 128],
                                x_bf[:, (dc % 2) * 512 + tc * 128:(dc % 2) * 512 + tc * 128 + 128],
                                eye[:, :]), sTT)
                            sch.w(v, sTT, tm)
                            ev = sch.add(v, lambda e, dc=dc, tc=tc4, ch=ch: e.tensor_copy(
                                xT[:, dc * 512 + tc * 128:dc * 512 + tc * 128 + 128],
                                ps_tr[:, ch * 1024:ch * 1024 + 128]), sVE)
                            wp[("trch", ch, "prev")] = (sVE, ev)
                            if tc4 == 3:
                                wp[("xtr", b, dc)] = (sTT, sch.v(sTT))
                                if dc == 7:
                                    wp[("xtevacV", b, 7)] = (sVE, sch.v(sVE))
                                if dc + 2 < 8:
                                    sch.w(g, sTT, sch.v(sTT))
                                    ld = sch.add(g, lambda e, b=b, dc=dc + 2: e.dma_start(
                                        AP(x_bf, (dc % 2) * 512, [[1024, 128], [128, 4], [1, 128]]),
                                        AP(xh, b * 524288 + dc * 128,
                                           [[1024, 128], [131072, 4], [1, 128]])),
                                        sXBF[dc % 2], 16)
                                    wp[("xbf", b, dc + 2)] = (sXBF[dc % 2], ld)
                        items.append(em)
                return items

            if b == 0:
                for em in make_xtr_items(0):
                    em()
            else:
                assert not xtq, f"undrained xtr items at block {b}"
            if b + 1 < NB:
                xbf_prefetch(b + 1)
                xtq.extend(make_xtr_items(b + 1))

            for pi, (W_sb, brow, dst, dup) in enumerate((
                    (Wq_sb, bq_row, Qd, True), (Wk_sb, bk_row, Kd, True),
                    (Wv_sb, bv_row, Vs, False))):
                for tc in range(4):
                    for dh in range(2):
                        n = pi * 8 + tc * 2 + dh
                        if b == 0 and tc == 0 and dh == 0:
                            W(t, ("w_" + ("wq", "wk", "wv")[pi],))
                            W(t, ("w_" + ("bq", "bk", "bv")[pi],))
                        W(t, ("xtevacV", b, 7))
                        if n >= 2:
                            W(t, ("qkvev", b, n - 2))
                        elif b > 0:
                            sch.w(t, sVN, b * QTB)
                        sch.add(t, lambda e, brow=brow, dh=dh, n=n: e.matmul(
                            ps_pv[n % 2][:, :], ones[0:1, :], brow[0:1, dh * 512:dh * 512 + 512],
                            start=True, stop=False), None)
                        for fc in range(8):
                            tm = sch.add(t, lambda e, W_sb=W_sb, fc=fc, tc=tc, dh=dh, n=n: e.matmul(
                                ps_pv[n % 2][:, :],
                                xT[:, fc * 512 + tc * 128:fc * 512 + tc * 128 + 128],
                                W_sb[:, fc * 1024 + dh * 512:fc * 1024 + dh * 512 + 512],
                                start=False, stop=(fc == 7)), sTT if fc == 7 else None)
                        wp[("qkvmm", b, pi, tc, dh)] = (sTT, tm)
                        W(v, ("qkvmm", b, pi, tc, dh))
                        prevu = (b, pi, tc - 2) if tc >= 2 else (
                            (b, pi - 1, tc + 2) if pi > 0 else
                            ((b - 1, 2, tc + 2) if b > 0 else None))
                        if prevu is not None and dh == 0:
                            W(v, ("folddma",) + prevu)
                        ev = sch.add(v, lambda e, tc=tc, dh=dh, n=n: e.tensor_copy(
                            Qtm[:, (tc % 2) * 1024 + dh * 512:(tc % 2) * 1024 + dh * 512 + 512],
                            ps_pv[n % 2][:, :]), sVE)
                        wp[("qkvev", b, n)] = (sVE, ev)
                        wp[("qkvevac", b, pi, tc, dh)] = (sVE, ev)
                    W(s, ("qkvevac", b, pi, tc, 1))
                    if pi == 2:
                        pass
                    sch.add(s, lambda e, dst=dst, tc=tc: e.dma_start(
                        AP(dst, 16 * tc * 8192, [[8192, 16], [1024, 8], [1, 1024]]),
                        AP(Qtm, (tc % 2) * 1024, [[2048, 128], [1, 1024]])), sFoldP[tc % 2], 16)
                    if dup:
                        fd = sch.add(s, lambda e, dst=dst, tc=tc: e.dma_start(
                            AP(dst, (64 + 16 * tc) * 8192, [[8192, 16], [1024, 8], [1, 1024]]),
                            AP(Qtm, (tc % 2) * 1024, [[2048, 128], [1, 1024]])), sFoldP[tc % 2], 16)
                    else:
                        fd = sch.v(sFoldP[tc % 2])
                    wp[("folddma", b, pi, tc)] = (sFoldP[tc % 2], fd)
            mark(("qkv_done", b), sTT)
            wp[("qkvevac_last", b)] = (sVE, sch.v(sVE))
            wp[("folds", b)] = (sFoldP[0], sch.v(sFoldP[0]))
            wp[("folds1", b)] = (sFoldP[1], sch.v(sFoldP[1]))

            for cc in range(64):
                ch = cc % 2
                if cc < 2:
                    W(t, ("folds", b))
                    W(t, ("folds1", b))
                Wtr(t, ("trch", ch, "prev"))
                tm = sch.add(t, lambda e, cc=cc, ch=ch: e.transpose(
                    ps_tr[:, ch * 1024:ch * 1024 + 64],
                    Vs[0:64, cc * 128:cc * 128 + 128], eye[0:64, 0:64]), sTT)
                eng, sem = ((sc, sSC) if ch == 0 else (v, sVE))
                sch.w(eng, sTT, tm)
                ev = sch.add(eng, lambda e, cc=cc, ch=ch: (
                    e.copy(VsT[:, 65 * cc:65 * cc + 64],
                           ps_tr[:, ch * 1024:ch * 1024 + 64])
                    if ch == 0 else
                    e.tensor_copy(VsT[:, 65 * cc:65 * cc + 64],
                                  ps_tr[:, ch * 1024:ch * 1024 + 64])), sem)
                wp[("trch", ch, "prev")] = (sem, ev)
            wp[("vst", b)] = (sVE, sch.v(sVE))
            wp[("vstS", b)] = (sSC, sch.v(sSC))

            PVLAG = 2
            for qt in range(QTB):
                qtg = b * QTB + qt
                for gg in range(NGQ + PVLAG):
                    gi = gidx_of(b, qt, gg)
                    if gg < NGQ:
                        if qt == 0 and gg == 0:
                            sch.w(t, sVE, wp[("vst", b)][1])
                            sch.w(t, sSC, wp[("vstS", b)][1])
                            W(t, ("folds", b))
                            W(t, ("folds1", b))
                            if b > 0:
                                W(t, ("opev", b - 1, 7))
                        if gi >= 2:
                            sch.w(t, sEXP, gi - 1)
                        hh = (gg % 2) * 1024
                        cc = 2 * gg
                        sch.add(t, lambda e, cc=cc, qt=qt, hh=hh: e.matmul(
                            ps_sb[:, hh:hh + 512],
                            Kd[0:64, cc * 128:cc * 128 + 128],
                            Qd[0:64, qt * 512:qt * 512 + 512],
                            start=True, stop=True, tile_position=(0, 0)), None)
                        ts = sch.add(t, lambda e, cc=cc, qt=qt, hh=hh: e.matmul(
                            ps_sb[:, hh + 512:hh + 1024],
                            Kd[64:128, (cc + 1) * 128:(cc + 1) * 128 + 128],
                            Qd[64:128, qt * 512:qt * 512 + 512],
                            start=True, stop=True, tile_position=(64, 0)), sTS)
                        sch.w(sc, sTS, ts)
                        if gi >= 3:
                            sch.w(sc, sPVC, gi - 2)
                        ex = sch.add(sc, lambda e, gg=gg: e.activation(
                            PT[:, (gg % 4) * 1024:(gg % 4) * 1024 + 1024],
                            ps_sb[:, (gg % 2) * 1024:(gg % 2) * 1024 + 1024],
                            AF.Exp, scale=SOFT_SCALE), sEXP)
                        wp[("exp", gi)] = (sEXP, ex)
                    pg = gg - PVLAG
                    if pg >= 0:
                        pgi = gidx_of(b, qt, pg)
                        W(t, ("exp", pgi))
                        if pg == 0 and qtg >= 2:
                            sch.w(t, sVN, qtg - 1)
                        if pg == 0 and qt < 2:
                            W(t, ("qkvev", b, 23))
                        sch.add(t, lambda e, pg=pg, qtg=qtg: e.matmul(
                            ps_pv[qtg % 2][0:65, :],
                            VsT[:, (2 * pg) * 65:(2 * pg) * 65 + 65],
                            PT[:, (pg % 4) * 1024:(pg % 4) * 1024 + 512],
                            start=(pg == 0), stop=False), None)
                        sch.add(t, lambda e, pg=pg, qtg=qtg: e.matmul(
                            ps_pv[qtg % 2][0:65, :],
                            VsT[:, (2 * pg + 1) * 65:(2 * pg + 1) * 65 + 65],
                            PT[:, (pg % 4) * 1024 + 512:(pg % 4) * 1024 + 1024],
                            start=False, stop=(pg == NGQ - 1)), sPVC)
                sch.w(v, sPVC, NGQ * (qtg + 1))
                if qtg >= 1:
                    W(v, ("recwr", qtg - 1))
                rc = sch.add(v, lambda e, qtg=qtg: e.reciprocal(
                    rec_sb[0:1, :], ps_pv[qtg % 2][64:65, :]), sREC)
                sch.w(g, sREC, rc)
                wr = sch.add(g, lambda e: e.dma_start(
                    AP(recscrh, 0, [[512, 1], [1, 512]]), rec_sb[0:1, :]), sG, 16)
                wp[("recwr", qtg)] = (sG, wr)
                sch.w(g, sG, wr)
                if qtg >= 1:
                    sch.w(g, sVN, qtg)
                rd = sch.add(g, lambda e, qtg=qtg: e.dma_start(
                    rec_bc[0:64, 0:512],
                    AP(recscrh, 0, [[0, 64], [1, 512]])), sG, 16)
                sch.w(v, sG, rd)
                if qt == 0 and b > 0:
                    W(v, ("outproj", b - 1))
                vn = sch.add(v, lambda e, qtg=qtg, qt=qt: e.scalar_tensor_tensor(
                    A2[0:64, qt * 512:qt * 512 + 512], ps_pv[qtg % 2][0:64, :], 1.0,
                    rec_bc[0:64, 0:512],
                    ALU.mult, ALU.mult), sVN)
                sch.w(g, sVN, vn)
                ad = sch.add(g, lambda e, qt=qt: e.dma_start(
                    A2[64:128, qt * 512:qt * 512 + 512],
                    A2[0:64, qt * 512:qt * 512 + 512]), sG, 16)
                wp[("a2dup", qtg)] = (sG, ad)
                emit_qstep()
                emit_qstep()
                for _ in range(3):
                    if xtq:
                        xtq.pop(0)()

            while xtq:
                xtq.pop(0)()
            for tc in range(4):
                if b > 0 or tc >= 2:
                    bb, tt = (b, tc - 2) if tc >= 2 else (b - 1, tc + 2)
                    W(s, ("s1evac", bb, tt))
                ld = sch.add(s, lambda e, b=b, tc=tc: e.dma_start(
                    x_tc[:, (tc % 2) * 1024:(tc % 2) * 1024 + 1024],
                    AP(xh, (b * 512 + tc * 128) * 1024, [[1024, 128], [1, 1024]])),
                    sXL[tc % 2], 16)
                wp[("xload", b, tc)] = (sXL[tc % 2], ld)
                for dh in range(2):
                    n = tc * 2 + dh
                    u = (n % 2) * 2
                    if n == 0:
                        if b == 0:
                            W(t, ("w_wo",))
                            W(t, ("w_bo",))
                        sch.w(t, sVN, b * QTB + QTB)
                        sch.w(t, sG, wp[("a2dup", b * QTB + QTB - 1)][1])
                    if n >= 2:
                        W(t, ("opev", b, n - 2))
                    sch.add(t, lambda e, dh=dh, u=u: e.matmul(
                        ps_sb[:, u * 512:u * 512 + 512], ones[0:1, :],
                        bo_row[0:1, dh * 512:dh * 512 + 512],
                        start=True, stop=False, tile_position=(0, 0)), None)
                    for m in range(8):
                        for par in range(2):
                            r = 2 * m + par
                            lo = par * 64 * 8192
                            tm = sch.add(t, lambda e, r=r, m=m, tc=tc, dh=dh, lo=lo, par=par, u=u: e.matmul(
                                ps_sb[:, (u + par) * 512:(u + par) * 512 + 512],
                                AP(A2, lo + 2048 * tc + r, [[8192, 64], [16, 128]]),
                                AP(Wo_sb, lo + m * 1024 + dh * 512, [[8192, 64], [1, 512]]),
                                start=(par == 1 and m == 0), stop=(m == 7),
                                tile_position=(64 * par, 0)), sTT if r == 15 else None)
                    wp[("opmm", b, tc, dh)] = (sTT, tm)
                    W(v, ("opmm", b, tc, dh))
                    sch.w(v, wp[("xload", b, tc)][0], wp[("xload", b, tc)][1])
                    sch.add_sync(v, lambda e, tc=tc, dh=dh, u=u: e.scalar_tensor_tensor(
                        s1[:, (tc % 2) * 1024 + dh * 512:(tc % 2) * 1024 + dh * 512 + 512],
                        ps_sb[:, u * 512:u * 512 + 512], 1.0,
                        x_tc[:, (tc % 2) * 1024 + dh * 512:(tc % 2) * 1024 + dh * 512 + 512],
                        ALU.mult, ALU.add), sVE)
                    ev = sch.add_sync(v, lambda e, tc=tc, dh=dh, u=u: e.scalar_tensor_tensor(
                        s1[:, (tc % 2) * 1024 + dh * 512:(tc % 2) * 1024 + dh * 512 + 512],
                        ps_sb[:, (u + 1) * 512:(u + 1) * 512 + 512], 1.0,
                        s1[:, (tc % 2) * 1024 + dh * 512:(tc % 2) * 1024 + dh * 512 + 512],
                        ALU.mult, ALU.add), sVE)
                    wp[("opev", b, n)] = (sVE, ev)
                    wp[("s1evac", b, tc)] = (sVE, ev)
                if tc == 3:
                    mark(("outproj", b), sTT)
                sl = (tc % 2) * 1024
                if b > 0 or tc >= 2:
                    bb, tt = (b, tc - 2) if tc >= 2 else (b - 1, tc + 2)
                    W(v, ("x1w", bb, tt))
                sch.add_sync(v, lambda e, sl=sl: e.reduce_sum(
                    negsum[:, :], s1[:, sl:sl + 1024], AX.X, negate=True), sVE)
                sch.add_sync(v, lambda e: e.tensor_scalar_mul(
                    negmean[:, :], negsum[:, :], 1.0 / 1024.0), sVE)
                sch.add_sync(v, lambda e, sl=sl: e.tensor_scalar(
                    s1[:, sl:sl + 1024], s1[:, sl:sl + 1024],
                    negmean[:, :], None, ALU.add), sVE)
                sch.add_sync(v, lambda e, sl=sl: e.scalar_tensor_tensor(
                    x1[:, sl:sl + 1024], s1[:, sl:sl + 1024], 1.0,
                    s1[:, sl:sl + 1024], ALU.mult, ALU.mult,
                    accum_out=varsum[:, :]), sVE)
                lv = sch.add(v, lambda e: e.tensor_scalar(
                    vareps[:, :], varsum[:, :], 1.0 / 1024.0, LN_EPS,
                    ALU.mult, ALU.add), sVE)
                sch.w(sc, sVE, lv)
                sch.add(sc, lambda e: e.activation(uln[:, :], vareps[:, :], AF.Ln), None)
                sr = sch.add(sc, lambda e: e.activation(
                    rstd[:, :], uln[:, :], AF.Exp, scale=-0.5), sSC)
                sch.w(v, sSC, sr)
                x1v = sch.add(v, lambda e, sl=sl: e.tensor_scalar(
                    x1[:, sl:sl + 1024], s1[:, sl:sl + 1024],
                    rstd[:, :], None, ALU.mult), sVE)
                sch.w(s, sVE, x1v)
                xw = sch.add(s, lambda e, b=b, tc=tc, sl=sl: e.dma_start(
                    AP(x1scrh, (b * 512 + tc * 128) * 1024, [[1024, 128], [1, 1024]]),
                    x1[:, sl:sl + 1024]), sX1W[tc % 2], 16)
                wp[("x1w", b, tc)] = (sX1W[tc % 2], xw)
                if tc == 3:
                    sch.w(g, sX1W[0], sch.v(sX1W[0]))
                    sch.w(g, sX1W[1], sch.v(sX1W[1]))
                    cw = sch.add(g, lambda e, b=b: e.dma_start(
                        AP(x1bfh, b * 524288, [[1024, 512], [1, 1024]]),
                        AP(x1scrh, b * 524288, [[1024, 512], [1, 1024]])), sXB1, 16)
                emit_qstep()
                emit_qstep()

        while qstate["i"] < len(qsteps):
            emit_qstep()

        ATT_T = sch.v(sTT); ATT_V = sch.v(sVE); ATT_G = sch.v(sG); ATT_S = sch.v(sS)

        sch.w(s, sTT, ATT_T)
        sch.w(s, sVE, ATT_V)
        sch.w(s, sG, ATT_G)
        sch.w(s, sVN, NB * QTB)
        sch.w(s, sS, ATT_S)
        sch.w(s, sQS, sch.v(sQS))
        sch.w(s, sFoldP[0], sch.v(sFoldP[0]))
        sch.w(s, sFoldP[1], sch.v(sFoldP[1]))
        sch.w(s, sFIN, sch.v(sFIN))
        for i in range(2):
            sch.w(s, sXL[i], sch.v(sXL[i]))
            sch.w(s, sX1W[i], sch.v(sX1W[i]))
            sch.w(s, sXBF[i], sch.v(sXBF[i]))
        for dc in range(8):
            sch.add(s, lambda e, dc=dc: e.dma_start_transpose(
                WqT1[:, dc * 4096:dc * 4096 + 4096],
                AP(wqt1h, dc * 128, [[1024, 4096], [1, 128]])), sWQT, 16)
        for oc in range(32):
            sch.add(s, lambda e, oc=oc: e.dma_start_transpose(
                WqT2[:, oc * 1024:oc * 1024 + 1024],
                AP(wq2h, oc * 128, [[4096, 1024], [1, 128]])), sWQT, 16)
        wp[("wqt",)] = (sWQT, sch.v(sWQT))

        sch.w(g, sVE, ATT_V)
        sch.add(g, lambda e: e.dma_start(
            AP(b1_rows, 0, [[4096, 1], [1, 4096]]),
            AP(b1h, 0, [[4096, 1], [1, 4096]])), sXB1, 16)
        wp[("x1bf",)] = (sXB1, sch.v(sXB1))

        for b in range(NB):
            W(s, ("x1bf",))
            if b > 0:
                W(s, ("w1done", b - 1))
            for dc in range(8):
                sch.add(s, lambda e, b=b, dc=dc: e.dma_start_transpose(
                    x1T[:, dc * 512:dc * 512 + 512],
                    AP(x1bfh, b * 524288 + dc * 128, [[1024, 512], [1, 128]])), sX1T, 16)
            wp[("x1T", b)] = (sX1T, sch.v(sX1T))

            for oc in range(32):
                W(t, ("x1T", b))
                W(t, ("wqt",))
                if oc >= 2:
                    W(t, ("hevac", b, oc - 2))
                elif b > 0:
                    W(t, ("hevac", b - 1, oc + 30))
                sch.add(t, lambda e, oc=oc: e.matmul(
                    ps_pv[oc % 2][:, :], b1_rows[0:1, oc * 128:oc * 128 + 128],
                    ones512[0:1, :], start=True, stop=False), None)
                for dc in range(8):
                    tm = sch.add(t, lambda e, oc=oc, dc=dc: e.matmul(
                        ps_pv[oc % 2][:, :],
                        WqT1[:, dc * 4096 + oc * 128:dc * 4096 + oc * 128 + 128],
                        x1T[:, dc * 512:dc * 512 + 512],
                        start=False, stop=(dc == 7)), sTT if dc == 7 else None)
                wp[("w1mm", b, oc)] = (sTT, tm)
                W(v, ("w1mm", b, oc))
                if b > 0 and oc < 2:
                    W(v, ("w2done", b - 1))
                ev = sch.add(v, lambda e, oc=oc: e.tensor_scalar_max(
                    hT[:, oc * 512:oc * 512 + 512], ps_pv[oc % 2][:, :], 0.0), sVE)
                wp[("hevac", b, oc)] = (sVE, ev)
            mark(("w1done", b), sTT)

            for tsub in range(4):
                if b > 0 or tsub >= 2:
                    bb, tt = (b, tsub - 2) if tsub >= 2 else (b - 1, tsub + 2)
                    W(s, ("s2done", bb, tt))
                ld = sch.add(s, lambda e, b=b, tsub=tsub: e.dma_start(
                    x1f[:, (tsub % 2) * 1024:(tsub % 2) * 1024 + 1024],
                    AP(x1scrh, (b * 512 + tsub * 128) * 1024, [[1024, 128], [1, 1024]])),
                    sX1F[tsub % 2], 16)
                wp[("x1fload", b, tsub)] = (sX1F[tsub % 2], ld)
                for dh in range(2):
                    W(t, ("hevac", b, 31))
                    if b * 4 + tsub >= 2:
                        pb, pt = divmod(b * 4 + tsub - 2, 4)
                        W(t, ("s2evac", pb, pt))
                    sch.add(t, lambda e, tsub=tsub, dh=dh: e.matmul(
                        ps_sb[:, (tsub % 2) * 1024 + dh * 512:(tsub % 2) * 1024 + dh * 512 + 512],
                        ones[0:1, :], b2_row[0:1, dh * 512:dh * 512 + 512],
                        start=True, stop=False), None)
                    for oc in range(32):
                        tm = sch.add(t, lambda e, tsub=tsub, dh=dh, oc=oc: e.matmul(
                            ps_sb[:, (tsub % 2) * 1024 + dh * 512:(tsub % 2) * 1024 + dh * 512 + 512],
                            hT[:, oc * 512 + tsub * 128:oc * 512 + tsub * 128 + 128],
                            WqT2[:, oc * 1024 + dh * 512:oc * 1024 + dh * 512 + 512],
                            start=False, stop=(oc == 31)),
                            sTT if (dh == 1 and oc == 31) else None)
                wp[("w2mm", b, tsub)] = (sTT, tm)
                W(v, ("w2mm", b, tsub))
                sch.w(v, sX1F[tsub % 2], ld)
                if b * 4 + tsub >= 1:
                    pb, pt = divmod(b * 4 + tsub - 1, 4)
                    W(v, ("outdma", pb, pt))
                sl = 0
                xsl = (tsub % 2) * 1024
                ev = sch.add_sync(v, lambda e, tsub=tsub, xsl=xsl: e.scalar_tensor_tensor(
                    s2[:, 0:1024], ps_sb[:, (tsub % 2) * 1024:(tsub % 2) * 1024 + 1024], 1.0,
                    x1f[:, xsl:xsl + 1024], ALU.mult, ALU.add), sVE)
                wp[("s2evac", b, tsub)] = (sVE, ev)
                wp[("s2done", b, tsub)] = (sVE, ev)
                sch.add_sync(v, lambda e: e.reduce_sum(
                    negsum[:, :], s2[:, 0:1024], AX.X, negate=True), sVE)
                sch.add_sync(v, lambda e: e.tensor_scalar_mul(
                    negmean[:, :], negsum[:, :], 1.0 / 1024.0), sVE)
                sch.add_sync(v, lambda e: e.tensor_scalar(
                    s2[:, 0:1024], s2[:, 0:1024],
                    negmean[:, :], None, ALU.add), sVE)
                sch.add_sync(v, lambda e: e.scalar_tensor_tensor(
                    outb[:, 0:1024], s2[:, 0:1024], 1.0,
                    s2[:, 0:1024], ALU.mult, ALU.mult,
                    accum_out=varsum[:, :]), sVE)
                lv = sch.add(v, lambda e: e.tensor_scalar(
                    vareps[:, :], varsum[:, :], 1.0 / 1024.0, LN_EPS,
                    ALU.mult, ALU.add), sVE)
                sch.w(sc, sVE, lv)
                sch.add(sc, lambda e: e.activation(uln[:, :], vareps[:, :], AF.Ln), None)
                sr = sch.add(sc, lambda e: e.activation(
                    rstd[:, :], uln[:, :], AF.Exp, scale=-0.5), sSC)
                sch.w(v, sSC, sr)
                ov = sch.add(v, lambda e: e.tensor_scalar(
                    outb[:, 0:1024], s2[:, 0:1024],
                    rstd[:, :], None, ALU.mult), sVE)
                wp[("outw", b, tsub)] = (sVE, ov)
                sch.w(s, sVE, ov)
                od = sch.add(s, lambda e, b=b, tsub=tsub: e.dma_start(
                    AP(yh, (b * 512 + tsub * 128) * 1024, [[1024, 128], [1, 1024]]),
                    outb[:, 0:1024]), sOUT, 16)
                wp[("outdma", b, tsub)] = (sOUT, od)
            mark(("w2done", b), sTT)

        sch.w(s, sOUT, sch.v(sOUT))
        sch.w(s, sX1T, sch.v(sX1T))
        for i in range(2):
            sch.w(s, sX1F[i], sch.v(sX1F[i]))
        sch.w(g, sG, sch.v(sG))
        sch.w(g, sXB1, sch.v(sXB1))

        @blk.sync
        def _(e):
            for f in sch.L["sync"]:
                f(e)

        @blk.gpsimd
        def _(e):
            for f in sch.L["gpsimd"]:
                f(e)

        @blk.tensor
        def _(e):
            for f in sch.L["tensor"]:
                f(e)

        @blk.vector
        def _(e):
            for f in sch.L["vector"]:
                f(e)

        @blk.scalar
        def _(e):
            for f in sch.L["scalar"]:
                f(e)

    return nc


def kernel(L0_attn_x, L0_attn_Wq, L0_attn_bq, L0_attn_Wk, L0_attn_bk,
           L0_attn_Wv, L0_attn_bv, L0_attn_Wo, L0_attn_bo,
           L0_ffn_W1, L0_ffn_b1, L0_ffn_W2, L0_ffn_b2):
    x = np.ascontiguousarray(np.asarray(L0_attn_x, dtype=np.float32)).reshape(16384, 1024)
    com = dict(
        Wq=np.ascontiguousarray(np.asarray(L0_attn_Wq, np.float32)),
        bq=np.ascontiguousarray(np.asarray(L0_attn_bq, np.float32)),
        Wk=np.ascontiguousarray(np.asarray(L0_attn_Wk, np.float32)),
        bk=np.ascontiguousarray(np.asarray(L0_attn_bk, np.float32)),
        Wv=np.ascontiguousarray(np.asarray(L0_attn_Wv, np.float32)),
        bv=np.ascontiguousarray(np.asarray(L0_attn_bv, np.float32)),
        Wo=np.ascontiguousarray(np.asarray(L0_attn_Wo, np.float32)),
        bo=np.ascontiguousarray(np.asarray(L0_attn_bo, np.float32)),
        W1=np.ascontiguousarray(np.asarray(L0_ffn_W1, np.float32)),
        b1=np.ascontiguousarray(np.asarray(L0_ffn_b1, np.float32)),
        W2=np.ascontiguousarray(np.asarray(L0_ffn_W2, np.float32)),
        b2=np.ascontiguousarray(np.asarray(L0_ffn_b2, np.float32)),
        eye=np.eye(128, dtype=np.float32),
    )
    nc = build_nc()
    in_maps = [dict(com, x=np.ascontiguousarray(x[c * 2048:(c + 1) * 2048]))
               for c in range(8)]
    res = run_bass_kernel_spmd(nc, in_maps, core_ids=list(range(8)))
    out = np.concatenate([np.asarray(r["y"], np.float32) for r in res.results], axis=0)
    return out.reshape(2, 8192, 1024)


# version 08
# speedup vs baseline: 1.1483x; 1.1483x over previous
import numpy as np
import concourse.bass as bass
import concourse.mybir as mybir
from concourse.bass_utils import run_bass_kernel_spmd

F32 = mybir.dt.float32
BF16 = mybir.dt.bfloat16
AF = mybir.ActivationFunctionType
ALU = mybir.AluOpType
AX = mybir.AxisListType
AP = bass.AP

NB = 4
QTB = 16
NGQ = 32
SOFT_SCALE = float(2.0 ** -10)
LN_EPS = 1e-5
QN = 4096 * 1024


class Sch:

    def __init__(self):
        self.L = {k: [] for k in ("sync", "gpsimd", "tensor", "vector", "scalar")}
        self.c = {}

    def w(self, eng, sem, v):
        if v > 0:
            self.L[eng].append(lambda e, s=sem, vv=v: e.wait_ge(s, vv))

    def add(self, eng, f, inc=None, n=1):
        if inc is None:
            self.L[eng].append(f)
            return None
        tot = self.c[id(inc)] = self.c.get(id(inc), 0) + n
        self.L[eng].append(lambda e, ff=f, s=inc, nn=n: ff(e).then_inc(s, nn))
        return tot

    def v(self, sem):
        return self.c.get(id(sem), 0)

    def add_sync(self, eng, f, inc):
        tot = self.add(eng, f, inc)
        self.w(eng, inc, tot)
        return tot


def build_nc():
    nc = bass.Bass("TRN2", target_bir_lowering=False)

    xh = nc.dram_tensor("x", [2048, 1024], F32, kind="ExternalInput")
    wqh = nc.dram_tensor("Wq", [1024, 1024], F32, kind="ExternalInput")
    bqh = nc.dram_tensor("bq", [1024], F32, kind="ExternalInput")
    wkh = nc.dram_tensor("Wk", [1024, 1024], F32, kind="ExternalInput")
    bkh = nc.dram_tensor("bk", [1024], F32, kind="ExternalInput")
    wvh = nc.dram_tensor("Wv", [1024, 1024], F32, kind="ExternalInput")
    bvh = nc.dram_tensor("bv", [1024], F32, kind="ExternalInput")
    woh = nc.dram_tensor("Wo", [1024, 1024], F32, kind="ExternalInput")
    boh = nc.dram_tensor("bo", [1024], F32, kind="ExternalInput")
    w1h = nc.dram_tensor("W1", [4096, 1024], F32, kind="ExternalInput")
    b1h = nc.dram_tensor("b1", [4096], F32, kind="ExternalInput")
    w2h = nc.dram_tensor("W2", [1024, 4096], F32, kind="ExternalInput")
    b2h = nc.dram_tensor("b2", [1024], F32, kind="ExternalInput")
    eyeh = nc.dram_tensor("eye", [128, 128], F32, kind="ExternalInput")
    yh = nc.dram_tensor("y", [2048, 1024], F32, kind="ExternalOutput")

    wqt1h = nc.dram_tensor("wqt1", [4096, 1024], BF16, kind="Internal")
    wq2h = nc.dram_tensor("wq2", [1024, 4096], BF16, kind="Internal")
    x1scrh = nc.dram_tensor("x1scr", [2048, 1024], F32, kind="Internal")
    x1bfh = nc.dram_tensor("x1bfscr", [2048, 1024], BF16, kind="Internal")
    recscrh = nc.dram_tensor("recscr", [512], F32, kind="Internal")
    gscrh = nc.dram_tensor("gscr", [256], F32, kind="Internal")
    thrscrh = nc.dram_tensor("thrscr", [8], F32, kind="Internal")

    BASE = 16512
    off = [BASE]

    def alloc(name, shape, dtype, at=None):
        if at is None:
            at = off[0]
            nbytes = shape[-1] * (2 if dtype == BF16 else 4)
            for d in shape[1:-1]:
                nbytes *= d
            off[0] = at + ((nbytes + 31) // 32) * 32
        assert off[0] <= 229344, f"sbuf overflow at {name}: {off[0]}"
        return nc.alloc_sbuf_tensor_at(name, shape, dtype, offset=at)

    eye = alloc("eye", [128, 128], BF16)
    ones = alloc("ones", [1, 128], BF16)
    ones512 = alloc("ones512", [1, 512], BF16)
    bq_row = alloc("bq_row", [1, 1024], BF16)
    bk_row = alloc("bk_row", [1, 1024], BF16)
    bv_row = alloc("bv_row", [1, 1024], BF16)
    bo_row = alloc("bo_row", [1, 1024], BF16)
    b2_row = alloc("b2_row", [1, 1024], BF16)
    thrbc = alloc("thrbc", [128, 8], F32)
    gv = alloc("gv", [1, 16], F32)
    acc1 = alloc("acc1", [128, 1], F32)
    acc2 = alloc("acc2", [128, 1], F32)
    gtmp = alloc("gtmp", [128, 1], F32)
    negsum = alloc("negsum", [128, 1], F32)
    negmean = alloc("negmean", [128, 1], F32)
    varsum = alloc("varsum", [128, 1], F32)
    vareps = alloc("vareps", [128, 1], F32)
    uln = alloc("uln", [128, 1], F32)
    rstd = alloc("rstd", [128, 1], F32)
    warm = alloc("warm", [1, 256], BF16)

    A0 = off[0]

    Wq_sb = alloc("Wq_sb", [128, 8192], BF16)
    Wk_sb = alloc("Wk_sb", [128, 8192], BF16)
    Wv_sb = alloc("Wv_sb", [128, 8192], BF16)
    Wo_sb = alloc("Wo_sb", [128, 8192], BF16)
    Qd = alloc("Qd", [128, 8192], BF16)
    Kd = alloc("Kd", [128, 8192], BF16)
    Vs = alloc("Vs", [64, 8192], BF16)
    A2 = alloc("A2", [128, 8192], BF16)
    VsT = alloc("VsT", [128, 4160], BF16)
    xT = alloc("xT", [128, 4096], BF16)
    x_tc = alloc("x_tc", [128, 2048], F32)
    x_bf = alloc("x_bf", [128, 1024], BF16)
    Qtm = alloc("Qtm", [128, 2048], BF16)
    PT = alloc("PT", [128, 4096], BF16)
    rec_sb = alloc("rec_sb", [1, 512], F32)
    rec_bc = alloc("rec_bc", [64, 512], F32)
    s1 = alloc("s1", [128, 2048], F32)
    x1 = alloc("x1", [128, 2048], F32)
    ATTN_END = off[0]

    off[0] = A0
    WqT1 = alloc("WqT1", [128, 32768], BF16)
    WqT2 = alloc("WqT2", [128, 32768], BF16)
    hT = alloc("hT", [128, 16384], BF16)
    x1T = alloc("x1T", [128, 4096], BF16)
    x1f = alloc("x1f", [128, 2048], F32)
    s2 = alloc("s2", [128, 1024], F32)
    outb = alloc("outb", [128, 1024], F32)
    b1_rows = alloc("b1_rows", [1, 4096], BF16)
    FFN_END = off[0]
    off[0] = ATTN_END
    grow = alloc("grow", [1, 256], F32)
    Wch = alloc("Wch", [128, 1024], F32)
    tq1 = alloc("tq1", [128, 512], BF16)
    tq2 = alloc("tq2", [128, 512], BF16)
    qout = alloc("qout", [128, 512], BF16)
    QNT_END = off[0]
    assert QNT_END <= 229344 and FFN_END <= 229344, (ATTN_END, FFN_END, QNT_END)
    off[0] = max(ATTN_END, FFN_END, QNT_END)

    sch = Sch()

    from contextlib import ExitStack
    with ExitStack() as es:
        E = es.enter_context
        ps_sb = E(nc.psum_tensor("ps_sb", [128, 2048], F32))
        ps_pv = [E(nc.psum_tensor(f"ps_pv{i}", [128, 512], F32)) for i in range(2)]
        ps_tr = E(nc.psum_tensor("ps_tr", [128, 2048], BF16))

        sS = E(nc.semaphore("sS"))
        sG = E(nc.semaphore("sG"))
        sQS = E(nc.semaphore("sQS"))
        sFoldP = [E(nc.semaphore(f"sFold{i}")) for i in range(2)]
        sFIN = E(nc.semaphore("sFIN"))
        sXL = [E(nc.semaphore(f"sXL{i}")) for i in range(2)]
        sX1W = [E(nc.semaphore(f"sX1W{i}")) for i in range(2)]
        sXBF = [E(nc.semaphore(f"sXBF{i}")) for i in range(2)]
        sWQT = E(nc.semaphore("sWQT"))
        sX1T = E(nc.semaphore("sX1T"))
        sX1F = [E(nc.semaphore(f"sX1F{i}")) for i in range(2)]
        sOUT = E(nc.semaphore("sOUT"))
        sXB1 = E(nc.semaphore("sXB1"))
        sTS = E(nc.semaphore("sTS"))
        sEXP = E(nc.semaphore("sEXP"))
        sPVC = E(nc.semaphore("sPVC"))
        sREC = E(nc.semaphore("sREC"))
        sVN = E(nc.semaphore("sVN"))
        sVE = E(nc.semaphore("sVE"))
        sTT = E(nc.semaphore("sTT"))
        sSC = E(nc.semaphore("sSC"))
        blk = E(nc.Block())

        wp = {}

        def W(eng, key):
            if key in wp:
                sem, v = wp[key]
                sch.w(eng, sem, v)

        def mark(key, sem):
            wp[key] = (sem, sch.v(sem))

        g = "gpsimd"; s = "sync"; t = "tensor"; v = "vector"; sc = "scalar"

        ew = sch.add(g, lambda e: e.dma_start(eye[:, :], eyeh.ap()[:, :]), sG, 16)
        wp[("w_eye",)] = (sG, ew)
        for nm, row, bh in (("bq", bq_row, bqh), ("bk", bk_row, bkh),
                            ("bv", bv_row, bvh), ("bo", bo_row, boh),
                            ("b2", b2_row, b2h)):
            bw = sch.add(g, lambda e, d=row, sr=bh: e.dma_start(
                AP(d, 0, [[1024, 1], [1, 1024]]),
                AP(sr, 0, [[1024, 1], [1, 1024]])), sG, 16)
            wp[("w_" + nm,)] = (sG, bw)
        for nm, W_sb, wh in (("wq", Wq_sb, wqh), ("wk", Wk_sb, wkh), ("wv", Wv_sb, wvh)):
            ww = sch.add(g, lambda e, d=W_sb, sr=wh: e.dma_start(
                AP(d, 0, [[8192, 128], [1024, 8], [1, 1024]]),
                AP(sr, 0, [[1024, 128], [131072, 8], [1, 1024]])), sG, 16)
            wp[("w_" + nm,)] = (sG, ww)
        sch.add(g, lambda e: e.dma_start(
            AP(Wo_sb, 0, [[8192, 64], [1024, 8], [1, 1024]]),
            AP(woh, 0, [[1024, 64], [131072, 8], [1, 1024]])), sG, 16)
        ww = sch.add(g, lambda e: e.dma_start(
            AP(Wo_sb, 64 * 8192, [[8192, 64], [1024, 8], [1, 1024]]),
            AP(woh, 65536, [[1024, 64], [131072, 8], [1, 1024]])), sG, 16)
        wp[("w_wo",)] = (sG, ww)
        mark("prologue_g", sG)

        sch.add(v, lambda e: e.memset(ones[0:1, :], 1.0), sVE)
        sch.add(v, lambda e: e.memset(ones512[0:1, :], 1.0), sVE)
        sch.add(v, lambda e: e.memset(VsT[:, :], 1.0), sVE)
        sch.add(v, lambda e: e.memset(acc1[:, :], 0.0), sVE)
        sch.add(v, lambda e: e.memset(acc2[:, :], 0.0), sVE)
        mark("prologue_v", sVE)

        W(sc, "prologue_v")
        sch.add(sc, lambda e: e.activation(warm[0:1, 0:128], ones[0:1, :], AF.Exp), sSC)
        sch.add(sc, lambda e: e.activation(warm[0:1, 128:256], ones[0:1, :], AF.Ln), sSC)

        qsteps = []

        def gamma_step(widx, ch):
            def fs(e, widx=widx, ch=ch):
                if widx == 0:
                    return e.dma_start(Wch[:, :], AP(w1h, ch * 131072,
                                                    [[1024, 128], [1, 1024]]))
                a, cg = ch % 8, ch // 8
                return e.dma_start(Wch[:, :], AP(w2h, a * 524288 + cg * 1024,
                                                 [[4096, 128], [1, 1024]]))
            def fv(ops, widx=widx, ch=ch):
                acc = acc1 if widx == 0 else acc2
                ops.append((v, "sync", lambda e: e.reduce_sum(gtmp[:, :], Wch[:, :], AX.X,
                                                      apply_absolute_value=True), sVE))
                ops.append((v, "plain", lambda e, a=acc: e.tensor_add(a[:, :], a[:, :], gtmp[:, :]), sVE))
            return ("gamma", fs, fv)

        def finalize_step():
            return ("finalize", None, None)

        def quant_step(widx, ch):
            def fs(e, widx=widx, ch=ch):
                if widx == 0:
                    return e.dma_start(Wch[:, :], AP(w1h, ch * 131072,
                                                    [[1024, 128], [1, 1024]]))
                a, cg = ch % 8, ch // 8
                return e.dma_start(Wch[:, :], AP(w2h, a * 524288 + cg * 1024,
                                                 [[4096, 128], [1, 1024]]))
            def fo(e, hh, widx=widx, ch=ch):
                if widx == 0:
                    return e.dma_start(AP(wqt1h, ch * 131072 + hh * 512,
                                          [[1024, 128], [1, 512]]), qout[:, :])
                a, cg = ch % 8, ch // 8
                return e.dma_start(AP(wq2h, a * 524288 + cg * 1024 + hh * 512,
                                      [[4096, 128], [1, 512]]), qout[:, :])
            return ("quant", fs, fo, widx)

        for widx in (0, 1):
            for ch in range(32):
                qsteps.append(gamma_step(widx, ch))
        qsteps.append(finalize_step())
        for widx in (0, 1):
            for ch in range(32):
                qsteps.append(quant_step(widx, ch))
        qstate = {"i": 0, "thr_ready": None, "last_q_dma": None, "last_qv": None}

        def emit_qstep():
            i = qstate["i"]
            if i >= len(qsteps):
                return
            qstate["i"] = i + 1
            st = qsteps[i]
            if st[0] == "gamma":
                if qstate["last_qv"]:
                    W(s, qstate["last_qv"])
                ld = sch.add(s, st[1], sS, 16)
                sch.w(v, sS, ld)
                ops = []
                st[2](ops)
                for eng, kind, f, sem in ops:
                    (sch.add_sync if kind == "sync" else sch.add)(eng, f, sem)
                mark(f"qv_{i}", sVE)
                qstate["last_qv"] = f"qv_{i}" 
            elif st[0] == "finalize":
                sch.w(g, sVE, sch.v(sVE))
                sch.add(g, lambda e: e.dma_start(AP(gscrh, 0, [[1, 128], [1, 1]]),
                                                 acc1[:, :]), sFIN, 16)
                sch.add(g, lambda e: e.dma_start(AP(gscrh, 128, [[1, 128], [1, 1]]),
                                                 acc2[:, :]), sFIN, 16)
                sch.w(g, sFIN, sch.v(sFIN))
                gld = sch.add(g, lambda e: e.dma_start(
                    grow[0:1, :], AP(gscrh, 0, [[256, 1], [1, 256]])), sFIN, 16)
                sch.w(v, sFIN, gld)
                sch.add(v, lambda e: e.reduce_sum(gv[0:1, 0:1], grow[0:1, 0:128], AX.X), sVE)
                sch.add_sync(v, lambda e: e.reduce_sum(gv[0:1, 1:2], grow[0:1, 128:256], AX.X), sVE)
                c1, c2 = 0.5 / QN, 0.5 * 1e-5
                sch.add(v, lambda e: e.tensor_scalar(gv[0:1, 4:5], gv[0:1, 0:1], c1, c2, ALU.mult, ALU.add), sVE)
                sch.add(v, lambda e: e.tensor_scalar(gv[0:1, 5:6], gv[0:1, 0:1], -c1, -c2, ALU.mult, ALU.add), sVE)
                sch.add(v, lambda e: e.tensor_scalar(gv[0:1, 6:7], gv[0:1, 1:2], c1, c2, ALU.mult, ALU.add), sVE)
                thr_v = sch.add(v, lambda e: e.tensor_scalar(gv[0:1, 7:8], gv[0:1, 1:2], -c1, -c2, ALU.mult, ALU.add), sVE)
                sch.w(g, sVE, thr_v)
                sch.add(g, lambda e: e.dma_start(AP(thrscrh, 0, [[8, 1], [1, 4]]),
                                                 gv[0:1, 4:8]), sFIN, 16)
                sch.w(g, sFIN, sch.v(sFIN))
                thr_g = sch.add(g, lambda e: e.dma_start(
                    thrbc[:, 0:4], AP(thrscrh, 0, [[0, 128], [1, 4]])), sFIN, 16)
                qstate["thr_ready"] = thr_g
            else:
                widx = st[3]
                if qstate["last_qv"]:
                    W(s, qstate["last_qv"])
                ld = sch.add(s, st[1], sS, 16)
                sch.w(v, sS, ld)
                if qstate["thr_ready"] is not None:
                    sch.w(v, sFIN, qstate["thr_ready"])
                    qstate["thr_ready"] = None
                cp, cn = 2 * widx, 2 * widx + 1
                for hh in range(2):
                    if qstate["last_q_dma"] is not None:
                        sch.w(v, sQS, qstate["last_q_dma"])
                    sch.add(v, lambda e, cp=cp, hh=hh: e.tensor_scalar(
                        tq1[:, :], Wch[:, hh * 512:hh * 512 + 512],
                        thrbc[:, cp:cp + 1], None, ALU.is_gt), sVE)
                    sch.add_sync(v, lambda e, cn=cn, hh=hh: e.tensor_scalar(
                        tq2[:, :], Wch[:, hh * 512:hh * 512 + 512],
                        thrbc[:, cn:cn + 1], None, ALU.is_lt), sVE)
                    qv = sch.add_sync(v, lambda e: e.tensor_sub(
                        qout[:, :], tq1[:, :], tq2[:, :]), sVE)
                    sch.w(s, sVE, qv)
                    qstate["last_q_dma"] = sch.add(
                        s, lambda e, st=st, hh=hh: st[2](e, hh), sQS, 16)
                mark(f"qv_{i}", sVE)
                qstate["last_qv"] = f"qv_{i}"

        gidx_of = lambda b, qt, gg: (b * QTB + qt) * NGQ + gg

        xtq = []
        xtrn = [0]
        wp.setdefault(("trch", 0, "prev"), None)
        wp.setdefault(("trch", 1, "prev"), None)

        def Wtr(eng, key):
            val = wp.get(key)
            if val:
                sch.w(eng, val[0], val[1])

        def xbf_prefetch(b):
            for dc2 in range(2):
                if b > 0:
                    W(g, ("xtr", b - 1, 6 + dc2))
                ld = sch.add(g, lambda e, b=b, dc=dc2: e.dma_start(
                    AP(x_bf, (dc % 2) * 512, [[1024, 128], [128, 4], [1, 128]]),
                    AP(xh, b * 524288 + dc * 128, [[1024, 128], [131072, 4], [1, 128]])),
                    sXBF[dc2 % 2], 16)
                wp[("xbf", b, dc2)] = (sXBF[dc2 % 2], ld)

        xbf_prefetch(0)
        for b in range(NB):

            def make_xtr_items(b):
                items = []
                for dc in range(8):
                    for tc4 in range(4):
                        def em(dc=dc, tc4=tc4, b=b):
                            i = xtrn[0]; ch = i % 2
                            xtrn[0] += 1
                            W(t, ("xbf", b, dc))
                            if b == 0 and dc == 0 and tc4 == 0:
                                W(t, ("w_eye",))
                                W(t, "prologue_v")
                            if b > 0 and dc == 0 and tc4 == 0:
                                W(t, ("qkv_done", b - 1))
                            Wtr(t, ("trch", ch, "prev"))
                            tm = sch.add(t, lambda e, dc=dc, tc=tc4, ch=ch: e.transpose(
                                ps_tr[:, ch * 1024:ch * 1024 + 128],
                                x_bf[:, (dc % 2) * 512 + tc * 128:(dc % 2) * 512 + tc * 128 + 128],
                                eye[:, :]), sTT)
                            eng, sem = ((sc, sSC) if ch == 0 else (v, sVE))
                            sch.w(eng, sTT, tm)
                            ev = sch.add(eng, lambda e, dc=dc, tc=tc4, ch=ch: (
                                e.copy(xT[:, dc * 512 + tc * 128:dc * 512 + tc * 128 + 128],
                                       ps_tr[:, ch * 1024:ch * 1024 + 128])
                                if ch == 0 else
                                e.tensor_copy(xT[:, dc * 512 + tc * 128:dc * 512 + tc * 128 + 128],
                                              ps_tr[:, ch * 1024:ch * 1024 + 128])), sem)
                            wp[("trch", ch, "prev")] = (sem, ev)
                            if tc4 == 3:
                                wp[("xtr", b, dc)] = (sTT, sch.v(sTT))
                                if dc == 7:
                                    wp[("xtevacS", b, 7)] = (sSC, sch.v(sSC))
                                    wp[("xtevacV", b, 7)] = (sVE, sch.v(sVE))
                                if dc + 2 < 8:
                                    sch.w(g, sTT, sch.v(sTT))
                                    ld = sch.add(g, lambda e, b=b, dc=dc + 2: e.dma_start(
                                        AP(x_bf, (dc % 2) * 512, [[1024, 128], [128, 4], [1, 128]]),
                                        AP(xh, b * 524288 + dc * 128,
                                           [[1024, 128], [131072, 4], [1, 128]])),
                                        sXBF[dc % 2], 16)
                                    wp[("xbf", b, dc + 2)] = (sXBF[dc % 2], ld)
                        items.append(em)
                return items

            for em in make_xtr_items(b):
                em()
            if b + 1 < NB:
                xbf_prefetch(b + 1)

            for pi, (W_sb, brow, dst, dup) in enumerate((
                    (Wq_sb, bq_row, Qd, True), (Wk_sb, bk_row, Kd, True),
                    (Wv_sb, bv_row, Vs, False))):
                for tc in range(4):
                    for dh in range(2):
                        n = pi * 8 + tc * 2 + dh
                        if b == 0 and tc == 0 and dh == 0:
                            W(t, ("w_" + ("wq", "wk", "wv")[pi],))
                            W(t, ("w_" + ("bq", "bk", "bv")[pi],))
                        W(t, ("xtevacS", b, 7))
                        W(t, ("xtevacV", b, 7))
                        if n >= 2:
                            W(t, ("qkvev", b, n - 2))
                        elif b > 0:
                            sch.w(t, sVN, b * QTB)
                        sch.add(t, lambda e, brow=brow, dh=dh, n=n: e.matmul(
                            ps_pv[n % 2][:, :], ones[0:1, :], brow[0:1, dh * 512:dh * 512 + 512],
                            start=True, stop=False), None)
                        for fc in range(8):
                            tm = sch.add(t, lambda e, W_sb=W_sb, fc=fc, tc=tc, dh=dh, n=n: e.matmul(
                                ps_pv[n % 2][:, :],
                                xT[:, fc * 512 + tc * 128:fc * 512 + tc * 128 + 128],
                                W_sb[:, fc * 1024 + dh * 512:fc * 1024 + dh * 512 + 512],
                                start=False, stop=(fc == 7)), sTT if fc == 7 else None)
                        wp[("qkvmm", b, pi, tc, dh)] = (sTT, tm)
                        W(v, ("qkvmm", b, pi, tc, dh))
                        prevu = (b, pi, tc - 2) if tc >= 2 else (
                            (b, pi - 1, tc + 2) if pi > 0 else
                            ((b - 1, 2, tc + 2) if b > 0 else None))
                        if prevu is not None and dh == 0:
                            W(v, ("folddma",) + prevu)
                        ev = sch.add(v, lambda e, tc=tc, dh=dh, n=n: e.tensor_copy(
                            Qtm[:, (tc % 2) * 1024 + dh * 512:(tc % 2) * 1024 + dh * 512 + 512],
                            ps_pv[n % 2][:, :]), sVE)
                        wp[("qkvev", b, n)] = (sVE, ev)
                        wp[("qkvevac", b, pi, tc, dh)] = (sVE, ev)
                    W(s, ("qkvevac", b, pi, tc, 1))
                    if pi == 2:
                        pass
                    sch.add(s, lambda e, dst=dst, tc=tc: e.dma_start(
                        AP(dst, 16 * tc * 8192, [[8192, 16], [1024, 8], [1, 1024]]),
                        AP(Qtm, (tc % 2) * 1024, [[2048, 128], [1, 1024]])), sFoldP[tc % 2], 16)
                    if dup:
                        fd = sch.add(s, lambda e, dst=dst, tc=tc: e.dma_start(
                            AP(dst, (64 + 16 * tc) * 8192, [[8192, 16], [1024, 8], [1, 1024]]),
                            AP(Qtm, (tc % 2) * 1024, [[2048, 128], [1, 1024]])), sFoldP[tc % 2], 16)
                    else:
                        fd = sch.v(sFoldP[tc % 2])
                    wp[("folddma", b, pi, tc)] = (sFoldP[tc % 2], fd)
            mark(("qkv_done", b), sTT)
            wp[("qkvevac_last", b)] = (sVE, sch.v(sVE))
            wp[("folds", b)] = (sFoldP[0], sch.v(sFoldP[0]))
            wp[("folds1", b)] = (sFoldP[1], sch.v(sFoldP[1]))

            for cc in range(64):
                ch = cc % 2
                if cc < 2:
                    W(t, ("folds", b))
                    W(t, ("folds1", b))
                Wtr(t, ("trch", ch, "prev"))
                tm = sch.add(t, lambda e, cc=cc, ch=ch: e.transpose(
                    ps_tr[:, ch * 1024:ch * 1024 + 64],
                    Vs[0:64, cc * 128:cc * 128 + 128], eye[0:64, 0:64]), sTT)
                eng, sem = ((sc, sSC) if ch == 0 else (v, sVE))
                sch.w(eng, sTT, tm)
                ev = sch.add(eng, lambda e, cc=cc, ch=ch: (
                    e.copy(VsT[:, 65 * cc:65 * cc + 64],
                           ps_tr[:, ch * 1024:ch * 1024 + 64])
                    if ch == 0 else
                    e.tensor_copy(VsT[:, 65 * cc:65 * cc + 64],
                                  ps_tr[:, ch * 1024:ch * 1024 + 64])), sem)
                wp[("trch", ch, "prev")] = (sem, ev)
            wp[("vst", b)] = (sVE, sch.v(sVE))
            wp[("vstS", b)] = (sSC, sch.v(sSC))

            PVLAG = 2
            for qt in range(QTB):
                qtg = b * QTB + qt
                for gg in range(NGQ + PVLAG):
                    gi = gidx_of(b, qt, gg)
                    if gg < NGQ:
                        if qt == 0 and gg == 0:
                            sch.w(t, sVE, wp[("vst", b)][1])
                            sch.w(t, sSC, wp[("vstS", b)][1])
                            W(t, ("folds", b))
                            W(t, ("folds1", b))
                            if b > 0:
                                W(t, ("opev", b - 1, 7))
                        if gi >= 2:
                            sch.w(t, sEXP, gi - 1)
                        hh = (gg % 2) * 1024
                        cc = 2 * gg
                        sch.add(t, lambda e, cc=cc, qt=qt, hh=hh: e.matmul(
                            ps_sb[:, hh:hh + 512],
                            Kd[0:64, cc * 128:cc * 128 + 128],
                            Qd[0:64, qt * 512:qt * 512 + 512],
                            start=True, stop=True, tile_position=(0, 0)), None)
                        ts = sch.add(t, lambda e, cc=cc, qt=qt, hh=hh: e.matmul(
                            ps_sb[:, hh + 512:hh + 1024],
                            Kd[64:128, (cc + 1) * 128:(cc + 1) * 128 + 128],
                            Qd[64:128, qt * 512:qt * 512 + 512],
                            start=True, stop=True, tile_position=(64, 0)), sTS)
                        sch.w(sc, sTS, ts)
                        if gi >= 3:
                            sch.w(sc, sPVC, gi - 2)
                        ex = sch.add(sc, lambda e, gg=gg: e.activation(
                            PT[:, (gg % 4) * 1024:(gg % 4) * 1024 + 1024],
                            ps_sb[:, (gg % 2) * 1024:(gg % 2) * 1024 + 1024],
                            AF.Exp, scale=SOFT_SCALE), sEXP)
                        wp[("exp", gi)] = (sEXP, ex)
                    pg = gg - PVLAG
                    if pg >= 0:
                        pgi = gidx_of(b, qt, pg)
                        W(t, ("exp", pgi))
                        if pg == 0 and qtg >= 2:
                            sch.w(t, sVN, qtg - 1)
                        if pg == 0 and qt < 2:
                            W(t, ("qkvev", b, 23))
                        sch.add(t, lambda e, pg=pg, qtg=qtg: e.matmul(
                            ps_pv[qtg % 2][0:65, :],
                            VsT[:, (2 * pg) * 65:(2 * pg) * 65 + 65],
                            PT[:, (pg % 4) * 1024:(pg % 4) * 1024 + 512],
                            start=(pg == 0), stop=False), None)
                        sch.add(t, lambda e, pg=pg, qtg=qtg: e.matmul(
                            ps_pv[qtg % 2][0:65, :],
                            VsT[:, (2 * pg + 1) * 65:(2 * pg + 1) * 65 + 65],
                            PT[:, (pg % 4) * 1024 + 512:(pg % 4) * 1024 + 1024],
                            start=False, stop=(pg == NGQ - 1)), sPVC)
                sch.w(v, sPVC, NGQ * (qtg + 1))
                if qtg >= 1:
                    W(v, ("recwr", qtg - 1))
                rc = sch.add(v, lambda e, qtg=qtg: e.reciprocal(
                    rec_sb[0:1, :], ps_pv[qtg % 2][64:65, :]), sREC)
                sch.w(g, sREC, rc)
                wr = sch.add(g, lambda e: e.dma_start(
                    AP(recscrh, 0, [[512, 1], [1, 512]]), rec_sb[0:1, :]), sG, 16)
                wp[("recwr", qtg)] = (sG, wr)
                sch.w(g, sG, wr)
                if qtg >= 1:
                    sch.w(g, sVN, qtg)
                rd = sch.add(g, lambda e, qtg=qtg: e.dma_start(
                    rec_bc[0:64, 0:512],
                    AP(recscrh, 0, [[0, 64], [1, 512]])), sG, 16)
                sch.w(v, sG, rd)
                if qt == 0 and b > 0:
                    W(v, ("outproj", b - 1))
                vn = sch.add(v, lambda e, qtg=qtg, qt=qt: e.scalar_tensor_tensor(
                    A2[0:64, qt * 512:qt * 512 + 512], ps_pv[qtg % 2][0:64, :], 1.0,
                    rec_bc[0:64, 0:512],
                    ALU.mult, ALU.mult), sVN)
                sch.w(g, sVN, vn)
                ad = sch.add(g, lambda e, qt=qt: e.dma_start(
                    A2[64:128, qt * 512:qt * 512 + 512],
                    A2[0:64, qt * 512:qt * 512 + 512]), sG, 16)
                wp[("a2dup", qtg)] = (sG, ad)
                emit_qstep()
                emit_qstep()

            for tc in range(4):
                if b > 0 or tc >= 2:
                    bb, tt = (b, tc - 2) if tc >= 2 else (b - 1, tc + 2)
                    W(s, ("s1evac", bb, tt))
                ld = sch.add(s, lambda e, b=b, tc=tc: e.dma_start(
                    x_tc[:, (tc % 2) * 1024:(tc % 2) * 1024 + 1024],
                    AP(xh, (b * 512 + tc * 128) * 1024, [[1024, 128], [1, 1024]])),
                    sXL[tc % 2], 16)
                wp[("xload", b, tc)] = (sXL[tc % 2], ld)
                for dh in range(2):
                    n = tc * 2 + dh
                    u = (n % 2) * 2
                    if n == 0:
                        if b == 0:
                            W(t, ("w_wo",))
                            W(t, ("w_bo",))
                        sch.w(t, sVN, b * QTB + QTB)
                        sch.w(t, sG, wp[("a2dup", b * QTB + QTB - 1)][1])
                    if n >= 2:
                        W(t, ("opev", b, n - 2))
                    sch.add(t, lambda e, dh=dh, u=u: e.matmul(
                        ps_sb[:, u * 512:u * 512 + 512], ones[0:1, :],
                        bo_row[0:1, dh * 512:dh * 512 + 512],
                        start=True, stop=False, tile_position=(0, 0)), None)
                    for m in range(8):
                        for par in range(2):
                            r = 2 * m + par
                            lo = par * 64 * 8192
                            tm = sch.add(t, lambda e, r=r, m=m, tc=tc, dh=dh, lo=lo, par=par, u=u: e.matmul(
                                ps_sb[:, (u + par) * 512:(u + par) * 512 + 512],
                                AP(A2, lo + 2048 * tc + r, [[8192, 64], [16, 128]]),
                                AP(Wo_sb, lo + m * 1024 + dh * 512, [[8192, 64], [1, 512]]),
                                start=(par == 1 and m == 0), stop=(m == 7),
                                tile_position=(64 * par, 0)), sTT if r == 15 else None)
                    wp[("opmm", b, tc, dh)] = (sTT, tm)
                    W(v, ("opmm", b, tc, dh))
                    sch.w(v, wp[("xload", b, tc)][0], wp[("xload", b, tc)][1])
                    sch.add_sync(v, lambda e, tc=tc, dh=dh, u=u: e.scalar_tensor_tensor(
                        s1[:, (tc % 2) * 1024 + dh * 512:(tc % 2) * 1024 + dh * 512 + 512],
                        ps_sb[:, u * 512:u * 512 + 512], 1.0,
                        x_tc[:, (tc % 2) * 1024 + dh * 512:(tc % 2) * 1024 + dh * 512 + 512],
                        ALU.mult, ALU.add), sVE)
                    ev = sch.add_sync(v, lambda e, tc=tc, dh=dh, u=u: e.scalar_tensor_tensor(
                        s1[:, (tc % 2) * 1024 + dh * 512:(tc % 2) * 1024 + dh * 512 + 512],
                        ps_sb[:, (u + 1) * 512:(u + 1) * 512 + 512], 1.0,
                        s1[:, (tc % 2) * 1024 + dh * 512:(tc % 2) * 1024 + dh * 512 + 512],
                        ALU.mult, ALU.add), sVE)
                    wp[("opev", b, n)] = (sVE, ev)
                    wp[("s1evac", b, tc)] = (sVE, ev)
                if tc == 3:
                    mark(("outproj", b), sTT)
                sl = (tc % 2) * 1024
                if b > 0 or tc >= 2:
                    bb, tt = (b, tc - 2) if tc >= 2 else (b - 1, tc + 2)
                    W(v, ("x1w", bb, tt))
                sch.add_sync(v, lambda e, sl=sl: e.reduce_sum(
                    negsum[:, :], s1[:, sl:sl + 1024], AX.X, negate=True), sVE)
                sch.add_sync(v, lambda e: e.tensor_scalar_mul(
                    negmean[:, :], negsum[:, :], 1.0 / 1024.0), sVE)
                sch.add_sync(v, lambda e, sl=sl: e.tensor_scalar(
                    s1[:, sl:sl + 1024], s1[:, sl:sl + 1024],
                    negmean[:, :], None, ALU.add), sVE)
                sch.add_sync(v, lambda e, sl=sl: e.scalar_tensor_tensor(
                    x1[:, sl:sl + 1024], s1[:, sl:sl + 1024], 1.0,
                    s1[:, sl:sl + 1024], ALU.mult, ALU.mult,
                    accum_out=varsum[:, :]), sVE)
                lv = sch.add(v, lambda e: e.tensor_scalar(
                    vareps[:, :], varsum[:, :], 1.0 / 1024.0, LN_EPS,
                    ALU.mult, ALU.add), sVE)
                sch.w(sc, sVE, lv)
                sch.add(sc, lambda e: e.activation(uln[:, :], vareps[:, :], AF.Ln), None)
                sr = sch.add(sc, lambda e: e.activation(
                    rstd[:, :], uln[:, :], AF.Exp, scale=-0.5), sSC)
                sch.w(v, sSC, sr)
                x1v = sch.add(v, lambda e, sl=sl: e.tensor_scalar(
                    x1[:, sl:sl + 1024], s1[:, sl:sl + 1024],
                    rstd[:, :], None, ALU.mult), sVE)
                sch.w(s, sVE, x1v)
                xw = sch.add(s, lambda e, b=b, tc=tc, sl=sl: e.dma_start(
                    AP(x1scrh, (b * 512 + tc * 128) * 1024, [[1024, 128], [1, 1024]]),
                    x1[:, sl:sl + 1024]), sX1W[tc % 2], 16)
                wp[("x1w", b, tc)] = (sX1W[tc % 2], xw)
                if tc == 3:
                    sch.w(g, sX1W[0], sch.v(sX1W[0]))
                    sch.w(g, sX1W[1], sch.v(sX1W[1]))
                    cw = sch.add(g, lambda e, b=b: e.dma_start(
                        AP(x1bfh, b * 524288, [[1024, 512], [1, 1024]]),
                        AP(x1scrh, b * 524288, [[1024, 512], [1, 1024]])), sXB1, 16)
                emit_qstep()
                emit_qstep()

        while qstate["i"] < len(qsteps):
            emit_qstep()

        ATT_T = sch.v(sTT); ATT_V = sch.v(sVE); ATT_G = sch.v(sG); ATT_S = sch.v(sS)

        sch.w(s, sTT, ATT_T)
        sch.w(s, sVE, ATT_V)
        sch.w(s, sG, ATT_G)
        sch.w(s, sVN, NB * QTB)
        sch.w(s, sS, ATT_S)
        sch.w(s, sQS, sch.v(sQS))
        sch.w(s, sFoldP[0], sch.v(sFoldP[0]))
        sch.w(s, sFoldP[1], sch.v(sFoldP[1]))
        sch.w(s, sFIN, sch.v(sFIN))
        for i in range(2):
            sch.w(s, sXL[i], sch.v(sXL[i]))
            sch.w(s, sX1W[i], sch.v(sX1W[i]))
            sch.w(s, sXBF[i], sch.v(sXBF[i]))
        for dc in range(8):
            sch.add(s, lambda e, dc=dc: e.dma_start_transpose(
                WqT1[:, dc * 4096:dc * 4096 + 4096],
                AP(wqt1h, dc * 128, [[1024, 4096], [1, 128]])), sWQT, 16)
        for oc in range(32):
            sch.add(s, lambda e, oc=oc: e.dma_start_transpose(
                WqT2[:, oc * 1024:oc * 1024 + 1024],
                AP(wq2h, oc * 128, [[4096, 1024], [1, 128]])), sWQT, 16)
        wp[("wqt",)] = (sWQT, sch.v(sWQT))

        sch.w(g, sVE, ATT_V)
        sch.add(g, lambda e: e.dma_start(
            AP(b1_rows, 0, [[4096, 1], [1, 4096]]),
            AP(b1h, 0, [[4096, 1], [1, 4096]])), sXB1, 16)
        wp[("x1bf",)] = (sXB1, sch.v(sXB1))

        for b in range(NB):
            W(s, ("x1bf",))
            if b > 0:
                W(s, ("w1done", b - 1))
            for dc in range(8):
                sch.add(s, lambda e, b=b, dc=dc: e.dma_start_transpose(
                    x1T[:, dc * 512:dc * 512 + 512],
                    AP(x1bfh, b * 524288 + dc * 128, [[1024, 512], [1, 128]])), sX1T, 16)
            wp[("x1T", b)] = (sX1T, sch.v(sX1T))

            for oc in range(32):
                W(t, ("x1T", b))
                W(t, ("wqt",))
                if oc >= 2:
                    W(t, ("hevac", b, oc - 2))
                elif b > 0:
                    W(t, ("hevac", b - 1, oc + 30))
                sch.add(t, lambda e, oc=oc: e.matmul(
                    ps_pv[oc % 2][:, :], b1_rows[0:1, oc * 128:oc * 128 + 128],
                    ones512[0:1, :], start=True, stop=False), None)
                for dc in range(8):
                    tm = sch.add(t, lambda e, oc=oc, dc=dc: e.matmul(
                        ps_pv[oc % 2][:, :],
                        WqT1[:, dc * 4096 + oc * 128:dc * 4096 + oc * 128 + 128],
                        x1T[:, dc * 512:dc * 512 + 512],
                        start=False, stop=(dc == 7)), sTT if dc == 7 else None)
                wp[("w1mm", b, oc)] = (sTT, tm)
                W(v, ("w1mm", b, oc))
                if b > 0 and oc < 2:
                    W(v, ("w2done", b - 1))
                ev = sch.add(v, lambda e, oc=oc: e.tensor_scalar_max(
                    hT[:, oc * 512:oc * 512 + 512], ps_pv[oc % 2][:, :], 0.0), sVE)
                wp[("hevac", b, oc)] = (sVE, ev)
            mark(("w1done", b), sTT)

            for tsub in range(4):
                if b > 0 or tsub >= 2:
                    bb, tt = (b, tsub - 2) if tsub >= 2 else (b - 1, tsub + 2)
                    W(s, ("s2done", bb, tt))
                ld = sch.add(s, lambda e, b=b, tsub=tsub: e.dma_start(
                    x1f[:, (tsub % 2) * 1024:(tsub % 2) * 1024 + 1024],
                    AP(x1scrh, (b * 512 + tsub * 128) * 1024, [[1024, 128], [1, 1024]])),
                    sX1F[tsub % 2], 16)
                wp[("x1fload", b, tsub)] = (sX1F[tsub % 2], ld)
                for dh in range(2):
                    W(t, ("hevac", b, 31))
                    if b * 4 + tsub >= 2:
                        pb, pt = divmod(b * 4 + tsub - 2, 4)
                        W(t, ("s2evac", pb, pt))
                    sch.add(t, lambda e, tsub=tsub, dh=dh: e.matmul(
                        ps_sb[:, (tsub % 2) * 1024 + dh * 512:(tsub % 2) * 1024 + dh * 512 + 512],
                        ones[0:1, :], b2_row[0:1, dh * 512:dh * 512 + 512],
                        start=True, stop=False), None)
                    for oc in range(32):
                        tm = sch.add(t, lambda e, tsub=tsub, dh=dh, oc=oc: e.matmul(
                            ps_sb[:, (tsub % 2) * 1024 + dh * 512:(tsub % 2) * 1024 + dh * 512 + 512],
                            hT[:, oc * 512 + tsub * 128:oc * 512 + tsub * 128 + 128],
                            WqT2[:, oc * 1024 + dh * 512:oc * 1024 + dh * 512 + 512],
                            start=False, stop=(oc == 31)),
                            sTT if (dh == 1 and oc == 31) else None)
                wp[("w2mm", b, tsub)] = (sTT, tm)
                W(v, ("w2mm", b, tsub))
                sch.w(v, sX1F[tsub % 2], ld)
                if b * 4 + tsub >= 1:
                    pb, pt = divmod(b * 4 + tsub - 1, 4)
                    W(v, ("outdma", pb, pt))
                sl = 0
                xsl = (tsub % 2) * 1024
                ev = sch.add_sync(v, lambda e, tsub=tsub, xsl=xsl: e.scalar_tensor_tensor(
                    s2[:, 0:1024], ps_sb[:, (tsub % 2) * 1024:(tsub % 2) * 1024 + 1024], 1.0,
                    x1f[:, xsl:xsl + 1024], ALU.mult, ALU.add), sVE)
                wp[("s2evac", b, tsub)] = (sVE, ev)
                wp[("s2done", b, tsub)] = (sVE, ev)
                sch.add_sync(v, lambda e: e.reduce_sum(
                    negsum[:, :], s2[:, 0:1024], AX.X, negate=True), sVE)
                sch.add_sync(v, lambda e: e.tensor_scalar_mul(
                    negmean[:, :], negsum[:, :], 1.0 / 1024.0), sVE)
                sch.add_sync(v, lambda e: e.tensor_scalar(
                    s2[:, 0:1024], s2[:, 0:1024],
                    negmean[:, :], None, ALU.add), sVE)
                sch.add_sync(v, lambda e: e.scalar_tensor_tensor(
                    outb[:, 0:1024], s2[:, 0:1024], 1.0,
                    s2[:, 0:1024], ALU.mult, ALU.mult,
                    accum_out=varsum[:, :]), sVE)
                lv = sch.add(v, lambda e: e.tensor_scalar(
                    vareps[:, :], varsum[:, :], 1.0 / 1024.0, LN_EPS,
                    ALU.mult, ALU.add), sVE)
                sch.w(sc, sVE, lv)
                sch.add(sc, lambda e: e.activation(uln[:, :], vareps[:, :], AF.Ln), None)
                sr = sch.add(sc, lambda e: e.activation(
                    rstd[:, :], uln[:, :], AF.Exp, scale=-0.5), sSC)
                sch.w(v, sSC, sr)
                ov = sch.add(v, lambda e: e.tensor_scalar(
                    outb[:, 0:1024], s2[:, 0:1024],
                    rstd[:, :], None, ALU.mult), sVE)
                wp[("outw", b, tsub)] = (sVE, ov)
                sch.w(s, sVE, ov)
                od = sch.add(s, lambda e, b=b, tsub=tsub: e.dma_start(
                    AP(yh, (b * 512 + tsub * 128) * 1024, [[1024, 128], [1, 1024]]),
                    outb[:, 0:1024]), sOUT, 16)
                wp[("outdma", b, tsub)] = (sOUT, od)
            mark(("w2done", b), sTT)

        sch.w(s, sOUT, sch.v(sOUT))
        sch.w(s, sX1T, sch.v(sX1T))
        for i in range(2):
            sch.w(s, sX1F[i], sch.v(sX1F[i]))
        sch.w(g, sG, sch.v(sG))
        sch.w(g, sXB1, sch.v(sXB1))

        @blk.sync
        def _(e):
            for f in sch.L["sync"]:
                f(e)

        @blk.gpsimd
        def _(e):
            for f in sch.L["gpsimd"]:
                f(e)

        @blk.tensor
        def _(e):
            for f in sch.L["tensor"]:
                f(e)

        @blk.vector
        def _(e):
            for f in sch.L["vector"]:
                f(e)

        @blk.scalar
        def _(e):
            for f in sch.L["scalar"]:
                f(e)

    return nc


def kernel(L0_attn_x, L0_attn_Wq, L0_attn_bq, L0_attn_Wk, L0_attn_bk,
           L0_attn_Wv, L0_attn_bv, L0_attn_Wo, L0_attn_bo,
           L0_ffn_W1, L0_ffn_b1, L0_ffn_W2, L0_ffn_b2):
    x = np.ascontiguousarray(np.asarray(L0_attn_x, dtype=np.float32)).reshape(16384, 1024)
    com = dict(
        Wq=np.ascontiguousarray(np.asarray(L0_attn_Wq, np.float32)),
        bq=np.ascontiguousarray(np.asarray(L0_attn_bq, np.float32)),
        Wk=np.ascontiguousarray(np.asarray(L0_attn_Wk, np.float32)),
        bk=np.ascontiguousarray(np.asarray(L0_attn_bk, np.float32)),
        Wv=np.ascontiguousarray(np.asarray(L0_attn_Wv, np.float32)),
        bv=np.ascontiguousarray(np.asarray(L0_attn_bv, np.float32)),
        Wo=np.ascontiguousarray(np.asarray(L0_attn_Wo, np.float32)),
        bo=np.ascontiguousarray(np.asarray(L0_attn_bo, np.float32)),
        W1=np.ascontiguousarray(np.asarray(L0_ffn_W1, np.float32)),
        b1=np.ascontiguousarray(np.asarray(L0_ffn_b1, np.float32)),
        W2=np.ascontiguousarray(np.asarray(L0_ffn_W2, np.float32)),
        b2=np.ascontiguousarray(np.asarray(L0_ffn_b2, np.float32)),
        eye=np.eye(128, dtype=np.float32),
    )
    nc = build_nc()
    in_maps = [dict(com, x=np.ascontiguousarray(x[c * 2048:(c + 1) * 2048]))
               for c in range(8)]
    res = run_bass_kernel_spmd(nc, in_maps, core_ids=list(range(8)))
    out = np.concatenate([np.asarray(r["y"], np.float32) for r in res.results], axis=0)
    return out.reshape(2, 8192, 1024)


# version 09
# speedup vs baseline: 1.1510x; 1.0024x over previous
import numpy as np
import concourse.bass as bass
import concourse.mybir as mybir
from concourse.bass_utils import run_bass_kernel_spmd

F32 = mybir.dt.float32
BF16 = mybir.dt.bfloat16
AF = mybir.ActivationFunctionType
ALU = mybir.AluOpType
AX = mybir.AxisListType
AP = bass.AP

NB = 4
QTB = 16
NGQ = 32
SOFT_SCALE = float(2.0 ** -10)
LN_EPS = 1e-5
QN = 4096 * 1024


class Sch:

    def __init__(self):
        self.L = {k: [] for k in ("sync", "gpsimd", "tensor", "vector", "scalar")}
        self.c = {}

    def w(self, eng, sem, v):
        if v > 0:
            self.L[eng].append(lambda e, s=sem, vv=v: e.wait_ge(s, vv))

    def add(self, eng, f, inc=None, n=1):
        if inc is None:
            self.L[eng].append(f)
            return None
        tot = self.c[id(inc)] = self.c.get(id(inc), 0) + n
        self.L[eng].append(lambda e, ff=f, s=inc, nn=n: ff(e).then_inc(s, nn))
        return tot

    def v(self, sem):
        return self.c.get(id(sem), 0)

    def add_sync(self, eng, f, inc):
        tot = self.add(eng, f, inc)
        self.w(eng, inc, tot)
        return tot


def build_nc():
    nc = bass.Bass("TRN2", target_bir_lowering=False)

    xh = nc.dram_tensor("x", [2048, 1024], F32, kind="ExternalInput")
    wqh = nc.dram_tensor("Wq", [1024, 1024], F32, kind="ExternalInput")
    bqh = nc.dram_tensor("bq", [1024], F32, kind="ExternalInput")
    wkh = nc.dram_tensor("Wk", [1024, 1024], F32, kind="ExternalInput")
    bkh = nc.dram_tensor("bk", [1024], F32, kind="ExternalInput")
    wvh = nc.dram_tensor("Wv", [1024, 1024], F32, kind="ExternalInput")
    bvh = nc.dram_tensor("bv", [1024], F32, kind="ExternalInput")
    woh = nc.dram_tensor("Wo", [1024, 1024], F32, kind="ExternalInput")
    boh = nc.dram_tensor("bo", [1024], F32, kind="ExternalInput")
    w1h = nc.dram_tensor("W1", [4096, 1024], F32, kind="ExternalInput")
    b1h = nc.dram_tensor("b1", [4096], F32, kind="ExternalInput")
    w2h = nc.dram_tensor("W2", [1024, 4096], F32, kind="ExternalInput")
    b2h = nc.dram_tensor("b2", [1024], F32, kind="ExternalInput")
    eyeh = nc.dram_tensor("eye", [128, 128], F32, kind="ExternalInput")
    yh = nc.dram_tensor("y", [2048, 1024], F32, kind="ExternalOutput")

    wqt1h = nc.dram_tensor("wqt1", [4096, 1024], BF16, kind="Internal")
    wq2h = nc.dram_tensor("wq2", [1024, 4096], BF16, kind="Internal")
    x1scrh = nc.dram_tensor("x1scr", [2048, 1024], F32, kind="Internal")
    x1bfh = nc.dram_tensor("x1bfscr", [2048, 1024], BF16, kind="Internal")
    recscrh = nc.dram_tensor("recscr", [512], F32, kind="Internal")
    gscrh = nc.dram_tensor("gscr", [256], F32, kind="Internal")
    thrscrh = nc.dram_tensor("thrscr", [8], F32, kind="Internal")

    BASE = 16512
    off = [BASE]

    def alloc(name, shape, dtype, at=None):
        if at is None:
            at = off[0]
            nbytes = shape[-1] * (2 if dtype == BF16 else 4)
            for d in shape[1:-1]:
                nbytes *= d
            off[0] = at + ((nbytes + 31) // 32) * 32
        assert off[0] <= 229344, f"sbuf overflow at {name}: {off[0]}"
        return nc.alloc_sbuf_tensor_at(name, shape, dtype, offset=at)

    eye = alloc("eye", [128, 128], BF16)
    ones = alloc("ones", [1, 128], BF16)
    ones512 = alloc("ones512", [1, 512], BF16)
    bq_row = alloc("bq_row", [1, 1024], BF16)
    bk_row = alloc("bk_row", [1, 1024], BF16)
    bv_row = alloc("bv_row", [1, 1024], BF16)
    bo_row = alloc("bo_row", [1, 1024], BF16)
    b2_row = alloc("b2_row", [1, 1024], BF16)
    thrbc = alloc("thrbc", [128, 8], F32)
    gv = alloc("gv", [1, 16], F32)
    acc1 = alloc("acc1", [128, 1], F32)
    acc2 = alloc("acc2", [128, 1], F32)
    gtmp = alloc("gtmp", [128, 1], F32)
    negsum = alloc("negsum", [128, 1], F32)
    negmean = alloc("negmean", [128, 1], F32)
    varsum = alloc("varsum", [128, 1], F32)
    vareps = alloc("vareps", [128, 1], F32)
    uln = alloc("uln", [128, 1], F32)
    rstd = alloc("rstd", [128, 1], F32)
    warm = alloc("warm", [1, 256], BF16)

    A0 = off[0]

    Qd = alloc("Qd", [128, 8192], BF16)
    Kd = alloc("Kd", [128, 8192], BF16)
    Vs = alloc("Vs", [64, 8192], BF16)
    VsT = alloc("VsT", [128, 4160], BF16)
    PT = alloc("PT", [128, 4096], BF16)
    Wq_sb = alloc("Wq_sb", [128, 8192], BF16)
    Wk_sb = alloc("Wk_sb", [128, 8192], BF16)
    Wv_sb = alloc("Wv_sb", [128, 8192], BF16)
    Wo_sb = alloc("Wo_sb", [128, 8192], BF16)
    A2 = alloc("A2", [128, 8192], BF16)
    xT = alloc("xT", [128, 4096], BF16)
    x_tc = alloc("x_tc", [128, 2048], F32)
    x_bf = alloc("x_bf", [128, 1024], BF16)
    Qtm = alloc("Qtm", [128, 2048], BF16)
    rec_sb = alloc("rec_sb", [1, 512], F32)
    rec_bc = alloc("rec_bc", [64, 512], F32)
    s1 = alloc("s1", [128, 2048], F32)
    x1 = alloc("x1", [128, 2048], F32)
    ATTN_END = off[0]

    off[0] = A0
    WqT1 = alloc("WqT1", [128, 32768], BF16)
    WqT2 = alloc("WqT2", [128, 32768], BF16)
    hT = alloc("hT", [128, 16384], BF16)
    x1T = alloc("x1T", [128, 4096], BF16)
    x1f = alloc("x1f", [128, 2048], F32)
    s2 = alloc("s2", [128, 1024], F32)
    outb = alloc("outb", [128, 1024], F32)
    b1_rows = alloc("b1_rows", [1, 4096], BF16)
    FFN_END = off[0]
    off[0] = ATTN_END
    grow = alloc("grow", [1, 256], F32)
    Wch = alloc("Wch", [128, 1024], F32)
    tq1 = alloc("tq1", [128, 512], BF16)
    tq2 = alloc("tq2", [128, 512], BF16)
    qout = alloc("qout", [128, 512], BF16)
    QNT_END = off[0]
    assert QNT_END <= 229344 and FFN_END <= 229344, (ATTN_END, FFN_END, QNT_END)
    off[0] = max(ATTN_END, FFN_END, QNT_END)

    sch = Sch()

    from contextlib import ExitStack
    with ExitStack() as es:
        E = es.enter_context
        ps_sb = E(nc.psum_tensor("ps_sb", [128, 2048], F32))
        ps_pv = [E(nc.psum_tensor(f"ps_pv{i}", [128, 512], F32)) for i in range(2)]
        ps_tr = E(nc.psum_tensor("ps_tr", [128, 2048], BF16))

        sS = E(nc.semaphore("sS"))
        sG = E(nc.semaphore("sG"))
        sQS = E(nc.semaphore("sQS"))
        sFoldP = [E(nc.semaphore(f"sFold{i}")) for i in range(2)]
        sFIN = E(nc.semaphore("sFIN"))
        sXL = [E(nc.semaphore(f"sXL{i}")) for i in range(2)]
        sX1W = [E(nc.semaphore(f"sX1W{i}")) for i in range(2)]
        sXBF = [E(nc.semaphore(f"sXBF{i}")) for i in range(2)]
        sWQT = E(nc.semaphore("sWQT"))
        sX1T = E(nc.semaphore("sX1T"))
        sX1F = [E(nc.semaphore(f"sX1F{i}")) for i in range(2)]
        sOUT = E(nc.semaphore("sOUT"))
        sXB1 = E(nc.semaphore("sXB1"))
        sTS = E(nc.semaphore("sTS"))
        sEXP = E(nc.semaphore("sEXP"))
        sPVC = E(nc.semaphore("sPVC"))
        sREC = E(nc.semaphore("sREC"))
        sVN = E(nc.semaphore("sVN"))
        sVE = E(nc.semaphore("sVE"))
        sTT = E(nc.semaphore("sTT"))
        sSC = E(nc.semaphore("sSC"))
        sWG = [E(nc.semaphore(f"sWG{i}")) for i in range(5)]
        sWQ1 = E(nc.semaphore("sWQ1"))
        blk = E(nc.Block())

        wp = {}

        def W(eng, key):
            if key in wp:
                sem, v = wp[key]
                sch.w(eng, sem, v)

        def mark(key, sem):
            wp[key] = (sem, sch.v(sem))

        g = "gpsimd"; s = "sync"; t = "tensor"; v = "vector"; sc = "scalar"

        sch.add(g, lambda e: e.dma_start(eye[:, :], eyeh.ap()[:, :]), sWG[0], 16)
        wp[("w_eye",)] = (sWG[0], 16)
        for i, (nm, W_sb, wh, row, bh) in enumerate((
                ("q", Wq_sb, wqh, bq_row, bqh),
                ("k", Wk_sb, wkh, bk_row, bkh),
                ("v", Wv_sb, wvh, bv_row, bvh))):
            sch.add(g, lambda e, d=W_sb, sr=wh: e.dma_start(
                AP(d, 0, [[8192, 128], [1024, 8], [1, 1024]]),
                AP(sr, 0, [[1024, 128], [131072, 8], [1, 1024]])), sWG[1 + i], 16)
            sch.add(g, lambda e, d=row, sr=bh: e.dma_start(
                AP(d, 0, [[1024, 1], [1, 1024]]),
                AP(sr, 0, [[1024, 1], [1, 1024]])), sWG[1 + i], 16)
            wp[("w_w" + nm,)] = wp[("w_b" + nm,)] = (sWG[1 + i], 32)
        sch.add(g, lambda e: e.dma_start(
            AP(Wo_sb, 0, [[8192, 64], [1024, 8], [1, 1024]]),
            AP(woh, 0, [[1024, 64], [131072, 8], [1, 1024]])), sWG[4], 16)
        sch.add(g, lambda e: e.dma_start(
            AP(Wo_sb, 64 * 8192, [[8192, 64], [1024, 8], [1, 1024]]),
            AP(woh, 65536, [[1024, 64], [131072, 8], [1, 1024]])), sWG[4], 16)
        for row, bh in ((bo_row, boh), (b2_row, b2h)):
            sch.add(g, lambda e, d=row, sr=bh: e.dma_start(
                AP(d, 0, [[1024, 1], [1, 1024]]),
                AP(sr, 0, [[1024, 1], [1, 1024]])), sWG[4], 16)
        wp[("w_wo",)] = wp[("w_bo",)] = wp[("w_b2",)] = (sWG[4], 64)
        mark("prologue_g", sG)

        sch.add(v, lambda e: e.memset(ones[0:1, :], 1.0), sVE)
        sch.add(v, lambda e: e.memset(ones512[0:1, :], 1.0), sVE)
        sch.add(v, lambda e: e.memset(VsT[:, :], 1.0), sVE)
        sch.add(v, lambda e: e.memset(acc1[:, :], 0.0), sVE)
        sch.add(v, lambda e: e.memset(acc2[:, :], 0.0), sVE)
        mark("prologue_v", sVE)

        W(sc, "prologue_v")
        sch.add(sc, lambda e: e.activation(warm[0:1, 0:128], ones[0:1, :], AF.Exp), sSC)
        sch.add(sc, lambda e: e.activation(warm[0:1, 128:256], ones[0:1, :], AF.Ln), sSC)

        qsteps = []

        def gamma_step(widx, ch):
            def fs(e, widx=widx, ch=ch):
                if widx == 0:
                    return e.dma_start(Wch[:, :], AP(w1h, ch * 131072,
                                                    [[1024, 128], [1, 1024]]))
                a, cg = ch % 8, ch // 8
                return e.dma_start(Wch[:, :], AP(w2h, a * 524288 + cg * 1024,
                                                 [[4096, 128], [1, 1024]]))
            def fv(ops, widx=widx, ch=ch):
                acc = acc1 if widx == 0 else acc2
                ops.append((v, "sync", lambda e: e.reduce_sum(gtmp[:, :], Wch[:, :], AX.X,
                                                      apply_absolute_value=True), sVE))
                ops.append((v, "plain", lambda e, a=acc: e.tensor_add(a[:, :], a[:, :], gtmp[:, :]), sVE))
            return ("gamma", fs, fv)

        def finalize_step():
            return ("finalize", None, None)

        def quant_step(widx, ch):
            def fs(e, widx=widx, ch=ch):
                if widx == 0:
                    return e.dma_start(Wch[:, :], AP(w1h, ch * 131072,
                                                    [[1024, 128], [1, 1024]]))
                a, cg = ch % 8, ch // 8
                return e.dma_start(Wch[:, :], AP(w2h, a * 524288 + cg * 1024,
                                                 [[4096, 128], [1, 1024]]))
            def fo(e, hh, widx=widx, ch=ch):
                if widx == 0:
                    return e.dma_start(AP(wqt1h, ch * 131072 + hh * 512,
                                          [[1024, 128], [1, 512]]), qout[:, :])
                a, cg = ch % 8, ch // 8
                return e.dma_start(AP(wq2h, a * 524288 + cg * 1024 + hh * 512,
                                      [[4096, 128], [1, 512]]), qout[:, :])
            return ("quant", fs, fo, widx)

        for widx in (0, 1):
            for ch in range(32):
                qsteps.append(gamma_step(widx, ch))
        qsteps.append(finalize_step())
        for widx in (0, 1):
            for ch in range(32):
                qsteps.append(quant_step(widx, ch))
        qstate = {"i": 0, "thr_ready": None, "last_q_dma": None, "last_qv": None}

        def emit_qstep():
            i = qstate["i"]
            if i >= len(qsteps):
                return
            qstate["i"] = i + 1
            st = qsteps[i]
            if st[0] == "gamma":
                if qstate["last_qv"]:
                    W(s, qstate["last_qv"])
                ld = sch.add(s, st[1], sS, 16)
                sch.w(v, sS, ld)
                ops = []
                st[2](ops)
                for eng, kind, f, sem in ops:
                    (sch.add_sync if kind == "sync" else sch.add)(eng, f, sem)
                mark(f"qv_{i}", sVE)
                qstate["last_qv"] = f"qv_{i}" 
            elif st[0] == "finalize":
                sch.w(g, sVE, sch.v(sVE))
                sch.add(g, lambda e: e.dma_start(AP(gscrh, 0, [[1, 128], [1, 1]]),
                                                 acc1[:, :]), sFIN, 16)
                sch.add(g, lambda e: e.dma_start(AP(gscrh, 128, [[1, 128], [1, 1]]),
                                                 acc2[:, :]), sFIN, 16)
                sch.w(g, sFIN, sch.v(sFIN))
                gld = sch.add(g, lambda e: e.dma_start(
                    grow[0:1, :], AP(gscrh, 0, [[256, 1], [1, 256]])), sFIN, 16)
                sch.w(v, sFIN, gld)
                sch.add(v, lambda e: e.reduce_sum(gv[0:1, 0:1], grow[0:1, 0:128], AX.X), sVE)
                sch.add_sync(v, lambda e: e.reduce_sum(gv[0:1, 1:2], grow[0:1, 128:256], AX.X), sVE)
                c1, c2 = 0.5 / QN, 0.5 * 1e-5
                sch.add(v, lambda e: e.tensor_scalar(gv[0:1, 4:5], gv[0:1, 0:1], c1, c2, ALU.mult, ALU.add), sVE)
                sch.add(v, lambda e: e.tensor_scalar(gv[0:1, 5:6], gv[0:1, 0:1], -c1, -c2, ALU.mult, ALU.add), sVE)
                sch.add(v, lambda e: e.tensor_scalar(gv[0:1, 6:7], gv[0:1, 1:2], c1, c2, ALU.mult, ALU.add), sVE)
                thr_v = sch.add(v, lambda e: e.tensor_scalar(gv[0:1, 7:8], gv[0:1, 1:2], -c1, -c2, ALU.mult, ALU.add), sVE)
                sch.w(g, sVE, thr_v)
                sch.add(g, lambda e: e.dma_start(AP(thrscrh, 0, [[8, 1], [1, 4]]),
                                                 gv[0:1, 4:8]), sFIN, 16)
                sch.w(g, sFIN, sch.v(sFIN))
                thr_g = sch.add(g, lambda e: e.dma_start(
                    thrbc[:, 0:4], AP(thrscrh, 0, [[0, 128], [1, 4]])), sFIN, 16)
                qstate["thr_ready"] = thr_g
            else:
                widx = st[3]
                if qstate["last_qv"]:
                    W(s, qstate["last_qv"])
                ld = sch.add(s, st[1], sS, 16)
                sch.w(v, sS, ld)
                if qstate["thr_ready"] is not None:
                    sch.w(v, sFIN, qstate["thr_ready"])
                    qstate["thr_ready"] = None
                cp, cn = 2 * widx, 2 * widx + 1
                for hh in range(2):
                    if qstate["last_q_dma"] is not None:
                        sch.w(v, sQS, qstate["last_q_dma"])
                    sch.add(v, lambda e, cp=cp, hh=hh: e.tensor_scalar(
                        tq1[:, :], Wch[:, hh * 512:hh * 512 + 512],
                        thrbc[:, cp:cp + 1], None, ALU.is_gt), sVE)
                    sch.add_sync(v, lambda e, cn=cn, hh=hh: e.tensor_scalar(
                        tq2[:, :], Wch[:, hh * 512:hh * 512 + 512],
                        thrbc[:, cn:cn + 1], None, ALU.is_lt), sVE)
                    qv = sch.add_sync(v, lambda e: e.tensor_sub(
                        qout[:, :], tq1[:, :], tq2[:, :]), sVE)
                    sch.w(s, sVE, qv)
                    qstate["last_q_dma"] = sch.add(
                        s, lambda e, st=st, hh=hh: st[2](e, hh), sQS, 16)
                mark(f"qv_{i}", sVE)
                qstate["last_qv"] = f"qv_{i}"

        gidx_of = lambda b, qt, gg: (b * QTB + qt) * NGQ + gg

        for b in range(NB):
            for dc2 in range(2):
                if b > 0:
                    W(g, ("xtr", b - 1, 6 + dc2))
                ld = sch.add(g, lambda e, b=b, dc=dc2: e.dma_start(
                    AP(x_bf, (dc % 2) * 512, [[1024, 128], [128, 4], [1, 128]]),
                    AP(xh, b * 524288 + dc * 128, [[1024, 128], [131072, 4], [1, 128]])),
                    sXBF[dc2 % 2], 16)
                wp[("xbf", b, dc2)] = (sXBF[dc2 % 2], ld)

            trn = [0]
            wp.setdefault(("trch", 0, "prev"), None)
            wp.setdefault(("trch", 1, "prev"), None)

            def Wtr(eng, key):
                val = wp.get(key)
                if val:
                    sch.w(eng, val[0], val[1])

            for dc in range(8):
                for tc4 in range(4):
                    fw = []
                    if b == 0 and dc == 0 and tc4 == 0:
                        fw = [("w_eye",), "prologue_v"]
                    i = trn[0]; ch = i % 2
                    W(t, ("xbf", b, dc))
                    for wkey in fw:
                        W(t, wkey)
                    if b > 0 and dc == 0 and tc4 == 0:
                        W(t, ("qkv_done", b - 1))
                    Wtr(t, ("trch", ch, "prev"))
                    tm = sch.add(t, lambda e, dc=dc, tc=tc4, ch=ch: e.transpose(
                        ps_tr[:, ch * 1024:ch * 1024 + 128],
                        x_bf[:, (dc % 2) * 512 + tc * 128:(dc % 2) * 512 + tc * 128 + 128],
                        eye[:, :]), sTT)
                    trn[0] += 1
                    eng, sem = ((sc, sSC) if ch == 0 else (v, sVE))
                    sch.w(eng, sTT, tm)
                    ev = sch.add(eng, lambda e, dc=dc, tc=tc4, ch=ch: (
                        e.copy(xT[:, dc * 512 + tc * 128:dc * 512 + tc * 128 + 128],
                               ps_tr[:, ch * 1024:ch * 1024 + 128])
                        if ch == 0 else
                        e.tensor_copy(xT[:, dc * 512 + tc * 128:dc * 512 + tc * 128 + 128],
                                      ps_tr[:, ch * 1024:ch * 1024 + 128])), sem)
                    wp[("trch", ch, "prev")] = (sem, ev)
                wp[("xtr", b, dc)] = (sTT, sch.v(sTT))
                wp[("xtevacS", b, dc)] = (sSC, sch.v(sSC))
                wp[("xtevacV", b, dc)] = (sVE, sch.v(sVE))
                if dc + 2 < 8:
                    sch.w(g, sTT, sch.v(sTT))
                    ld = sch.add(g, lambda e, b=b, dc=dc + 2: e.dma_start(
                        AP(x_bf, (dc % 2) * 512, [[1024, 128], [128, 4], [1, 128]]),
                        AP(xh, b * 524288 + dc * 128, [[1024, 128], [131072, 4], [1, 128]])),
                        sXBF[dc % 2], 16)
                    wp[("xbf", b, dc + 2)] = (sXBF[dc % 2], ld)

            for pi, (W_sb, brow, dst, dup) in enumerate((
                    (Wq_sb, bq_row, Qd, True), (Wk_sb, bk_row, Kd, True),
                    (Wv_sb, bv_row, Vs, False))):
                for tc in range(4):
                    for dh in range(2):
                        n = pi * 8 + tc * 2 + dh
                        if b == 0 and tc == 0 and dh == 0:
                            W(t, ("w_" + ("wq", "wk", "wv")[pi],))
                            W(t, ("w_" + ("bq", "bk", "bv")[pi],))
                        W(t, ("xtevacS", b, 7))
                        W(t, ("xtevacV", b, 7))
                        if n >= 2:
                            W(t, ("qkvev", b, n - 2))
                        elif b > 0:
                            sch.w(t, sVN, b * QTB)
                        sch.add(t, lambda e, brow=brow, dh=dh, n=n: e.matmul(
                            ps_pv[n % 2][:, :], ones[0:1, :], brow[0:1, dh * 512:dh * 512 + 512],
                            start=True, stop=False), None)
                        for fc in range(8):
                            tm = sch.add(t, lambda e, W_sb=W_sb, fc=fc, tc=tc, dh=dh, n=n: e.matmul(
                                ps_pv[n % 2][:, :],
                                xT[:, fc * 512 + tc * 128:fc * 512 + tc * 128 + 128],
                                W_sb[:, fc * 1024 + dh * 512:fc * 1024 + dh * 512 + 512],
                                start=False, stop=(fc == 7)), sTT if fc == 7 else None)
                        wp[("qkvmm", b, pi, tc, dh)] = (sTT, tm)
                        W(v, ("qkvmm", b, pi, tc, dh))
                        prevu = (b, pi, tc - 2) if tc >= 2 else (
                            (b, pi - 1, tc + 2) if pi > 0 else
                            ((b - 1, 2, tc + 2) if b > 0 else None))
                        if prevu is not None and dh == 0:
                            W(v, ("folddma",) + prevu)
                        ev = sch.add(v, lambda e, tc=tc, dh=dh, n=n: e.tensor_copy(
                            Qtm[:, (tc % 2) * 1024 + dh * 512:(tc % 2) * 1024 + dh * 512 + 512],
                            ps_pv[n % 2][:, :]), sVE)
                        wp[("qkvev", b, n)] = (sVE, ev)
                        wp[("qkvevac", b, pi, tc, dh)] = (sVE, ev)
                    W(s, ("qkvevac", b, pi, tc, 1))
                    if pi == 2:
                        pass
                    sch.add(s, lambda e, dst=dst, tc=tc: e.dma_start(
                        AP(dst, 16 * tc * 8192, [[8192, 16], [1024, 8], [1, 1024]]),
                        AP(Qtm, (tc % 2) * 1024, [[2048, 128], [1, 1024]])), sFoldP[tc % 2], 16)
                    if dup:
                        fd = sch.add(s, lambda e, dst=dst, tc=tc: e.dma_start(
                            AP(dst, (64 + 16 * tc) * 8192, [[8192, 16], [1024, 8], [1, 1024]]),
                            AP(Qtm, (tc % 2) * 1024, [[2048, 128], [1, 1024]])), sFoldP[tc % 2], 16)
                    else:
                        fd = sch.v(sFoldP[tc % 2])
                    wp[("folddma", b, pi, tc)] = (sFoldP[tc % 2], fd)
            mark(("qkv_done", b), sTT)
            wp[("qkvevac_last", b)] = (sVE, sch.v(sVE))
            wp[("folds", b)] = (sFoldP[0], sch.v(sFoldP[0]))
            wp[("folds1", b)] = (sFoldP[1], sch.v(sFoldP[1]))

            for cc in range(64):
                ch = cc % 2
                if cc < 2:
                    W(t, ("folds", b))
                    W(t, ("folds1", b))
                Wtr(t, ("trch", ch, "prev"))
                tm = sch.add(t, lambda e, cc=cc, ch=ch: e.transpose(
                    ps_tr[:, ch * 1024:ch * 1024 + 64],
                    Vs[0:64, cc * 128:cc * 128 + 128], eye[0:64, 0:64]), sTT)
                eng, sem = ((sc, sSC) if ch == 0 else (v, sVE))
                sch.w(eng, sTT, tm)
                ev = sch.add(eng, lambda e, cc=cc, ch=ch: (
                    e.copy(VsT[:, 65 * cc:65 * cc + 64],
                           ps_tr[:, ch * 1024:ch * 1024 + 64])
                    if ch == 0 else
                    e.tensor_copy(VsT[:, 65 * cc:65 * cc + 64],
                                  ps_tr[:, ch * 1024:ch * 1024 + 64])), sem)
                wp[("trch", ch, "prev")] = (sem, ev)
            wp[("vst", b)] = (sVE, sch.v(sVE))
            wp[("vstS", b)] = (sSC, sch.v(sSC))

            PVLAG = 2
            for qt in range(QTB):
                qtg = b * QTB + qt
                for gg in range(NGQ + PVLAG):
                    gi = gidx_of(b, qt, gg)
                    if gg < NGQ:
                        if qt == 0 and gg == 0:
                            sch.w(t, sVE, wp[("vst", b)][1])
                            sch.w(t, sSC, wp[("vstS", b)][1])
                            W(t, ("folds", b))
                            W(t, ("folds1", b))
                            if b > 0:
                                W(t, ("opev", b - 1, 7))
                        if gi >= 2:
                            sch.w(t, sEXP, gi - 1)
                        hh = (gg % 2) * 1024
                        cc = 2 * gg
                        sch.add(t, lambda e, cc=cc, qt=qt, hh=hh: e.matmul(
                            ps_sb[:, hh:hh + 512],
                            Kd[0:64, cc * 128:cc * 128 + 128],
                            Qd[0:64, qt * 512:qt * 512 + 512],
                            start=True, stop=True, tile_position=(0, 0)), None)
                        ts = sch.add(t, lambda e, cc=cc, qt=qt, hh=hh: e.matmul(
                            ps_sb[:, hh + 512:hh + 1024],
                            Kd[64:128, (cc + 1) * 128:(cc + 1) * 128 + 128],
                            Qd[64:128, qt * 512:qt * 512 + 512],
                            start=True, stop=True, tile_position=(64, 0)), sTS)
                        sch.w(sc, sTS, ts)
                        if gi >= 3:
                            sch.w(sc, sPVC, gi - 2)
                        ex = sch.add(sc, lambda e, gg=gg: e.activation(
                            PT[:, (gg % 4) * 1024:(gg % 4) * 1024 + 1024],
                            ps_sb[:, (gg % 2) * 1024:(gg % 2) * 1024 + 1024],
                            AF.Exp, scale=SOFT_SCALE), sEXP)
                        wp[("exp", gi)] = (sEXP, ex)
                    pg = gg - PVLAG
                    if pg >= 0:
                        pgi = gidx_of(b, qt, pg)
                        W(t, ("exp", pgi))
                        if pg == 0 and qtg >= 2:
                            sch.w(t, sVN, qtg - 1)
                        if pg == 0 and qt < 2:
                            W(t, ("qkvev", b, 23))
                        sch.add(t, lambda e, pg=pg, qtg=qtg: e.matmul(
                            ps_pv[qtg % 2][0:65, :],
                            VsT[:, (2 * pg) * 65:(2 * pg) * 65 + 65],
                            PT[:, (pg % 4) * 1024:(pg % 4) * 1024 + 512],
                            start=(pg == 0), stop=False), None)
                        sch.add(t, lambda e, pg=pg, qtg=qtg: e.matmul(
                            ps_pv[qtg % 2][0:65, :],
                            VsT[:, (2 * pg + 1) * 65:(2 * pg + 1) * 65 + 65],
                            PT[:, (pg % 4) * 1024 + 512:(pg % 4) * 1024 + 1024],
                            start=False, stop=(pg == NGQ - 1)), sPVC)
                sch.w(v, sPVC, NGQ * (qtg + 1))
                if qtg >= 1:
                    W(v, ("recwr", qtg - 1))
                rc = sch.add(v, lambda e, qtg=qtg: e.reciprocal(
                    rec_sb[0:1, :], ps_pv[qtg % 2][64:65, :]), sREC)
                sch.w(g, sREC, rc)
                wr = sch.add(g, lambda e: e.dma_start(
                    AP(recscrh, 0, [[512, 1], [1, 512]]), rec_sb[0:1, :]), sG, 16)
                wp[("recwr", qtg)] = (sG, wr)
                sch.w(g, sG, wr)
                if qtg >= 1:
                    sch.w(g, sVN, qtg)
                rd = sch.add(g, lambda e, qtg=qtg: e.dma_start(
                    rec_bc[0:64, 0:512],
                    AP(recscrh, 0, [[0, 64], [1, 512]])), sG, 16)
                sch.w(v, sG, rd)
                if qt == 0 and b > 0:
                    W(v, ("outproj", b - 1))
                vn = sch.add(v, lambda e, qtg=qtg, qt=qt: e.scalar_tensor_tensor(
                    A2[0:64, qt * 512:qt * 512 + 512], ps_pv[qtg % 2][0:64, :], 1.0,
                    rec_bc[0:64, 0:512],
                    ALU.mult, ALU.mult), sVN)
                sch.w(g, sVN, vn)
                ad = sch.add(g, lambda e, qt=qt: e.dma_start(
                    A2[64:128, qt * 512:qt * 512 + 512],
                    A2[0:64, qt * 512:qt * 512 + 512]), sG, 16)
                wp[("a2dup", qtg)] = (sG, ad)
                emit_qstep()
                emit_qstep()

            for tc in range(4):
                if b > 0 or tc >= 2:
                    bb, tt = (b, tc - 2) if tc >= 2 else (b - 1, tc + 2)
                    W(s, ("s1evac", bb, tt))
                ld = sch.add(s, lambda e, b=b, tc=tc: e.dma_start(
                    x_tc[:, (tc % 2) * 1024:(tc % 2) * 1024 + 1024],
                    AP(xh, (b * 512 + tc * 128) * 1024, [[1024, 128], [1, 1024]])),
                    sXL[tc % 2], 16)
                wp[("xload", b, tc)] = (sXL[tc % 2], ld)
                for dh in range(2):
                    n = tc * 2 + dh
                    u = (n % 2) * 2
                    if n == 0:
                        if b == 0:
                            W(t, ("w_wo",))
                            W(t, ("w_bo",))
                        sch.w(t, sVN, b * QTB + QTB)
                        sch.w(t, sG, wp[("a2dup", b * QTB + QTB - 1)][1])
                    if n >= 2:
                        W(t, ("opev", b, n - 2))
                    sch.add(t, lambda e, dh=dh, u=u: e.matmul(
                        ps_sb[:, u * 512:u * 512 + 512], ones[0:1, :],
                        bo_row[0:1, dh * 512:dh * 512 + 512],
                        start=True, stop=False, tile_position=(0, 0)), None)
                    for m in range(8):
                        for par in range(2):
                            r = 2 * m + par
                            lo = par * 64 * 8192
                            tm = sch.add(t, lambda e, r=r, m=m, tc=tc, dh=dh, lo=lo, par=par, u=u: e.matmul(
                                ps_sb[:, (u + par) * 512:(u + par) * 512 + 512],
                                AP(A2, lo + 2048 * tc + r, [[8192, 64], [16, 128]]),
                                AP(Wo_sb, lo + m * 1024 + dh * 512, [[8192, 64], [1, 512]]),
                                start=(par == 1 and m == 0), stop=(m == 7),
                                tile_position=(64 * par, 0)), sTT if r == 15 else None)
                    wp[("opmm", b, tc, dh)] = (sTT, tm)
                    W(v, ("opmm", b, tc, dh))
                    sch.w(v, wp[("xload", b, tc)][0], wp[("xload", b, tc)][1])
                    sch.add_sync(v, lambda e, tc=tc, dh=dh, u=u: e.scalar_tensor_tensor(
                        s1[:, (tc % 2) * 1024 + dh * 512:(tc % 2) * 1024 + dh * 512 + 512],
                        ps_sb[:, u * 512:u * 512 + 512], 1.0,
                        x_tc[:, (tc % 2) * 1024 + dh * 512:(tc % 2) * 1024 + dh * 512 + 512],
                        ALU.mult, ALU.add), sVE)
                    ev = sch.add_sync(v, lambda e, tc=tc, dh=dh, u=u: e.scalar_tensor_tensor(
                        s1[:, (tc % 2) * 1024 + dh * 512:(tc % 2) * 1024 + dh * 512 + 512],
                        ps_sb[:, (u + 1) * 512:(u + 1) * 512 + 512], 1.0,
                        s1[:, (tc % 2) * 1024 + dh * 512:(tc % 2) * 1024 + dh * 512 + 512],
                        ALU.mult, ALU.add), sVE)
                    wp[("opev", b, n)] = (sVE, ev)
                    wp[("s1evac", b, tc)] = (sVE, ev)
                if tc == 3:
                    mark(("outproj", b), sTT)
                sl = (tc % 2) * 1024
                if b > 0 or tc >= 2:
                    bb, tt = (b, tc - 2) if tc >= 2 else (b - 1, tc + 2)
                    W(v, ("x1w", bb, tt))
                sch.add_sync(v, lambda e, sl=sl: e.reduce_sum(
                    negsum[:, :], s1[:, sl:sl + 1024], AX.X, negate=True), sVE)
                sch.add_sync(v, lambda e: e.tensor_scalar_mul(
                    negmean[:, :], negsum[:, :], 1.0 / 1024.0), sVE)
                sch.add_sync(v, lambda e, sl=sl: e.tensor_scalar(
                    s1[:, sl:sl + 1024], s1[:, sl:sl + 1024],
                    negmean[:, :], None, ALU.add), sVE)
                sch.add_sync(v, lambda e, sl=sl: e.scalar_tensor_tensor(
                    x1[:, sl:sl + 1024], s1[:, sl:sl + 1024], 1.0,
                    s1[:, sl:sl + 1024], ALU.mult, ALU.mult,
                    accum_out=varsum[:, :]), sVE)
                lv = sch.add(v, lambda e: e.tensor_scalar(
                    vareps[:, :], varsum[:, :], 1.0 / 1024.0, LN_EPS,
                    ALU.mult, ALU.add), sVE)
                sch.w(sc, sVE, lv)
                sch.add_sync(sc, lambda e: e.activation(uln[:, :], vareps[:, :], AF.Ln), sSC)
                sr = sch.add(sc, lambda e: e.activation(
                    rstd[:, :], uln[:, :], AF.Exp, scale=-0.5), sSC)
                sch.w(v, sSC, sr)
                x1v = sch.add(v, lambda e, sl=sl: e.tensor_scalar(
                    x1[:, sl:sl + 1024], s1[:, sl:sl + 1024],
                    rstd[:, :], None, ALU.mult), sVE)
                sch.w(s, sVE, x1v)
                xw = sch.add(s, lambda e, b=b, tc=tc, sl=sl: e.dma_start(
                    AP(x1scrh, (b * 512 + tc * 128) * 1024, [[1024, 128], [1, 1024]]),
                    x1[:, sl:sl + 1024]), sX1W[tc % 2], 16)
                wp[("x1w", b, tc)] = (sX1W[tc % 2], xw)
                emit_qstep()
                emit_qstep()

        while qstate["i"] < len(qsteps):
            emit_qstep()

        ATT_T = sch.v(sTT); ATT_V = sch.v(sVE); ATT_G = sch.v(sG); ATT_S = sch.v(sS)

        sch.w(s, sPVC, NB * QTB * NGQ)
        sch.w(s, sQS, sch.v(sQS))
        sch.w(s, sG, wp[("a2dup", NB * QTB - 1)][1])
        for dc in range(8):
            sch.add(s, lambda e, dc=dc: e.dma_start_transpose(
                WqT1[:, dc * 4096:dc * 4096 + 4096],
                AP(wqt1h, dc * 128, [[1024, 4096], [1, 128]])), sWQ1, 16)
        wp[("wqt1",)] = (sWQ1, sch.v(sWQ1))
        W(s, ("outproj", NB - 1))
        sch.w(s, sVE, ATT_V)
        for oc in range(32):
            sch.add(s, lambda e, oc=oc: e.dma_start_transpose(
                WqT2[:, oc * 1024:oc * 1024 + 1024],
                AP(wq2h, oc * 128, [[4096, 1024], [1, 128]])), sWQT, 16)
        wp[("wqt",)] = (sWQT, sch.v(sWQT))

        sch.w(g, sVE, ATT_V)
        for i in range(2):
            sch.w(g, sX1W[i], sch.v(sX1W[i]))
        sch.add(g, lambda e: e.dma_start(
            AP(b1_rows, 0, [[4096, 1], [1, 4096]]),
            AP(b1h, 0, [[4096, 1], [1, 4096]])), sXB1, 16)
        for b in range(NB):
            cw = sch.add(g, lambda e, b=b: e.dma_start(
                AP(x1bfh, b * 524288, [[1024, 512], [1, 1024]]),
                AP(x1scrh, b * 524288, [[1024, 512], [1, 1024]])), sXB1, 16)
        wp[("x1bf",)] = (sXB1, sch.v(sXB1))

        for b in range(NB):
            W(s, ("x1bf",))
            if b == 0:
                W(s, ("outproj", NB - 1))
            if b > 0:
                W(s, ("w1done", b - 1))
            for dc in range(8):
                sch.add(s, lambda e, b=b, dc=dc: e.dma_start_transpose(
                    x1T[:, dc * 512:dc * 512 + 512],
                    AP(x1bfh, b * 524288 + dc * 128, [[1024, 512], [1, 128]])), sX1T, 16)
            wp[("x1T", b)] = (sX1T, sch.v(sX1T))

            for oc in range(32):
                W(t, ("x1T", b))
                W(t, ("wqt1",))
                if oc >= 2:
                    W(t, ("hevac", b, oc - 2))
                elif b > 0:
                    W(t, ("hevac", b - 1, oc + 30))
                sch.add(t, lambda e, oc=oc: e.matmul(
                    ps_pv[oc % 2][:, :], b1_rows[0:1, oc * 128:oc * 128 + 128],
                    ones512[0:1, :], start=True, stop=False), None)
                for dc in range(8):
                    tm = sch.add(t, lambda e, oc=oc, dc=dc: e.matmul(
                        ps_pv[oc % 2][:, :],
                        WqT1[:, dc * 4096 + oc * 128:dc * 4096 + oc * 128 + 128],
                        x1T[:, dc * 512:dc * 512 + 512],
                        start=False, stop=(dc == 7)), sTT if dc == 7 else None)
                wp[("w1mm", b, oc)] = (sTT, tm)
                W(v, ("w1mm", b, oc))
                if b > 0 and oc < 2:
                    W(v, ("w2done", b - 1))
                ev = sch.add(v, lambda e, oc=oc: e.tensor_scalar_max(
                    hT[:, oc * 512:oc * 512 + 512], ps_pv[oc % 2][:, :], 0.0), sVE)
                wp[("hevac", b, oc)] = (sVE, ev)
            mark(("w1done", b), sTT)

            for tsub in range(4):
                if b > 0 or tsub >= 2:
                    bb, tt = (b, tsub - 2) if tsub >= 2 else (b - 1, tsub + 2)
                    W(s, ("s2done", bb, tt))
                else:
                    W(s, ("x1bf",))
                ld = sch.add(s, lambda e, b=b, tsub=tsub: e.dma_start(
                    x1f[:, (tsub % 2) * 1024:(tsub % 2) * 1024 + 1024],
                    AP(x1scrh, (b * 512 + tsub * 128) * 1024, [[1024, 128], [1, 1024]])),
                    sX1F[tsub % 2], 16)
                wp[("x1fload", b, tsub)] = (sX1F[tsub % 2], ld)
                for dh in range(2):
                    W(t, ("hevac", b, 31))
                    if b == 0 and tsub == 0 and dh == 0:
                        W(t, ("wqt",))
                        W(t, ("wqt1",))
                    if b * 4 + tsub >= 2:
                        pb, pt = divmod(b * 4 + tsub - 2, 4)
                        W(t, ("s2evac", pb, pt))
                    sch.add(t, lambda e, tsub=tsub, dh=dh: e.matmul(
                        ps_sb[:, (tsub % 2) * 1024 + dh * 512:(tsub % 2) * 1024 + dh * 512 + 512],
                        ones[0:1, :], b2_row[0:1, dh * 512:dh * 512 + 512],
                        start=True, stop=False), None)
                    for oc in range(32):
                        tm = sch.add(t, lambda e, tsub=tsub, dh=dh, oc=oc: e.matmul(
                            ps_sb[:, (tsub % 2) * 1024 + dh * 512:(tsub % 2) * 1024 + dh * 512 + 512],
                            hT[:, oc * 512 + tsub * 128:oc * 512 + tsub * 128 + 128],
                            WqT2[:, oc * 1024 + dh * 512:oc * 1024 + dh * 512 + 512],
                            start=False, stop=(oc == 31)),
                            sTT if (dh == 1 and oc == 31) else None)
                wp[("w2mm", b, tsub)] = (sTT, tm)
                W(v, ("w2mm", b, tsub))
                sch.w(v, sX1F[tsub % 2], ld)
                if b * 4 + tsub >= 1:
                    pb, pt = divmod(b * 4 + tsub - 1, 4)
                    W(v, ("outdma", pb, pt))
                sl = 0
                xsl = (tsub % 2) * 1024
                ev = sch.add_sync(v, lambda e, tsub=tsub, xsl=xsl: e.scalar_tensor_tensor(
                    s2[:, 0:1024], ps_sb[:, (tsub % 2) * 1024:(tsub % 2) * 1024 + 1024], 1.0,
                    x1f[:, xsl:xsl + 1024], ALU.mult, ALU.add), sVE)
                wp[("s2evac", b, tsub)] = (sVE, ev)
                wp[("s2done", b, tsub)] = (sVE, ev)
                sch.add_sync(v, lambda e: e.reduce_sum(
                    negsum[:, :], s2[:, 0:1024], AX.X, negate=True), sVE)
                sch.add_sync(v, lambda e: e.tensor_scalar_mul(
                    negmean[:, :], negsum[:, :], 1.0 / 1024.0), sVE)
                sch.add_sync(v, lambda e: e.tensor_scalar(
                    s2[:, 0:1024], s2[:, 0:1024],
                    negmean[:, :], None, ALU.add), sVE)
                sch.add_sync(v, lambda e: e.scalar_tensor_tensor(
                    outb[:, 0:1024], s2[:, 0:1024], 1.0,
                    s2[:, 0:1024], ALU.mult, ALU.mult,
                    accum_out=varsum[:, :]), sVE)
                lv = sch.add(v, lambda e: e.tensor_scalar(
                    vareps[:, :], varsum[:, :], 1.0 / 1024.0, LN_EPS,
                    ALU.mult, ALU.add), sVE)
                sch.w(sc, sVE, lv)
                sch.add_sync(sc, lambda e: e.activation(uln[:, :], vareps[:, :], AF.Ln), sSC)
                sr = sch.add(sc, lambda e: e.activation(
                    rstd[:, :], uln[:, :], AF.Exp, scale=-0.5), sSC)
                sch.w(v, sSC, sr)
                ov = sch.add(v, lambda e: e.tensor_scalar(
                    outb[:, 0:1024], s2[:, 0:1024],
                    rstd[:, :], None, ALU.mult), sVE)
                wp[("outw", b, tsub)] = (sVE, ov)
                sch.w(s, sVE, ov)
                od = sch.add(s, lambda e, b=b, tsub=tsub: e.dma_start(
                    AP(yh, (b * 512 + tsub * 128) * 1024, [[1024, 128], [1, 1024]]),
                    outb[:, 0:1024]), sOUT, 16)
                wp[("outdma", b, tsub)] = (sOUT, od)
            mark(("w2done", b), sTT)

        sch.w(s, sOUT, sch.v(sOUT))
        sch.w(s, sX1T, sch.v(sX1T))
        for i in range(2):
            sch.w(s, sX1F[i], sch.v(sX1F[i]))
        sch.w(g, sG, sch.v(sG))
        sch.w(g, sXB1, sch.v(sXB1))

        @blk.sync
        def _(e):
            for f in sch.L["sync"]:
                f(e)

        @blk.gpsimd
        def _(e):
            for f in sch.L["gpsimd"]:
                f(e)

        @blk.tensor
        def _(e):
            for f in sch.L["tensor"]:
                f(e)

        @blk.vector
        def _(e):
            for f in sch.L["vector"]:
                f(e)

        @blk.scalar
        def _(e):
            for f in sch.L["scalar"]:
                f(e)

    return nc


def kernel(L0_attn_x, L0_attn_Wq, L0_attn_bq, L0_attn_Wk, L0_attn_bk,
           L0_attn_Wv, L0_attn_bv, L0_attn_Wo, L0_attn_bo,
           L0_ffn_W1, L0_ffn_b1, L0_ffn_W2, L0_ffn_b2):
    x = np.ascontiguousarray(np.asarray(L0_attn_x, dtype=np.float32)).reshape(16384, 1024)
    com = dict(
        Wq=np.ascontiguousarray(np.asarray(L0_attn_Wq, np.float32)),
        bq=np.ascontiguousarray(np.asarray(L0_attn_bq, np.float32)),
        Wk=np.ascontiguousarray(np.asarray(L0_attn_Wk, np.float32)),
        bk=np.ascontiguousarray(np.asarray(L0_attn_bk, np.float32)),
        Wv=np.ascontiguousarray(np.asarray(L0_attn_Wv, np.float32)),
        bv=np.ascontiguousarray(np.asarray(L0_attn_bv, np.float32)),
        Wo=np.ascontiguousarray(np.asarray(L0_attn_Wo, np.float32)),
        bo=np.ascontiguousarray(np.asarray(L0_attn_bo, np.float32)),
        W1=np.ascontiguousarray(np.asarray(L0_ffn_W1, np.float32)),
        b1=np.ascontiguousarray(np.asarray(L0_ffn_b1, np.float32)),
        W2=np.ascontiguousarray(np.asarray(L0_ffn_W2, np.float32)),
        b2=np.ascontiguousarray(np.asarray(L0_ffn_b2, np.float32)),
        eye=np.eye(128, dtype=np.float32),
    )
    nc = build_nc()
    in_maps = [dict(com, x=np.ascontiguousarray(x[c * 2048:(c + 1) * 2048]))
               for c in range(8)]
    res = run_bass_kernel_spmd(nc, in_maps, core_ids=list(range(8)))
    out = np.concatenate([np.asarray(r["y"], np.float32) for r in res.results], axis=0)
    return out.reshape(2, 8192, 1024)


# version 10
# speedup vs baseline: 1.1571x; 1.0053x over previous
import numpy as np
import concourse.bass as bass
import concourse.mybir as mybir
from concourse.bass_utils import run_bass_kernel_spmd

F32 = mybir.dt.float32
BF16 = mybir.dt.bfloat16
AF = mybir.ActivationFunctionType
ALU = mybir.AluOpType
AX = mybir.AxisListType
AP = bass.AP

NB = 4
QTB = 16
NGQ = 32
SOFT_SCALE = float(2.0 ** -10)
LN_EPS = 1e-5
QN = 4096 * 1024


class Sch:

    def __init__(self):
        self.L = {k: [] for k in ("sync", "gpsimd", "tensor", "vector", "scalar")}
        self.c = {}

    def w(self, eng, sem, v):
        if v > 0:
            self.L[eng].append(lambda e, s=sem, vv=v: e.wait_ge(s, vv))

    def add(self, eng, f, inc=None, n=1):
        if inc is None:
            self.L[eng].append(f)
            return None
        tot = self.c[id(inc)] = self.c.get(id(inc), 0) + n
        self.L[eng].append(lambda e, ff=f, s=inc, nn=n: ff(e).then_inc(s, nn))
        return tot

    def v(self, sem):
        return self.c.get(id(sem), 0)

    def add_sync(self, eng, f, inc):
        tot = self.add(eng, f, inc)
        self.w(eng, inc, tot)
        return tot


def build_nc():
    nc = bass.Bass("TRN2", target_bir_lowering=False)

    xh = nc.dram_tensor("x", [2048, 1024], F32, kind="ExternalInput")
    wqh = nc.dram_tensor("Wq", [1024, 1024], F32, kind="ExternalInput")
    bqh = nc.dram_tensor("bq", [1024], F32, kind="ExternalInput")
    wkh = nc.dram_tensor("Wk", [1024, 1024], F32, kind="ExternalInput")
    bkh = nc.dram_tensor("bk", [1024], F32, kind="ExternalInput")
    wvh = nc.dram_tensor("Wv", [1024, 1024], F32, kind="ExternalInput")
    bvh = nc.dram_tensor("bv", [1024], F32, kind="ExternalInput")
    woh = nc.dram_tensor("Wo", [1024, 1024], F32, kind="ExternalInput")
    boh = nc.dram_tensor("bo", [1024], F32, kind="ExternalInput")
    w1h = nc.dram_tensor("W1", [4096, 1024], F32, kind="ExternalInput")
    b1h = nc.dram_tensor("b1", [4096], F32, kind="ExternalInput")
    w2h = nc.dram_tensor("W2", [1024, 4096], F32, kind="ExternalInput")
    b2h = nc.dram_tensor("b2", [1024], F32, kind="ExternalInput")
    eyeh = nc.dram_tensor("eye", [128, 128], F32, kind="ExternalInput")
    yh = nc.dram_tensor("y", [2048, 1024], F32, kind="ExternalOutput")

    wqt1h = nc.dram_tensor("wqt1", [4096, 1024], BF16, kind="Internal")
    wq2h = nc.dram_tensor("wq2", [1024, 4096], BF16, kind="Internal")
    x1scrh = nc.dram_tensor("x1scr", [2048, 1024], F32, kind="Internal")
    x1bfh = nc.dram_tensor("x1bfscr", [2048, 1024], BF16, kind="Internal")
    recscrh = nc.dram_tensor("recscr", [512], F32, kind="Internal")
    gscrh = nc.dram_tensor("gscr", [256], F32, kind="Internal")
    thrscrh = nc.dram_tensor("thrscr", [8], F32, kind="Internal")

    BASE = 16512
    off = [BASE]

    def alloc(name, shape, dtype, at=None):
        if at is None:
            at = off[0]
            nbytes = shape[-1] * (2 if dtype == BF16 else 4)
            for d in shape[1:-1]:
                nbytes *= d
            off[0] = at + ((nbytes + 31) // 32) * 32
        assert off[0] <= 229344, f"sbuf overflow at {name}: {off[0]}"
        return nc.alloc_sbuf_tensor_at(name, shape, dtype, offset=at)

    eye = alloc("eye", [128, 128], BF16)
    ones = alloc("ones", [1, 128], BF16)
    ones512 = alloc("ones512", [1, 512], BF16)
    bq_row = alloc("bq_row", [1, 1024], BF16)
    bk_row = alloc("bk_row", [1, 1024], BF16)
    bv_row = alloc("bv_row", [1, 1024], BF16)
    bo_row = alloc("bo_row", [1, 1024], BF16)
    b2_row = alloc("b2_row", [1, 1024], BF16)
    thrbc = alloc("thrbc", [128, 8], F32)
    gv = alloc("gv", [1, 16], F32)
    acc1 = alloc("acc1", [128, 1], F32)
    acc2 = alloc("acc2", [128, 1], F32)
    gtmp = alloc("gtmp", [128, 1], F32)
    negsum = alloc("negsum", [128, 1], F32)
    negmean = alloc("negmean", [128, 1], F32)
    varsum = alloc("varsum", [128, 1], F32)
    vareps = alloc("vareps", [128, 1], F32)
    uln = alloc("uln", [128, 1], F32)
    rstd = alloc("rstd", [128, 1], F32)
    warm = alloc("warm", [1, 256], BF16)

    A0 = off[0]

    Qd = alloc("Qd", [128, 8192], BF16)
    Kd = alloc("Kd", [128, 8192], BF16)
    Vs = alloc("Vs", [64, 8192], BF16)
    VsT = alloc("VsT", [128, 4160], BF16)
    PT = alloc("PT", [128, 4096], BF16)
    Wq_sb = alloc("Wq_sb", [128, 8192], BF16)
    Wk_sb = alloc("Wk_sb", [128, 8192], BF16)
    Wv_sb = alloc("Wv_sb", [128, 8192], BF16)
    Wo_sb = alloc("Wo_sb", [128, 8192], BF16)
    A2 = alloc("A2", [128, 8192], BF16)
    xT = alloc("xT", [128, 4096], BF16)
    x_tc = alloc("x_tc", [128, 2048], F32)
    x_bf = alloc("x_bf", [128, 1024], BF16)
    Qtm = alloc("Qtm", [128, 2048], BF16)
    rec_sb = alloc("rec_sb", [1, 512], F32)
    rec_bc = alloc("rec_bc", [64, 512], F32)
    s1 = alloc("s1", [128, 2048], F32)
    x1 = alloc("x1", [128, 2048], F32)
    ATTN_END = off[0]

    off[0] = A0
    WqT1 = alloc("WqT1", [128, 32768], BF16)
    WqT2 = alloc("WqT2", [128, 32768], BF16)
    hT = alloc("hT", [128, 16384], BF16)
    x1T = alloc("x1T", [128, 4096], BF16)
    x1f = alloc("x1f", [128, 2048], F32)
    s2 = alloc("s2", [128, 1024], F32)
    outb = alloc("outb", [128, 1024], F32)
    b1_rows = alloc("b1_rows", [1, 4096], BF16)
    FFN_END = off[0]
    off[0] = ATTN_END
    grow = alloc("grow", [1, 256], F32)
    Wch = alloc("Wch", [128, 1024], F32)
    tq1 = alloc("tq1", [128, 512], BF16)
    tq2 = alloc("tq2", [128, 512], BF16)
    qout = alloc("qout", [128, 512], BF16)
    QNT_END = off[0]
    assert QNT_END <= 229344 and FFN_END <= 229344, (ATTN_END, FFN_END, QNT_END)
    off[0] = max(ATTN_END, FFN_END, QNT_END)

    sch = Sch()

    from contextlib import ExitStack
    with ExitStack() as es:
        E = es.enter_context
        ps_sb = E(nc.psum_tensor("ps_sb", [128, 2048], F32))
        ps_pv = [E(nc.psum_tensor(f"ps_pv{i}", [128, 512], F32)) for i in range(2)]
        ps_tr = E(nc.psum_tensor("ps_tr", [128, 2048], BF16))

        sS = E(nc.semaphore("sS"))
        sG = E(nc.semaphore("sG"))
        sQS = E(nc.semaphore("sQS"))
        sQS2 = E(nc.semaphore("sQS2"))
        sFoldP = [E(nc.semaphore(f"sFold{i}")) for i in range(2)]
        sFIN = E(nc.semaphore("sFIN"))
        sXL = [E(nc.semaphore(f"sXL{i}")) for i in range(2)]
        sX1W = [E(nc.semaphore(f"sX1W{i}")) for i in range(2)]
        sXBF = [E(nc.semaphore(f"sXBF{i}")) for i in range(2)]
        sWQT = E(nc.semaphore("sWQT"))
        sX1T = E(nc.semaphore("sX1T"))
        sX1F = [E(nc.semaphore(f"sX1F{i}")) for i in range(2)]
        sOUT = E(nc.semaphore("sOUT"))
        sXB1 = E(nc.semaphore("sXB1"))
        sCB = [E(nc.semaphore(f"sCB{i}")) for i in range(4)]
        sTS = E(nc.semaphore("sTS"))
        sEXP = E(nc.semaphore("sEXP"))
        sPVC = E(nc.semaphore("sPVC"))
        sREC = E(nc.semaphore("sREC"))
        sVN = E(nc.semaphore("sVN"))
        sVE = E(nc.semaphore("sVE"))
        sTT = E(nc.semaphore("sTT"))
        sSC = E(nc.semaphore("sSC"))
        sWG = [E(nc.semaphore(f"sWG{i}")) for i in range(5)]
        sWQ1 = E(nc.semaphore("sWQ1"))
        blk = E(nc.Block())

        wp = {}

        def W(eng, key):
            if key in wp:
                sem, v = wp[key]
                sch.w(eng, sem, v)

        def mark(key, sem):
            wp[key] = (sem, sch.v(sem))

        g = "gpsimd"; s = "sync"; t = "tensor"; v = "vector"; sc = "scalar"

        sch.add(g, lambda e: e.dma_start(eye[:, :], eyeh.ap()[:, :]), sWG[0], 16)
        wp[("w_eye",)] = (sWG[0], 16)
        for i, (nm, W_sb, wh, row, bh) in enumerate((
                ("q", Wq_sb, wqh, bq_row, bqh),
                ("k", Wk_sb, wkh, bk_row, bkh),
                ("v", Wv_sb, wvh, bv_row, bvh))):
            sch.add(g, lambda e, d=W_sb, sr=wh: e.dma_start(
                AP(d, 0, [[8192, 128], [1024, 8], [1, 1024]]),
                AP(sr, 0, [[1024, 128], [131072, 8], [1, 1024]])), sWG[1 + i], 16)
            sch.add(g, lambda e, d=row, sr=bh: e.dma_start(
                AP(d, 0, [[1024, 1], [1, 1024]]),
                AP(sr, 0, [[1024, 1], [1, 1024]])), sWG[1 + i], 16)
            wp[("w_w" + nm,)] = wp[("w_b" + nm,)] = (sWG[1 + i], 32)
        sch.add(g, lambda e: e.dma_start(
            AP(Wo_sb, 0, [[8192, 64], [1024, 8], [1, 1024]]),
            AP(woh, 0, [[1024, 64], [131072, 8], [1, 1024]])), sWG[4], 16)
        sch.add(g, lambda e: e.dma_start(
            AP(Wo_sb, 64 * 8192, [[8192, 64], [1024, 8], [1, 1024]]),
            AP(woh, 65536, [[1024, 64], [131072, 8], [1, 1024]])), sWG[4], 16)
        for row, bh in ((bo_row, boh), (b2_row, b2h)):
            sch.add(g, lambda e, d=row, sr=bh: e.dma_start(
                AP(d, 0, [[1024, 1], [1, 1024]]),
                AP(sr, 0, [[1024, 1], [1, 1024]])), sWG[4], 16)
        wp[("w_wo",)] = wp[("w_bo",)] = wp[("w_b2",)] = (sWG[4], 64)
        mark("prologue_g", sG)

        sch.add(v, lambda e: e.memset(ones[0:1, :], 1.0), sVE)
        sch.add(v, lambda e: e.memset(ones512[0:1, :], 1.0), sVE)
        sch.add(v, lambda e: e.memset(VsT[:, :], 1.0), sVE)
        sch.add(v, lambda e: e.memset(acc1[:, :], 0.0), sVE)
        sch.add(v, lambda e: e.memset(acc2[:, :], 0.0), sVE)
        mark("prologue_v", sVE)

        W(sc, "prologue_v")
        sch.add(sc, lambda e: e.activation(warm[0:1, 0:128], ones[0:1, :], AF.Exp), sSC)
        sch.add(sc, lambda e: e.activation(warm[0:1, 128:256], ones[0:1, :], AF.Ln), sSC)

        qsteps = []

        def gamma_step(widx, ch):
            def fs(e, widx=widx, ch=ch):
                if widx == 0:
                    return e.dma_start(Wch[:, :], AP(w1h, ch * 131072,
                                                    [[1024, 128], [1, 1024]]))
                a, cg = ch % 8, ch // 8
                return e.dma_start(Wch[:, :], AP(w2h, a * 524288 + cg * 1024,
                                                 [[4096, 128], [1, 1024]]))
            def fv(ops, widx=widx, ch=ch):
                acc = acc1 if widx == 0 else acc2
                ops.append((v, "sync", lambda e: e.reduce_sum(gtmp[:, :], Wch[:, :], AX.X,
                                                      apply_absolute_value=True), sVE))
                ops.append((v, "plain", lambda e, a=acc: e.tensor_add(a[:, :], a[:, :], gtmp[:, :]), sVE))
            return ("gamma", fs, fv)

        def finalize_step():
            return ("finalize", None, None)

        def quant_step(widx, ch):
            def fs(e, widx=widx, ch=ch):
                if widx == 0:
                    return e.dma_start(Wch[:, :], AP(w1h, ch * 131072,
                                                    [[1024, 128], [1, 1024]]))
                a, cg = ch % 8, ch // 8
                return e.dma_start(Wch[:, :], AP(w2h, a * 524288 + cg * 1024,
                                                 [[4096, 128], [1, 1024]]))
            def fo(e, hh, widx=widx, ch=ch):
                if widx == 0:
                    return e.dma_start(AP(wqt1h, ch * 131072 + hh * 512,
                                          [[1024, 128], [1, 512]]), qout[:, :])
                a, cg = ch % 8, ch // 8
                return e.dma_start(AP(wq2h, a * 524288 + cg * 1024 + hh * 512,
                                      [[4096, 128], [1, 512]]), qout[:, :])
            return ("quant", fs, fo, widx)

        for widx in (0, 1):
            for ch in range(32):
                qsteps.append(gamma_step(widx, ch))
        qsteps.append(finalize_step())
        for widx in (0, 1):
            for ch in range(32):
                qsteps.append(quant_step(widx, ch))
        qstate = {"i": 0, "thr_ready": None, "last_q_dma": None, "last_qv": None}

        def emit_qstep():
            i = qstate["i"]
            if i >= len(qsteps):
                return
            qstate["i"] = i + 1
            st = qsteps[i]
            if st[0] == "gamma":
                if qstate["last_qv"]:
                    W(s, qstate["last_qv"])
                ld = sch.add(s, st[1], sS, 16)
                sch.w(v, sS, ld)
                ops = []
                st[2](ops)
                for eng, kind, f, sem in ops:
                    (sch.add_sync if kind == "sync" else sch.add)(eng, f, sem)
                mark(f"qv_{i}", sVE)
                qstate["last_qv"] = f"qv_{i}" 
            elif st[0] == "finalize":
                sch.w(g, sVE, sch.v(sVE))
                sch.add(g, lambda e: e.dma_start(AP(gscrh, 0, [[1, 128], [1, 1]]),
                                                 acc1[:, :]), sFIN, 16)
                sch.add(g, lambda e: e.dma_start(AP(gscrh, 128, [[1, 128], [1, 1]]),
                                                 acc2[:, :]), sFIN, 16)
                sch.w(g, sFIN, sch.v(sFIN))
                gld = sch.add(g, lambda e: e.dma_start(
                    grow[0:1, :], AP(gscrh, 0, [[256, 1], [1, 256]])), sFIN, 16)
                sch.w(v, sFIN, gld)
                sch.add(v, lambda e: e.reduce_sum(gv[0:1, 0:1], grow[0:1, 0:128], AX.X), sVE)
                sch.add_sync(v, lambda e: e.reduce_sum(gv[0:1, 1:2], grow[0:1, 128:256], AX.X), sVE)
                c1, c2 = 0.5 / QN, 0.5 * 1e-5
                sch.add(v, lambda e: e.tensor_scalar(gv[0:1, 4:5], gv[0:1, 0:1], c1, c2, ALU.mult, ALU.add), sVE)
                sch.add(v, lambda e: e.tensor_scalar(gv[0:1, 5:6], gv[0:1, 0:1], -c1, -c2, ALU.mult, ALU.add), sVE)
                sch.add(v, lambda e: e.tensor_scalar(gv[0:1, 6:7], gv[0:1, 1:2], c1, c2, ALU.mult, ALU.add), sVE)
                thr_v = sch.add(v, lambda e: e.tensor_scalar(gv[0:1, 7:8], gv[0:1, 1:2], -c1, -c2, ALU.mult, ALU.add), sVE)
                sch.w(g, sVE, thr_v)
                sch.add(g, lambda e: e.dma_start(AP(thrscrh, 0, [[8, 1], [1, 4]]),
                                                 gv[0:1, 4:8]), sFIN, 16)
                sch.w(g, sFIN, sch.v(sFIN))
                thr_g = sch.add(g, lambda e: e.dma_start(
                    thrbc[:, 0:4], AP(thrscrh, 0, [[0, 128], [1, 4]])), sFIN, 16)
                qstate["thr_ready"] = thr_g
            else:
                widx = st[3]
                if qstate["last_qv"]:
                    W(s, qstate["last_qv"])
                ld = sch.add(s, st[1], sS, 16)
                sch.w(v, sS, ld)
                if qstate["thr_ready"] is not None:
                    sch.w(v, sFIN, qstate["thr_ready"])
                    qstate["thr_ready"] = None
                cp, cn = 2 * widx, 2 * widx + 1
                for hh in range(2):
                    if qstate["last_q_dma"] is not None:
                        sch.w(v, *qstate["last_q_dma"])
                    sch.add(v, lambda e, cp=cp, hh=hh: e.tensor_scalar(
                        tq1[:, :], Wch[:, hh * 512:hh * 512 + 512],
                        thrbc[:, cp:cp + 1], None, ALU.is_gt), sVE)
                    sch.add_sync(v, lambda e, cn=cn, hh=hh: e.tensor_scalar(
                        tq2[:, :], Wch[:, hh * 512:hh * 512 + 512],
                        thrbc[:, cn:cn + 1], None, ALU.is_lt), sVE)
                    qv = sch.add_sync(v, lambda e: e.tensor_sub(
                        qout[:, :], tq1[:, :], tq2[:, :]), sVE)
                    sch.w(s, sVE, qv)
                    qsem = sQS if widx == 0 else sQS2
                    qstate["last_q_dma"] = (qsem, sch.add(
                        s, lambda e, st=st, hh=hh: st[2](e, hh), qsem, 16))
                mark(f"qv_{i}", sVE)
                qstate["last_qv"] = f"qv_{i}"

        gidx_of = lambda b, qt, gg: (b * QTB + qt) * NGQ + gg

        for b in range(NB):
            for dc2 in range(2):
                if b > 0:
                    W(g, ("xtr", b - 1, 6 + dc2))
                ld = sch.add(g, lambda e, b=b, dc=dc2: e.dma_start(
                    AP(x_bf, (dc % 2) * 512, [[1024, 128], [128, 4], [1, 128]]),
                    AP(xh, b * 524288 + dc * 128, [[1024, 128], [131072, 4], [1, 128]])),
                    sXBF[dc2 % 2], 16)
                wp[("xbf", b, dc2)] = (sXBF[dc2 % 2], ld)

            trn = [0]
            wp.setdefault(("trch", 0, "prev"), None)
            wp.setdefault(("trch", 1, "prev"), None)

            def Wtr(eng, key):
                val = wp.get(key)
                if val:
                    sch.w(eng, val[0], val[1])

            for dc in range(8):
                for tc4 in range(4):
                    fw = []
                    if b == 0 and dc == 0 and tc4 == 0:
                        fw = [("w_eye",), "prologue_v"]
                    i = trn[0]; ch = i % 2
                    W(t, ("xbf", b, dc))
                    for wkey in fw:
                        W(t, wkey)
                    if b > 0 and dc == 0 and tc4 == 0:
                        W(t, ("qkv_done", b - 1))
                    Wtr(t, ("trch", ch, "prev"))
                    tm = sch.add(t, lambda e, dc=dc, tc=tc4, ch=ch: e.transpose(
                        ps_tr[:, ch * 1024:ch * 1024 + 128],
                        x_bf[:, (dc % 2) * 512 + tc * 128:(dc % 2) * 512 + tc * 128 + 128],
                        eye[:, :]), sTT)
                    trn[0] += 1
                    eng, sem = ((sc, sSC) if ch == 0 else (v, sVE))
                    sch.w(eng, sTT, tm)
                    ev = sch.add(eng, lambda e, dc=dc, tc=tc4, ch=ch: (
                        e.copy(xT[:, dc * 512 + tc * 128:dc * 512 + tc * 128 + 128],
                               ps_tr[:, ch * 1024:ch * 1024 + 128])
                        if ch == 0 else
                        e.tensor_copy(xT[:, dc * 512 + tc * 128:dc * 512 + tc * 128 + 128],
                                      ps_tr[:, ch * 1024:ch * 1024 + 128])), sem)
                    wp[("trch", ch, "prev")] = (sem, ev)
                wp[("xtr", b, dc)] = (sTT, sch.v(sTT))
                wp[("xtevacS", b, dc)] = (sSC, sch.v(sSC))
                wp[("xtevacV", b, dc)] = (sVE, sch.v(sVE))
                if dc + 2 < 8:
                    sch.w(g, sTT, sch.v(sTT))
                    ld = sch.add(g, lambda e, b=b, dc=dc + 2: e.dma_start(
                        AP(x_bf, (dc % 2) * 512, [[1024, 128], [128, 4], [1, 128]]),
                        AP(xh, b * 524288 + dc * 128, [[1024, 128], [131072, 4], [1, 128]])),
                        sXBF[dc % 2], 16)
                    wp[("xbf", b, dc + 2)] = (sXBF[dc % 2], ld)

            for pi, (W_sb, brow, dst, dup) in enumerate((
                    (Wq_sb, bq_row, Qd, True), (Wk_sb, bk_row, Kd, True),
                    (Wv_sb, bv_row, Vs, False))):
                for tc in range(4):
                    for dh in range(2):
                        n = pi * 8 + tc * 2 + dh
                        if b == 0 and tc == 0 and dh == 0:
                            W(t, ("w_" + ("wq", "wk", "wv")[pi],))
                            W(t, ("w_" + ("bq", "bk", "bv")[pi],))
                        W(t, ("xtevacS", b, 7))
                        W(t, ("xtevacV", b, 7))
                        if n >= 2:
                            W(t, ("qkvev", b, n - 2))
                        elif b > 0:
                            sch.w(t, sVN, b * QTB)
                        sch.add(t, lambda e, brow=brow, dh=dh, n=n: e.matmul(
                            ps_pv[n % 2][:, :], ones[0:1, :], brow[0:1, dh * 512:dh * 512 + 512],
                            start=True, stop=False), None)
                        for fc in range(8):
                            tm = sch.add(t, lambda e, W_sb=W_sb, fc=fc, tc=tc, dh=dh, n=n: e.matmul(
                                ps_pv[n % 2][:, :],
                                xT[:, fc * 512 + tc * 128:fc * 512 + tc * 128 + 128],
                                W_sb[:, fc * 1024 + dh * 512:fc * 1024 + dh * 512 + 512],
                                start=False, stop=(fc == 7)), sTT if fc == 7 else None)
                        wp[("qkvmm", b, pi, tc, dh)] = (sTT, tm)
                        W(v, ("qkvmm", b, pi, tc, dh))
                        prevu = (b, pi, tc - 2) if tc >= 2 else (
                            (b, pi - 1, tc + 2) if pi > 0 else
                            ((b - 1, 2, tc + 2) if b > 0 else None))
                        if prevu is not None and dh == 0:
                            W(v, ("folddma",) + prevu)
                        ev = sch.add(v, lambda e, tc=tc, dh=dh, n=n: e.tensor_copy(
                            Qtm[:, (tc % 2) * 1024 + dh * 512:(tc % 2) * 1024 + dh * 512 + 512],
                            ps_pv[n % 2][:, :]), sVE)
                        wp[("qkvev", b, n)] = (sVE, ev)
                        wp[("qkvevac", b, pi, tc, dh)] = (sVE, ev)
                    W(s, ("qkvevac", b, pi, tc, 1))
                    if pi == 2:
                        pass
                    sch.add(s, lambda e, dst=dst, tc=tc: e.dma_start(
                        AP(dst, 16 * tc * 8192, [[8192, 16], [1024, 8], [1, 1024]]),
                        AP(Qtm, (tc % 2) * 1024, [[2048, 128], [1, 1024]])), sFoldP[tc % 2], 16)
                    if dup:
                        fd = sch.add(s, lambda e, dst=dst, tc=tc: e.dma_start(
                            AP(dst, (64 + 16 * tc) * 8192, [[8192, 16], [1024, 8], [1, 1024]]),
                            AP(Qtm, (tc % 2) * 1024, [[2048, 128], [1, 1024]])), sFoldP[tc % 2], 16)
                    else:
                        fd = sch.v(sFoldP[tc % 2])
                    wp[("folddma", b, pi, tc)] = (sFoldP[tc % 2], fd)
            mark(("qkv_done", b), sTT)
            wp[("qkvevac_last", b)] = (sVE, sch.v(sVE))
            wp[("folds", b)] = (sFoldP[0], sch.v(sFoldP[0]))
            wp[("folds1", b)] = (sFoldP[1], sch.v(sFoldP[1]))

            for cc in range(64):
                ch = cc % 2
                if cc < 2:
                    W(t, ("folds", b))
                    W(t, ("folds1", b))
                Wtr(t, ("trch", ch, "prev"))
                tm = sch.add(t, lambda e, cc=cc, ch=ch: e.transpose(
                    ps_tr[:, ch * 1024:ch * 1024 + 64],
                    Vs[0:64, cc * 128:cc * 128 + 128], eye[0:64, 0:64]), sTT)
                eng, sem = ((sc, sSC) if ch == 0 else (v, sVE))
                sch.w(eng, sTT, tm)
                ev = sch.add(eng, lambda e, cc=cc, ch=ch: (
                    e.copy(VsT[:, 65 * cc:65 * cc + 64],
                           ps_tr[:, ch * 1024:ch * 1024 + 64])
                    if ch == 0 else
                    e.tensor_copy(VsT[:, 65 * cc:65 * cc + 64],
                                  ps_tr[:, ch * 1024:ch * 1024 + 64])), sem)
                wp[("trch", ch, "prev")] = (sem, ev)
            wp[("vst", b)] = (sVE, sch.v(sVE))
            wp[("vstS", b)] = (sSC, sch.v(sSC))

            PVLAG = 2
            for qt in range(QTB):
                qtg = b * QTB + qt
                for gg in range(NGQ + PVLAG):
                    gi = gidx_of(b, qt, gg)
                    if gg < NGQ:
                        if qt == 0 and gg == 0:
                            sch.w(t, sVE, wp[("vst", b)][1])
                            sch.w(t, sSC, wp[("vstS", b)][1])
                            W(t, ("folds", b))
                            W(t, ("folds1", b))
                            if b > 0:
                                W(t, ("opev", b - 1, 7))
                        if gi >= 2:
                            sch.w(t, sEXP, gi - 1)
                        hh = (gg % 2) * 1024
                        cc = 2 * gg
                        sch.add(t, lambda e, cc=cc, qt=qt, hh=hh: e.matmul(
                            ps_sb[:, hh:hh + 512],
                            Kd[0:64, cc * 128:cc * 128 + 128],
                            Qd[0:64, qt * 512:qt * 512 + 512],
                            start=True, stop=True, tile_position=(0, 0)), None)
                        ts = sch.add(t, lambda e, cc=cc, qt=qt, hh=hh: e.matmul(
                            ps_sb[:, hh + 512:hh + 1024],
                            Kd[64:128, (cc + 1) * 128:(cc + 1) * 128 + 128],
                            Qd[64:128, qt * 512:qt * 512 + 512],
                            start=True, stop=True, tile_position=(64, 0)), sTS)
                        sch.w(sc, sTS, ts)
                        if gi >= 3:
                            sch.w(sc, sPVC, gi - 2)
                        ex = sch.add(sc, lambda e, gg=gg: e.activation(
                            PT[:, (gg % 4) * 1024:(gg % 4) * 1024 + 1024],
                            ps_sb[:, (gg % 2) * 1024:(gg % 2) * 1024 + 1024],
                            AF.Exp, scale=SOFT_SCALE), sEXP)
                        wp[("exp", gi)] = (sEXP, ex)
                    pg = gg - PVLAG
                    if pg >= 0:
                        pgi = gidx_of(b, qt, pg)
                        W(t, ("exp", pgi))
                        if pg == 0 and qtg >= 2:
                            sch.w(t, sVN, qtg - 1)
                        if pg == 0 and qt < 2:
                            W(t, ("qkvev", b, 23))
                        sch.add(t, lambda e, pg=pg, qtg=qtg: e.matmul(
                            ps_pv[qtg % 2][0:65, :],
                            VsT[:, (2 * pg) * 65:(2 * pg) * 65 + 65],
                            PT[:, (pg % 4) * 1024:(pg % 4) * 1024 + 512],
                            start=(pg == 0), stop=False), None)
                        sch.add(t, lambda e, pg=pg, qtg=qtg: e.matmul(
                            ps_pv[qtg % 2][0:65, :],
                            VsT[:, (2 * pg + 1) * 65:(2 * pg + 1) * 65 + 65],
                            PT[:, (pg % 4) * 1024 + 512:(pg % 4) * 1024 + 1024],
                            start=False, stop=(pg == NGQ - 1)), sPVC)
                sch.w(v, sPVC, NGQ * (qtg + 1))
                if qtg >= 1:
                    W(v, ("recwr", qtg - 1))
                rc = sch.add(v, lambda e, qtg=qtg: e.reciprocal(
                    rec_sb[0:1, :], ps_pv[qtg % 2][64:65, :]), sREC)
                sch.w(g, sREC, rc)
                wr = sch.add(g, lambda e: e.dma_start(
                    AP(recscrh, 0, [[512, 1], [1, 512]]), rec_sb[0:1, :]), sG, 16)
                wp[("recwr", qtg)] = (sG, wr)
                sch.w(g, sG, wr)
                if qtg >= 1:
                    sch.w(g, sVN, qtg)
                rd = sch.add(g, lambda e, qtg=qtg: e.dma_start(
                    rec_bc[0:64, 0:512],
                    AP(recscrh, 0, [[0, 64], [1, 512]])), sG, 16)
                sch.w(v, sG, rd)
                if qt == 0 and b > 0:
                    W(v, ("outproj", b - 1))
                vn = sch.add(v, lambda e, qtg=qtg, qt=qt: e.scalar_tensor_tensor(
                    A2[0:64, qt * 512:qt * 512 + 512], ps_pv[qtg % 2][0:64, :], 1.0,
                    rec_bc[0:64, 0:512],
                    ALU.mult, ALU.mult), sVN)
                sch.w(g, sVN, vn)
                ad = sch.add(g, lambda e, qt=qt: e.dma_start(
                    A2[64:128, qt * 512:qt * 512 + 512],
                    A2[0:64, qt * 512:qt * 512 + 512]), sG, 16)
                wp[("a2dup", qtg)] = (sG, ad)
                emit_qstep()
                emit_qstep()

            for tc in range(4):
                if b > 0 or tc >= 2:
                    bb, tt = (b, tc - 2) if tc >= 2 else (b - 1, tc + 2)
                    W(s, ("s1evac", bb, tt))
                ld = sch.add(s, lambda e, b=b, tc=tc: e.dma_start(
                    x_tc[:, (tc % 2) * 1024:(tc % 2) * 1024 + 1024],
                    AP(xh, (b * 512 + tc * 128) * 1024, [[1024, 128], [1, 1024]])),
                    sXL[tc % 2], 16)
                wp[("xload", b, tc)] = (sXL[tc % 2], ld)
                for dh in range(2):
                    n = tc * 2 + dh
                    u = (n % 2) * 2
                    if n == 0:
                        if b == 0:
                            W(t, ("w_wo",))
                            W(t, ("w_bo",))
                        sch.w(t, sVN, b * QTB + QTB)
                        sch.w(t, sG, wp[("a2dup", b * QTB + QTB - 1)][1])
                    if n >= 2:
                        W(t, ("opev", b, n - 2))
                    sch.add(t, lambda e, dh=dh, u=u: e.matmul(
                        ps_sb[:, u * 512:u * 512 + 512], ones[0:1, :],
                        bo_row[0:1, dh * 512:dh * 512 + 512],
                        start=True, stop=False, tile_position=(0, 0)), None)
                    for m in range(8):
                        for par in range(2):
                            r = 2 * m + par
                            lo = par * 64 * 8192
                            tm = sch.add(t, lambda e, r=r, m=m, tc=tc, dh=dh, lo=lo, par=par, u=u: e.matmul(
                                ps_sb[:, (u + par) * 512:(u + par) * 512 + 512],
                                AP(A2, lo + 2048 * tc + r, [[8192, 64], [16, 128]]),
                                AP(Wo_sb, lo + m * 1024 + dh * 512, [[8192, 64], [1, 512]]),
                                start=(par == 1 and m == 0), stop=(m == 7),
                                tile_position=(64 * par, 0)), sTT if r == 15 else None)
                    wp[("opmm", b, tc, dh)] = (sTT, tm)
                    W(v, ("opmm", b, tc, dh))
                    sch.w(v, wp[("xload", b, tc)][0], wp[("xload", b, tc)][1])
                    sch.add_sync(v, lambda e, tc=tc, dh=dh, u=u: e.scalar_tensor_tensor(
                        s1[:, (tc % 2) * 1024 + dh * 512:(tc % 2) * 1024 + dh * 512 + 512],
                        ps_sb[:, u * 512:u * 512 + 512], 1.0,
                        x_tc[:, (tc % 2) * 1024 + dh * 512:(tc % 2) * 1024 + dh * 512 + 512],
                        ALU.mult, ALU.add), sVE)
                    ev = sch.add_sync(v, lambda e, tc=tc, dh=dh, u=u: e.scalar_tensor_tensor(
                        s1[:, (tc % 2) * 1024 + dh * 512:(tc % 2) * 1024 + dh * 512 + 512],
                        ps_sb[:, (u + 1) * 512:(u + 1) * 512 + 512], 1.0,
                        s1[:, (tc % 2) * 1024 + dh * 512:(tc % 2) * 1024 + dh * 512 + 512],
                        ALU.mult, ALU.add), sVE)
                    wp[("opev", b, n)] = (sVE, ev)
                    wp[("s1evac", b, tc)] = (sVE, ev)
                if tc == 3:
                    mark(("outproj", b), sTT)
                sl = (tc % 2) * 1024
                if b > 0 or tc >= 2:
                    bb, tt = (b, tc - 2) if tc >= 2 else (b - 1, tc + 2)
                    W(v, ("x1w", bb, tt))
                sch.add_sync(v, lambda e, sl=sl: e.reduce_sum(
                    negsum[:, :], s1[:, sl:sl + 1024], AX.X, negate=True), sVE)
                sch.add_sync(v, lambda e: e.tensor_scalar_mul(
                    negmean[:, :], negsum[:, :], 1.0 / 1024.0), sVE)
                sch.add_sync(v, lambda e, sl=sl: e.tensor_scalar(
                    s1[:, sl:sl + 1024], s1[:, sl:sl + 1024],
                    negmean[:, :], None, ALU.add), sVE)
                sch.add_sync(v, lambda e, sl=sl: e.scalar_tensor_tensor(
                    x1[:, sl:sl + 1024], s1[:, sl:sl + 1024], 1.0,
                    s1[:, sl:sl + 1024], ALU.mult, ALU.mult,
                    accum_out=varsum[:, :]), sVE)
                lv = sch.add(v, lambda e: e.tensor_scalar(
                    vareps[:, :], varsum[:, :], 1.0 / 1024.0, LN_EPS,
                    ALU.mult, ALU.add), sVE)
                sch.w(sc, sVE, lv)
                sch.add_sync(sc, lambda e: e.activation(uln[:, :], vareps[:, :], AF.Ln), sSC)
                sr = sch.add(sc, lambda e: e.activation(
                    rstd[:, :], uln[:, :], AF.Exp, scale=-0.5), sSC)
                sch.w(v, sSC, sr)
                x1v = sch.add(v, lambda e, sl=sl: e.tensor_scalar(
                    x1[:, sl:sl + 1024], s1[:, sl:sl + 1024],
                    rstd[:, :], None, ALU.mult), sVE)
                sch.w(s, sVE, x1v)
                xw = sch.add(s, lambda e, b=b, tc=tc, sl=sl: e.dma_start(
                    AP(x1scrh, (b * 512 + tc * 128) * 1024, [[1024, 128], [1, 1024]]),
                    x1[:, sl:sl + 1024]), sX1W[tc % 2], 16)
                wp[("x1w", b, tc)] = (sX1W[tc % 2], xw)
                emit_qstep()
                emit_qstep()
            sch.w(g, sX1W[0], wp[("x1w", b, 2)][1])
            sch.w(g, sX1W[1], wp[("x1w", b, 3)][1])
            sch.add(g, lambda e, b=b: e.dma_start(
                AP(x1bfh, b * 524288, [[1024, 512], [1, 1024]]),
                AP(x1scrh, b * 524288, [[1024, 512], [1, 1024]])), sCB[b], 16)

        while qstate["i"] < len(qsteps):
            emit_qstep()

        ATT_T = sch.v(sTT); ATT_V = sch.v(sVE); ATT_G = sch.v(sG); ATT_S = sch.v(sS)

        sch.w(s, sPVC, NB * QTB * NGQ)
        sch.w(s, sQS, sch.v(sQS))
        sch.w(s, sG, wp[("a2dup", NB * QTB - 1)][1])
        for dc in range(8):
            sch.add(s, lambda e, dc=dc: e.dma_start_transpose(
                WqT1[:, dc * 4096:dc * 4096 + 4096],
                AP(wqt1h, dc * 128, [[1024, 4096], [1, 128]])), sWQ1, 16)
        wp[("wqt1",)] = (sWQ1, sch.v(sWQ1))
        W(s, ("outproj", NB - 1))
        sch.w(s, sQS2, sch.v(sQS2))
        sch.w(s, sVE, ATT_V)
        for oc in range(32):
            sch.add(s, lambda e, oc=oc: e.dma_start_transpose(
                WqT2[:, oc * 1024:oc * 1024 + 1024],
                AP(wq2h, oc * 128, [[4096, 1024], [1, 128]])), sWQT, 16)
        wp[("wqt",)] = (sWQT, sch.v(sWQT))

        if qstate["last_qv"]:
            W(g, qstate["last_qv"])
        sch.add(g, lambda e: e.dma_start(
            AP(b1_rows, 0, [[4096, 1], [1, 4096]]),
            AP(b1h, 0, [[4096, 1], [1, 4096]])), sXB1, 16)

        for b in range(NB):
            sch.w(s, sCB[b], 16)
            if b == 0:
                W(s, ("outproj", NB - 1))
            if b > 0:
                W(s, ("w1done", b - 1))
            for dc in range(8):
                sch.add(s, lambda e, b=b, dc=dc: e.dma_start_transpose(
                    x1T[:, dc * 512:dc * 512 + 512],
                    AP(x1bfh, b * 524288 + dc * 128, [[1024, 512], [1, 128]])), sX1T, 16)
            wp[("x1T", b)] = (sX1T, sch.v(sX1T))

            for oc in range(32):
                W(t, ("x1T", b))
                W(t, ("wqt1",))
                if b == 0 and oc == 0:
                    sch.w(t, sXB1, 16)
                if oc >= 2:
                    W(t, ("hevac", b, oc - 2))
                elif b > 0:
                    W(t, ("hevac", b - 1, oc + 30))
                sch.add(t, lambda e, oc=oc: e.matmul(
                    ps_pv[oc % 2][:, :], b1_rows[0:1, oc * 128:oc * 128 + 128],
                    ones512[0:1, :], start=True, stop=False), None)
                for dc in range(8):
                    tm = sch.add(t, lambda e, oc=oc, dc=dc: e.matmul(
                        ps_pv[oc % 2][:, :],
                        WqT1[:, dc * 4096 + oc * 128:dc * 4096 + oc * 128 + 128],
                        x1T[:, dc * 512:dc * 512 + 512],
                        start=False, stop=(dc == 7)), sTT if dc == 7 else None)
                wp[("w1mm", b, oc)] = (sTT, tm)
                W(v, ("w1mm", b, oc))
                if b > 0 and oc < 2:
                    W(v, ("w2done", b - 1))
                ev = sch.add(v, lambda e, oc=oc: e.tensor_scalar_max(
                    hT[:, oc * 512:oc * 512 + 512], ps_pv[oc % 2][:, :], 0.0), sVE)
                wp[("hevac", b, oc)] = (sVE, ev)
            mark(("w1done", b), sTT)

            for tsub in range(4):
                if b > 0 or tsub >= 2:
                    bb, tt = (b, tsub - 2) if tsub >= 2 else (b - 1, tsub + 2)
                    W(s, ("s2done", bb, tt))
                else:
                    W(s, ("s1evac", NB - 1, 3))
                ld = sch.add(s, lambda e, b=b, tsub=tsub: e.dma_start(
                    x1f[:, (tsub % 2) * 1024:(tsub % 2) * 1024 + 1024],
                    AP(x1scrh, (b * 512 + tsub * 128) * 1024, [[1024, 128], [1, 1024]])),
                    sX1F[tsub % 2], 16)
                wp[("x1fload", b, tsub)] = (sX1F[tsub % 2], ld)
                for dh in range(2):
                    W(t, ("hevac", b, 31))
                    if b == 0 and tsub == 0 and dh == 0:
                        W(t, ("wqt",))
                        W(t, ("wqt1",))
                    if b * 4 + tsub >= 2:
                        pb, pt = divmod(b * 4 + tsub - 2, 4)
                        W(t, ("s2evac", pb, pt))
                    sch.add(t, lambda e, tsub=tsub, dh=dh: e.matmul(
                        ps_sb[:, (tsub % 2) * 1024 + dh * 512:(tsub % 2) * 1024 + dh * 512 + 512],
                        ones[0:1, :], b2_row[0:1, dh * 512:dh * 512 + 512],
                        start=True, stop=False), None)
                    for oc in range(32):
                        tm = sch.add(t, lambda e, tsub=tsub, dh=dh, oc=oc: e.matmul(
                            ps_sb[:, (tsub % 2) * 1024 + dh * 512:(tsub % 2) * 1024 + dh * 512 + 512],
                            hT[:, oc * 512 + tsub * 128:oc * 512 + tsub * 128 + 128],
                            WqT2[:, oc * 1024 + dh * 512:oc * 1024 + dh * 512 + 512],
                            start=False, stop=(oc == 31)),
                            sTT if (dh == 1 and oc == 31) else None)
                wp[("w2mm", b, tsub)] = (sTT, tm)
                W(v, ("w2mm", b, tsub))
                sch.w(v, sX1F[tsub % 2], ld)
                if b * 4 + tsub >= 1:
                    pb, pt = divmod(b * 4 + tsub - 1, 4)
                    W(v, ("outdma", pb, pt))
                sl = 0
                xsl = (tsub % 2) * 1024
                ev = sch.add_sync(v, lambda e, tsub=tsub, xsl=xsl: e.scalar_tensor_tensor(
                    s2[:, 0:1024], ps_sb[:, (tsub % 2) * 1024:(tsub % 2) * 1024 + 1024], 1.0,
                    x1f[:, xsl:xsl + 1024], ALU.mult, ALU.add), sVE)
                wp[("s2evac", b, tsub)] = (sVE, ev)
                wp[("s2done", b, tsub)] = (sVE, ev)
                sch.add_sync(v, lambda e: e.reduce_sum(
                    negsum[:, :], s2[:, 0:1024], AX.X, negate=True), sVE)
                sch.add_sync(v, lambda e: e.tensor_scalar_mul(
                    negmean[:, :], negsum[:, :], 1.0 / 1024.0), sVE)
                sch.add_sync(v, lambda e: e.tensor_scalar(
                    s2[:, 0:1024], s2[:, 0:1024],
                    negmean[:, :], None, ALU.add), sVE)
                sch.add_sync(v, lambda e: e.scalar_tensor_tensor(
                    outb[:, 0:1024], s2[:, 0:1024], 1.0,
                    s2[:, 0:1024], ALU.mult, ALU.mult,
                    accum_out=varsum[:, :]), sVE)
                lv = sch.add(v, lambda e: e.tensor_scalar(
                    vareps[:, :], varsum[:, :], 1.0 / 1024.0, LN_EPS,
                    ALU.mult, ALU.add), sVE)
                sch.w(sc, sVE, lv)
                sch.add_sync(sc, lambda e: e.activation(uln[:, :], vareps[:, :], AF.Ln), sSC)
                sr = sch.add(sc, lambda e: e.activation(
                    rstd[:, :], uln[:, :], AF.Exp, scale=-0.5), sSC)
                sch.w(v, sSC, sr)
                ov = sch.add(v, lambda e: e.tensor_scalar(
                    outb[:, 0:1024], s2[:, 0:1024],
                    rstd[:, :], None, ALU.mult), sVE)
                wp[("outw", b, tsub)] = (sVE, ov)
                sch.w(s, sVE, ov)
                od = sch.add(s, lambda e, b=b, tsub=tsub: e.dma_start(
                    AP(yh, (b * 512 + tsub * 128) * 1024, [[1024, 128], [1, 1024]]),
                    outb[:, 0:1024]), sOUT, 16)
                wp[("outdma", b, tsub)] = (sOUT, od)
            mark(("w2done", b), sTT)

        sch.w(s, sOUT, sch.v(sOUT))
        sch.w(s, sX1T, sch.v(sX1T))
        for i in range(2):
            sch.w(s, sX1F[i], sch.v(sX1F[i]))
        sch.w(g, sG, sch.v(sG))
        sch.w(g, sXB1, sch.v(sXB1))
        for i in range(4):
            sch.w(g, sCB[i], 16)

        @blk.sync
        def _(e):
            for f in sch.L["sync"]:
                f(e)

        @blk.gpsimd
        def _(e):
            for f in sch.L["gpsimd"]:
                f(e)

        @blk.tensor
        def _(e):
            for f in sch.L["tensor"]:
                f(e)

        @blk.vector
        def _(e):
            for f in sch.L["vector"]:
                f(e)

        @blk.scalar
        def _(e):
            for f in sch.L["scalar"]:
                f(e)

    return nc


def kernel(L0_attn_x, L0_attn_Wq, L0_attn_bq, L0_attn_Wk, L0_attn_bk,
           L0_attn_Wv, L0_attn_bv, L0_attn_Wo, L0_attn_bo,
           L0_ffn_W1, L0_ffn_b1, L0_ffn_W2, L0_ffn_b2):
    x = np.ascontiguousarray(np.asarray(L0_attn_x, dtype=np.float32)).reshape(16384, 1024)
    com = dict(
        Wq=np.ascontiguousarray(np.asarray(L0_attn_Wq, np.float32)),
        bq=np.ascontiguousarray(np.asarray(L0_attn_bq, np.float32)),
        Wk=np.ascontiguousarray(np.asarray(L0_attn_Wk, np.float32)),
        bk=np.ascontiguousarray(np.asarray(L0_attn_bk, np.float32)),
        Wv=np.ascontiguousarray(np.asarray(L0_attn_Wv, np.float32)),
        bv=np.ascontiguousarray(np.asarray(L0_attn_bv, np.float32)),
        Wo=np.ascontiguousarray(np.asarray(L0_attn_Wo, np.float32)),
        bo=np.ascontiguousarray(np.asarray(L0_attn_bo, np.float32)),
        W1=np.ascontiguousarray(np.asarray(L0_ffn_W1, np.float32)),
        b1=np.ascontiguousarray(np.asarray(L0_ffn_b1, np.float32)),
        W2=np.ascontiguousarray(np.asarray(L0_ffn_W2, np.float32)),
        b2=np.ascontiguousarray(np.asarray(L0_ffn_b2, np.float32)),
        eye=np.eye(128, dtype=np.float32),
    )
    nc = build_nc()
    in_maps = [dict(com, x=np.ascontiguousarray(x[c * 2048:(c + 1) * 2048]))
               for c in range(8)]
    res = run_bass_kernel_spmd(nc, in_maps, core_ids=list(range(8)))
    out = np.concatenate([np.asarray(r["y"], np.float32) for r in res.results], axis=0)
    return out.reshape(2, 8192, 1024)


# version 11
# speedup vs baseline: 1.2135x; 1.0488x over previous
import numpy as np
import concourse.bass as bass
import concourse.mybir as mybir
from concourse.bass_utils import run_bass_kernel_spmd

F32 = mybir.dt.float32
BF16 = mybir.dt.bfloat16
AF = mybir.ActivationFunctionType
ALU = mybir.AluOpType
AX = mybir.AxisListType
AP = bass.AP

NB = 4
QTB = 16
NGQ = 32
SOFT_SCALE = float(2.0 ** -10)
LN_EPS = 1e-5
QN = 4096 * 1024


class Sch:

    def __init__(self):
        self.L = {k: [] for k in ("sync", "gpsimd", "tensor", "vector", "scalar")}
        self.c = {}

    def w(self, eng, sem, v):
        if v > 0:
            self.L[eng].append(lambda e, s=sem, vv=v: e.wait_ge(s, vv))

    def add(self, eng, f, inc=None, n=1):
        if inc is None:
            self.L[eng].append(f)
            return None
        tot = self.c[id(inc)] = self.c.get(id(inc), 0) + n
        self.L[eng].append(lambda e, ff=f, s=inc, nn=n: ff(e).then_inc(s, nn))
        return tot

    def v(self, sem):
        return self.c.get(id(sem), 0)

    def add_sync(self, eng, f, inc):
        tot = self.add(eng, f, inc)
        self.w(eng, inc, tot)
        return tot


def build_nc():
    nc = bass.Bass("TRN2", target_bir_lowering=False)

    xh = nc.dram_tensor("x", [2048, 1024], F32, kind="ExternalInput")
    wqh = nc.dram_tensor("Wq", [1024, 1024], F32, kind="ExternalInput")
    bqh = nc.dram_tensor("bq", [1024], F32, kind="ExternalInput")
    wkh = nc.dram_tensor("Wk", [1024, 1024], F32, kind="ExternalInput")
    bkh = nc.dram_tensor("bk", [1024], F32, kind="ExternalInput")
    wvh = nc.dram_tensor("Wv", [1024, 1024], F32, kind="ExternalInput")
    bvh = nc.dram_tensor("bv", [1024], F32, kind="ExternalInput")
    woh = nc.dram_tensor("Wo", [1024, 1024], F32, kind="ExternalInput")
    boh = nc.dram_tensor("bo", [1024], F32, kind="ExternalInput")
    w1h = nc.dram_tensor("W1", [4096, 1024], F32, kind="ExternalInput")
    b1h = nc.dram_tensor("b1", [4096], F32, kind="ExternalInput")
    w2h = nc.dram_tensor("W2", [1024, 4096], F32, kind="ExternalInput")
    b2h = nc.dram_tensor("b2", [1024], F32, kind="ExternalInput")
    eyeh = nc.dram_tensor("eye", [128, 128], F32, kind="ExternalInput")
    yh = nc.dram_tensor("y", [2048, 1024], F32, kind="ExternalOutput")

    wqt1h = nc.dram_tensor("wqt1", [4096, 1024], BF16, kind="Internal")
    wq2h = nc.dram_tensor("wq2", [1024, 4096], BF16, kind="Internal")
    x1scrh = nc.dram_tensor("x1scr", [2048, 1024], F32, kind="Internal")
    x1bfh = nc.dram_tensor("x1bfscr", [2048, 1024], BF16, kind="Internal")
    recscrh = nc.dram_tensor("recscr", [512], F32, kind="Internal")
    gscrh = nc.dram_tensor("gscr", [256], F32, kind="Internal")
    thrscrh = nc.dram_tensor("thrscr", [8], F32, kind="Internal")

    BASE = 16512
    off = [BASE]

    def alloc(name, shape, dtype, at=None):
        if at is None:
            at = off[0]
            nbytes = shape[-1] * (2 if dtype == BF16 else 4)
            for d in shape[1:-1]:
                nbytes *= d
            off[0] = at + ((nbytes + 31) // 32) * 32
        assert off[0] <= 229344, f"sbuf overflow at {name}: {off[0]}"
        return nc.alloc_sbuf_tensor_at(name, shape, dtype, offset=at)

    eye = alloc("eye", [128, 128], BF16)
    ones = alloc("ones", [1, 128], BF16)
    ones512 = alloc("ones512", [1, 512], BF16)
    bq_row = alloc("bq_row", [1, 1024], BF16)
    bk_row = alloc("bk_row", [1, 1024], BF16)
    bv_row = alloc("bv_row", [1, 1024], BF16)
    bo_row = alloc("bo_row", [1, 1024], BF16)
    b2_row = alloc("b2_row", [1, 1024], BF16)
    thrbc = alloc("thrbc", [128, 8], F32)
    gv = alloc("gv", [1, 16], F32)
    acc1 = alloc("acc1", [128, 1], F32)
    acc2 = alloc("acc2", [128, 1], F32)
    gtmp = alloc("gtmp", [128, 1], F32)
    negsum = alloc("negsum", [128, 1], F32)
    negmean = alloc("negmean", [128, 1], F32)
    varsum = alloc("varsum", [128, 1], F32)
    vareps = alloc("vareps", [128, 1], F32)
    uln = alloc("uln", [128, 1], F32)
    rstd = alloc("rstd", [128, 1], F32)
    warm = alloc("warm", [1, 256], BF16)
    b1c = alloc("b1c", [128, 32], F32)

    A0 = off[0]

    Qd = alloc("Qd", [128, 8192], BF16)
    Kd = alloc("Kd", [128, 8192], BF16)
    Vs = alloc("Vs", [64, 8192], BF16)
    VsT = alloc("VsT", [128, 4160], BF16)
    PT = alloc("PT", [128, 4096], BF16)
    Wq_sb = alloc("Wq_sb", [128, 8192], BF16)
    Wk_sb = alloc("Wk_sb", [128, 8192], BF16)
    Wv_sb = alloc("Wv_sb", [128, 8192], BF16)
    Wo_sb = alloc("Wo_sb", [128, 8192], BF16)
    A2 = alloc("A2", [128, 8192], BF16)
    xT = alloc("xT", [128, 4096], BF16)
    x_tc = alloc("x_tc", [128, 2048], F32)
    x_bf = alloc("x_bf", [128, 1024], BF16)
    Qtm = alloc("Qtm", [128, 2048], BF16)
    rec_sb = alloc("rec_sb", [1, 512], F32)
    rec_bc = alloc("rec_bc", [64, 512], F32)
    s1 = alloc("s1", [128, 2048], F32)
    x1 = alloc("x1", [128, 2048], F32)
    ATTN_END = off[0]

    off[0] = A0
    WqT1 = alloc("WqT1", [128, 32768], BF16)
    WqT2 = alloc("WqT2", [128, 32768], BF16)
    hT = alloc("hT", [128, 16384], BF16)
    x1T = alloc("x1T", [128, 4096], BF16)
    x1f = alloc("x1f", [128, 2048], F32)
    s2 = alloc("s2", [128, 1024], F32)
    outb = alloc("outb", [128, 1024], F32)
    FFN_END = off[0]
    off[0] = ATTN_END
    grow = alloc("grow", [1, 256], F32)
    Wch = alloc("Wch", [128, 1024], F32)
    tq1 = alloc("tq1", [128, 512], BF16)
    tq2 = alloc("tq2", [128, 512], BF16)
    qout = alloc("qout", [128, 512], BF16)
    QNT_END = off[0]
    assert QNT_END <= 229344 and FFN_END <= 229344, (ATTN_END, FFN_END, QNT_END)
    off[0] = max(ATTN_END, FFN_END, QNT_END)

    sch = Sch()

    from contextlib import ExitStack
    with ExitStack() as es:
        E = es.enter_context
        ps_sb = E(nc.psum_tensor("ps_sb", [128, 2048], F32))
        ps_pv = [E(nc.psum_tensor(f"ps_pv{i}", [128, 512], F32)) for i in range(2)]
        ps_tr = E(nc.psum_tensor("ps_tr", [128, 2048], BF16))

        sS = E(nc.semaphore("sS"))
        sG = E(nc.semaphore("sG"))
        sQS = E(nc.semaphore("sQS"))
        sQS2 = E(nc.semaphore("sQS2"))
        sFoldP = [E(nc.semaphore(f"sFold{i}")) for i in range(2)]
        sFIN = E(nc.semaphore("sFIN"))
        sXL = [E(nc.semaphore(f"sXL{i}")) for i in range(2)]
        sX1W = [E(nc.semaphore(f"sX1W{i}")) for i in range(2)]
        sXBF = [E(nc.semaphore(f"sXBF{i}")) for i in range(2)]
        sWQT = E(nc.semaphore("sWQT"))
        sX1T = E(nc.semaphore("sX1T"))
        sX1F = [E(nc.semaphore(f"sX1F{i}")) for i in range(2)]
        sOUT = E(nc.semaphore("sOUT"))
        sXB1 = E(nc.semaphore("sXB1"))
        sCB = [E(nc.semaphore(f"sCB{i}")) for i in range(4)]
        sXW3 = E(nc.semaphore("sXW3"))
        sTS = E(nc.semaphore("sTS"))
        sEXP = E(nc.semaphore("sEXP"))
        sPVC = E(nc.semaphore("sPVC"))
        sREC = E(nc.semaphore("sREC"))
        sVN = E(nc.semaphore("sVN"))
        sVE = E(nc.semaphore("sVE"))
        sTT = E(nc.semaphore("sTT"))
        sSC = E(nc.semaphore("sSC"))
        sWG = [E(nc.semaphore(f"sWG{i}")) for i in range(5)]
        sWQ1 = E(nc.semaphore("sWQ1"))
        blk = E(nc.Block())

        wp = {}

        def W(eng, key):
            if key in wp:
                sem, v = wp[key]
                sch.w(eng, sem, v)

        def mark(key, sem):
            wp[key] = (sem, sch.v(sem))

        g = "gpsimd"; s = "sync"; t = "tensor"; v = "vector"; sc = "scalar"

        sch.add(g, lambda e: e.dma_start(eye[:, :], eyeh.ap()[:, :]), sWG[0], 16)
        wp[("w_eye",)] = (sWG[0], 16)
        for i, (nm, W_sb, wh, row, bh) in enumerate((
                ("q", Wq_sb, wqh, bq_row, bqh),
                ("k", Wk_sb, wkh, bk_row, bkh),
                ("v", Wv_sb, wvh, bv_row, bvh))):
            sch.add(g, lambda e, d=W_sb, sr=wh: e.dma_start(
                AP(d, 0, [[8192, 128], [1024, 8], [1, 1024]]),
                AP(sr, 0, [[1024, 128], [131072, 8], [1, 1024]])), sWG[1 + i], 16)
            sch.add(g, lambda e, d=row, sr=bh: e.dma_start(
                AP(d, 0, [[1024, 1], [1, 1024]]),
                AP(sr, 0, [[1024, 1], [1, 1024]])), sWG[1 + i], 16)
            wp[("w_w" + nm,)] = wp[("w_b" + nm,)] = (sWG[1 + i], 32)
        sch.add(g, lambda e: e.dma_start(
            AP(Wo_sb, 0, [[8192, 64], [1024, 8], [1, 1024]]),
            AP(woh, 0, [[1024, 64], [131072, 8], [1, 1024]])), sWG[4], 16)
        sch.add(g, lambda e: e.dma_start(
            AP(Wo_sb, 64 * 8192, [[8192, 64], [1024, 8], [1, 1024]]),
            AP(woh, 65536, [[1024, 64], [131072, 8], [1, 1024]])), sWG[4], 16)
        for row, bh in ((bo_row, boh), (b2_row, b2h)):
            sch.add(g, lambda e, d=row, sr=bh: e.dma_start(
                AP(d, 0, [[1024, 1], [1, 1024]]),
                AP(sr, 0, [[1024, 1], [1, 1024]])), sWG[4], 16)
        for oc in range(32):
            sch.add(g, lambda e, oc=oc: e.dma_start(
                b1c[:, oc:oc + 1],
                AP(b1h, oc * 128, [[1, 128], [1, 1]])), sWG[4], 16)
        wp[("w_wo",)] = wp[("w_bo",)] = wp[("w_b2",)] = wp[("w_b1c",)] = (sWG[4], 36 * 16)
        mark("prologue_g", sG)

        sch.add(v, lambda e: e.memset(ones[0:1, :], 1.0), sVE)
        sch.add(v, lambda e: e.memset(ones512[0:1, :], 1.0), sVE)
        sch.add(v, lambda e: e.memset(VsT[:, :], 1.0), sVE)
        sch.add(v, lambda e: e.memset(acc1[:, :], 0.0), sVE)
        sch.add(v, lambda e: e.memset(acc2[:, :], 0.0), sVE)
        mark("prologue_v", sVE)

        W(sc, "prologue_v")
        sch.add(sc, lambda e: e.activation(warm[0:1, 0:128], ones[0:1, :], AF.Exp), sSC)
        sch.add(sc, lambda e: e.activation(warm[0:1, 128:256], ones[0:1, :], AF.Ln), sSC)

        qsteps = []

        def gamma_step(widx, ch):
            def fs(e, widx=widx, ch=ch):
                if widx == 0:
                    return e.dma_start(Wch[:, :], AP(w1h, ch * 131072,
                                                    [[1024, 128], [1, 1024]]))
                a, cg = ch % 8, ch // 8
                return e.dma_start(Wch[:, :], AP(w2h, a * 524288 + cg * 1024,
                                                 [[4096, 128], [1, 1024]]))
            def fv(ops, widx=widx, ch=ch):
                acc = acc1 if widx == 0 else acc2
                ops.append((v, "sync", lambda e: e.reduce_sum(gtmp[:, :], Wch[:, :], AX.X,
                                                      apply_absolute_value=True), sVE))
                ops.append((v, "plain", lambda e, a=acc: e.tensor_add(a[:, :], a[:, :], gtmp[:, :]), sVE))
            return ("gamma", fs, fv)

        def finalize_step():
            return ("finalize", None, None)

        def quant_step(widx, ch):
            def fs(e, widx=widx, ch=ch):
                if widx == 0:
                    return e.dma_start(Wch[:, :], AP(w1h, ch * 131072,
                                                    [[1024, 128], [1, 1024]]))
                a, cg = ch % 8, ch // 8
                return e.dma_start(Wch[:, :], AP(w2h, a * 524288 + cg * 1024,
                                                 [[4096, 128], [1, 1024]]))
            def fo(e, hh, widx=widx, ch=ch):
                if widx == 0:
                    return e.dma_start(AP(wqt1h, ch * 131072 + hh * 512,
                                          [[1024, 128], [1, 512]]), qout[:, :])
                a, cg = ch % 8, ch // 8
                return e.dma_start(AP(wq2h, a * 524288 + cg * 1024 + hh * 512,
                                      [[4096, 128], [1, 512]]), qout[:, :])
            return ("quant", fs, fo, widx)

        for widx in (0, 1):
            for ch in range(32):
                qsteps.append(gamma_step(widx, ch))
        qsteps.append(finalize_step())
        for widx in (0, 1):
            for ch in range(32):
                qsteps.append(quant_step(widx, ch))
        qstate = {"i": 0, "thr_ready": None, "last_q_dma": None, "last_qv": None}

        def emit_qstep():
            i = qstate["i"]
            if i >= len(qsteps):
                return
            qstate["i"] = i + 1
            st = qsteps[i]
            if st[0] == "gamma":
                if qstate["last_qv"]:
                    W(s, qstate["last_qv"])
                ld = sch.add(s, st[1], sS, 16)
                sch.w(v, sS, ld)
                ops = []
                st[2](ops)
                for eng, kind, f, sem in ops:
                    (sch.add_sync if kind == "sync" else sch.add)(eng, f, sem)
                mark(f"qv_{i}", sVE)
                qstate["last_qv"] = f"qv_{i}" 
            elif st[0] == "finalize":
                sch.w(g, sVE, sch.v(sVE))
                sch.add(g, lambda e: e.dma_start(AP(gscrh, 0, [[1, 128], [1, 1]]),
                                                 acc1[:, :]), sFIN, 16)
                sch.add(g, lambda e: e.dma_start(AP(gscrh, 128, [[1, 128], [1, 1]]),
                                                 acc2[:, :]), sFIN, 16)
                sch.w(g, sFIN, sch.v(sFIN))
                gld = sch.add(g, lambda e: e.dma_start(
                    grow[0:1, :], AP(gscrh, 0, [[256, 1], [1, 256]])), sFIN, 16)
                sch.w(v, sFIN, gld)
                sch.add(v, lambda e: e.reduce_sum(gv[0:1, 0:1], grow[0:1, 0:128], AX.X), sVE)
                sch.add_sync(v, lambda e: e.reduce_sum(gv[0:1, 1:2], grow[0:1, 128:256], AX.X), sVE)
                c1, c2 = 0.5 / QN, 0.5 * 1e-5
                sch.add(v, lambda e: e.tensor_scalar(gv[0:1, 4:5], gv[0:1, 0:1], c1, c2, ALU.mult, ALU.add), sVE)
                sch.add(v, lambda e: e.tensor_scalar(gv[0:1, 5:6], gv[0:1, 0:1], -c1, -c2, ALU.mult, ALU.add), sVE)
                sch.add(v, lambda e: e.tensor_scalar(gv[0:1, 6:7], gv[0:1, 1:2], c1, c2, ALU.mult, ALU.add), sVE)
                thr_v = sch.add(v, lambda e: e.tensor_scalar(gv[0:1, 7:8], gv[0:1, 1:2], -c1, -c2, ALU.mult, ALU.add), sVE)
                sch.w(g, sVE, thr_v)
                sch.add(g, lambda e: e.dma_start(AP(thrscrh, 0, [[8, 1], [1, 4]]),
                                                 gv[0:1, 4:8]), sFIN, 16)
                sch.w(g, sFIN, sch.v(sFIN))
                thr_g = sch.add(g, lambda e: e.dma_start(
                    thrbc[:, 0:4], AP(thrscrh, 0, [[0, 128], [1, 4]])), sFIN, 16)
                qstate["thr_ready"] = thr_g
            else:
                widx = st[3]
                if qstate["last_qv"]:
                    W(s, qstate["last_qv"])
                ld = sch.add(s, st[1], sS, 16)
                sch.w(v, sS, ld)
                if qstate["thr_ready"] is not None:
                    sch.w(v, sFIN, qstate["thr_ready"])
                    qstate["thr_ready"] = None
                cp, cn = 2 * widx, 2 * widx + 1
                for hh in range(2):
                    if qstate["last_q_dma"] is not None:
                        sch.w(v, *qstate["last_q_dma"])
                    sch.add(v, lambda e, cp=cp, hh=hh: e.tensor_scalar(
                        tq1[:, :], Wch[:, hh * 512:hh * 512 + 512],
                        thrbc[:, cp:cp + 1], None, ALU.is_gt), sVE)
                    sch.add_sync(v, lambda e, cn=cn, hh=hh: e.tensor_scalar(
                        tq2[:, :], Wch[:, hh * 512:hh * 512 + 512],
                        thrbc[:, cn:cn + 1], None, ALU.is_lt), sVE)
                    qv = sch.add_sync(v, lambda e: e.tensor_sub(
                        qout[:, :], tq1[:, :], tq2[:, :]), sVE)
                    sch.w(s, sVE, qv)
                    qsem = sQS if widx == 0 else sQS2
                    qstate["last_q_dma"] = (qsem, sch.add(
                        s, lambda e, st=st, hh=hh: st[2](e, hh), qsem, 16))
                mark(f"qv_{i}", sVE)
                qstate["last_qv"] = f"qv_{i}"

        deferred_ln = []
        gidx_of = lambda b, qt, gg: (b * QTB + qt) * NGQ + gg

        for b in range(NB):
            for dc2 in range(2):
                if b > 0:
                    W(g, ("xtr", b - 1, 6 + dc2))
                ld = sch.add(g, lambda e, b=b, dc=dc2: e.dma_start(
                    AP(x_bf, (dc % 2) * 512, [[1024, 128], [128, 4], [1, 128]]),
                    AP(xh, b * 524288 + dc * 128, [[1024, 128], [131072, 4], [1, 128]])),
                    sXBF[dc2 % 2], 16)
                wp[("xbf", b, dc2)] = (sXBF[dc2 % 2], ld)

            trn = [0]
            wp.setdefault(("trch", 0, "prev"), None)
            wp.setdefault(("trch", 1, "prev"), None)

            def Wtr(eng, key):
                val = wp.get(key)
                if val:
                    sch.w(eng, val[0], val[1])

            for dc in range(8):
                for tc4 in range(4):
                    fw = []
                    if b == 0 and dc == 0 and tc4 == 0:
                        fw = [("w_eye",), "prologue_v"]
                    i = trn[0]; ch = i % 2
                    W(t, ("xbf", b, dc))
                    for wkey in fw:
                        W(t, wkey)
                    if b > 0 and dc == 0 and tc4 == 0:
                        W(t, ("qkv_done", b - 1))
                    Wtr(t, ("trch", ch, "prev"))
                    tm = sch.add(t, lambda e, dc=dc, tc=tc4, ch=ch: e.transpose(
                        ps_tr[:, ch * 1024:ch * 1024 + 128],
                        x_bf[:, (dc % 2) * 512 + tc * 128:(dc % 2) * 512 + tc * 128 + 128],
                        eye[:, :]), sTT)
                    trn[0] += 1
                    eng, sem = ((sc, sSC) if ch == 0 else (v, sVE))
                    sch.w(eng, sTT, tm)
                    ev = sch.add(eng, lambda e, dc=dc, tc=tc4, ch=ch: (
                        e.copy(xT[:, dc * 512 + tc * 128:dc * 512 + tc * 128 + 128],
                               ps_tr[:, ch * 1024:ch * 1024 + 128])
                        if ch == 0 else
                        e.tensor_copy(xT[:, dc * 512 + tc * 128:dc * 512 + tc * 128 + 128],
                                      ps_tr[:, ch * 1024:ch * 1024 + 128])), sem)
                    wp[("trch", ch, "prev")] = (sem, ev)
                wp[("xtr", b, dc)] = (sTT, sch.v(sTT))
                wp[("xtevacS", b, dc)] = (sSC, sch.v(sSC))
                wp[("xtevacV", b, dc)] = (sVE, sch.v(sVE))
                if dc + 2 < 8:
                    sch.w(g, sTT, sch.v(sTT))
                    ld = sch.add(g, lambda e, b=b, dc=dc + 2: e.dma_start(
                        AP(x_bf, (dc % 2) * 512, [[1024, 128], [128, 4], [1, 128]]),
                        AP(xh, b * 524288 + dc * 128, [[1024, 128], [131072, 4], [1, 128]])),
                        sXBF[dc % 2], 16)
                    wp[("xbf", b, dc + 2)] = (sXBF[dc % 2], ld)

            for pi, (W_sb, brow, dst, dup) in enumerate((
                    (Wq_sb, bq_row, Qd, True), (Wk_sb, bk_row, Kd, True),
                    (Wv_sb, bv_row, Vs, False))):
                for tc in range(4):
                    for dh in range(2):
                        n = pi * 8 + tc * 2 + dh
                        if b == 0 and tc == 0 and dh == 0:
                            W(t, ("w_" + ("wq", "wk", "wv")[pi],))
                            W(t, ("w_" + ("bq", "bk", "bv")[pi],))
                        W(t, ("xtevacS", b, 7))
                        W(t, ("xtevacV", b, 7))
                        if n >= 2:
                            W(t, ("qkvev", b, n - 2))
                        elif b > 0:
                            sch.w(t, sVN, b * QTB)
                        sch.add(t, lambda e, brow=brow, dh=dh, n=n: e.matmul(
                            ps_pv[n % 2][:, :], ones[0:1, :], brow[0:1, dh * 512:dh * 512 + 512],
                            start=True, stop=False), None)
                        for fc in range(8):
                            tm = sch.add(t, lambda e, W_sb=W_sb, fc=fc, tc=tc, dh=dh, n=n: e.matmul(
                                ps_pv[n % 2][:, :],
                                xT[:, fc * 512 + tc * 128:fc * 512 + tc * 128 + 128],
                                W_sb[:, fc * 1024 + dh * 512:fc * 1024 + dh * 512 + 512],
                                start=False, stop=(fc == 7)), sTT if fc == 7 else None)
                        wp[("qkvmm", b, pi, tc, dh)] = (sTT, tm)
                        W(v, ("qkvmm", b, pi, tc, dh))
                        prevu = (b, pi, tc - 2) if tc >= 2 else (
                            (b, pi - 1, tc + 2) if pi > 0 else
                            ((b - 1, 2, tc + 2) if b > 0 else None))
                        if prevu is not None and dh == 0:
                            W(v, ("folddma",) + prevu)
                        ev = sch.add(v, lambda e, tc=tc, dh=dh, n=n: e.tensor_copy(
                            Qtm[:, (tc % 2) * 1024 + dh * 512:(tc % 2) * 1024 + dh * 512 + 512],
                            ps_pv[n % 2][:, :]), sVE)
                        wp[("qkvev", b, n)] = (sVE, ev)
                        wp[("qkvevac", b, pi, tc, dh)] = (sVE, ev)
                    W(s, ("qkvevac", b, pi, tc, 1))
                    if pi == 2:
                        pass
                    sch.add(s, lambda e, dst=dst, tc=tc: e.dma_start(
                        AP(dst, 16 * tc * 8192, [[8192, 16], [1024, 8], [1, 1024]]),
                        AP(Qtm, (tc % 2) * 1024, [[2048, 128], [1, 1024]])), sFoldP[tc % 2], 16)
                    if dup:
                        fd = sch.add(s, lambda e, dst=dst, tc=tc: e.dma_start(
                            AP(dst, (64 + 16 * tc) * 8192, [[8192, 16], [1024, 8], [1, 1024]]),
                            AP(Qtm, (tc % 2) * 1024, [[2048, 128], [1, 1024]])), sFoldP[tc % 2], 16)
                    else:
                        fd = sch.v(sFoldP[tc % 2])
                    wp[("folddma", b, pi, tc)] = (sFoldP[tc % 2], fd)
            mark(("qkv_done", b), sTT)
            wp[("qkvevac_last", b)] = (sVE, sch.v(sVE))
            wp[("folds", b)] = (sFoldP[0], sch.v(sFoldP[0]))
            wp[("folds1", b)] = (sFoldP[1], sch.v(sFoldP[1]))

            for cc in range(64):
                ch = cc % 2
                if cc < 2:
                    W(t, ("folds", b))
                    W(t, ("folds1", b))
                Wtr(t, ("trch", ch, "prev"))
                tm = sch.add(t, lambda e, cc=cc, ch=ch: e.transpose(
                    ps_tr[:, ch * 1024:ch * 1024 + 64],
                    Vs[0:64, cc * 128:cc * 128 + 128], eye[0:64, 0:64]), sTT)
                eng, sem = ((sc, sSC) if ch == 0 else (v, sVE))
                sch.w(eng, sTT, tm)
                ev = sch.add(eng, lambda e, cc=cc, ch=ch: (
                    e.copy(VsT[:, 65 * cc:65 * cc + 64],
                           ps_tr[:, ch * 1024:ch * 1024 + 64])
                    if ch == 0 else
                    e.tensor_copy(VsT[:, 65 * cc:65 * cc + 64],
                                  ps_tr[:, ch * 1024:ch * 1024 + 64])), sem)
                wp[("trch", ch, "prev")] = (sem, ev)
            wp[("vst", b)] = (sVE, sch.v(sVE))
            wp[("vstS", b)] = (sSC, sch.v(sSC))

            PVLAG = 2
            for qt in range(QTB):
                qtg = b * QTB + qt
                for gg in range(NGQ + PVLAG):
                    gi = gidx_of(b, qt, gg)
                    if gg < NGQ:
                        if qt == 0 and gg == 0:
                            sch.w(t, sVE, wp[("vst", b)][1])
                            sch.w(t, sSC, wp[("vstS", b)][1])
                            W(t, ("folds", b))
                            W(t, ("folds1", b))
                            if b > 0:
                                W(t, ("opev", b - 1, 7))
                        if gi >= 2:
                            sch.w(t, sEXP, gi - 1)
                        hh = (gg % 2) * 1024
                        cc = 2 * gg
                        sch.add(t, lambda e, cc=cc, qt=qt, hh=hh: e.matmul(
                            ps_sb[:, hh:hh + 512],
                            Kd[0:64, cc * 128:cc * 128 + 128],
                            Qd[0:64, qt * 512:qt * 512 + 512],
                            start=True, stop=True, tile_position=(0, 0)), None)
                        ts = sch.add(t, lambda e, cc=cc, qt=qt, hh=hh: e.matmul(
                            ps_sb[:, hh + 512:hh + 1024],
                            Kd[64:128, (cc + 1) * 128:(cc + 1) * 128 + 128],
                            Qd[64:128, qt * 512:qt * 512 + 512],
                            start=True, stop=True, tile_position=(64, 0)), sTS)
                        sch.w(sc, sTS, ts)
                        if gi >= 3:
                            sch.w(sc, sPVC, gi - 2)
                        ex = sch.add(sc, lambda e, gg=gg: e.activation(
                            PT[:, (gg % 4) * 1024:(gg % 4) * 1024 + 1024],
                            ps_sb[:, (gg % 2) * 1024:(gg % 2) * 1024 + 1024],
                            AF.Exp, scale=SOFT_SCALE), sEXP)
                        wp[("exp", gi)] = (sEXP, ex)
                    pg = gg - PVLAG
                    if pg >= 0:
                        pgi = gidx_of(b, qt, pg)
                        W(t, ("exp", pgi))
                        if pg == 0 and qtg >= 2:
                            sch.w(t, sVN, qtg - 1)
                        if pg == 0 and qt < 2:
                            W(t, ("qkvev", b, 23))
                        sch.add(t, lambda e, pg=pg, qtg=qtg: e.matmul(
                            ps_pv[qtg % 2][0:65, :],
                            VsT[:, (2 * pg) * 65:(2 * pg) * 65 + 65],
                            PT[:, (pg % 4) * 1024:(pg % 4) * 1024 + 512],
                            start=(pg == 0), stop=False), None)
                        sch.add(t, lambda e, pg=pg, qtg=qtg: e.matmul(
                            ps_pv[qtg % 2][0:65, :],
                            VsT[:, (2 * pg + 1) * 65:(2 * pg + 1) * 65 + 65],
                            PT[:, (pg % 4) * 1024 + 512:(pg % 4) * 1024 + 1024],
                            start=False, stop=(pg == NGQ - 1)), sPVC)
                sch.w(v, sPVC, NGQ * (qtg + 1))
                if qtg >= 1:
                    W(v, ("recwr", qtg - 1))
                rc = sch.add(v, lambda e, qtg=qtg: e.reciprocal(
                    rec_sb[0:1, :], ps_pv[qtg % 2][64:65, :]), sREC)
                sch.w(g, sREC, rc)
                wr = sch.add(g, lambda e: e.dma_start(
                    AP(recscrh, 0, [[512, 1], [1, 512]]), rec_sb[0:1, :]), sG, 16)
                wp[("recwr", qtg)] = (sG, wr)
                sch.w(g, sG, wr)
                if qtg >= 1:
                    sch.w(g, sVN, qtg)
                rd = sch.add(g, lambda e, qtg=qtg: e.dma_start(
                    rec_bc[0:64, 0:512],
                    AP(recscrh, 0, [[0, 64], [1, 512]])), sG, 16)
                sch.w(v, sG, rd)
                if qt == 0 and b > 0:
                    W(v, ("outproj", b - 1))
                vn = sch.add(v, lambda e, qtg=qtg, qt=qt: e.scalar_tensor_tensor(
                    A2[0:64, qt * 512:qt * 512 + 512], ps_pv[qtg % 2][0:64, :], 1.0,
                    rec_bc[0:64, 0:512],
                    ALU.mult, ALU.mult), sVN)
                sch.w(g, sVN, vn)
                ad = sch.add(g, lambda e, qt=qt: e.dma_start(
                    A2[64:128, qt * 512:qt * 512 + 512],
                    A2[0:64, qt * 512:qt * 512 + 512]), sG, 16)
                wp[("a2dup", qtg)] = (sG, ad)
                emit_qstep()
                emit_qstep()

            for tc in range(4):
                if b == NB - 1 and tc == 2:
                    sch.w(s, sPVC, NB * QTB * NGQ)
                    sch.w(s, sQS, sch.v(sQS))
                    sch.w(s, sG, wp[("a2dup", NB * QTB - 1)][1])
                    for dc in range(8):
                        sch.add(s, lambda e, dc=dc: e.dma_start_transpose(
                            WqT1[:, dc * 4096:dc * 4096 + 4096],
                            AP(wqt1h, dc * 128, [[1024, 4096], [1, 128]])), sWQ1, 16)
                    wp[("wqt1",)] = (sWQ1, sch.v(sWQ1))
                if b > 0 or tc >= 2:
                    bb, tt = (b, tc - 2) if tc >= 2 else (b - 1, tc + 2)
                    W(s, ("s1evac", bb, tt))
                ld = sch.add(s, lambda e, b=b, tc=tc: e.dma_start(
                    x_tc[:, (tc % 2) * 1024:(tc % 2) * 1024 + 1024],
                    AP(xh, (b * 512 + tc * 128) * 1024, [[1024, 128], [1, 1024]])),
                    sXL[tc % 2], 16)
                wp[("xload", b, tc)] = (sXL[tc % 2], ld)
                for dh in range(2):
                    n = tc * 2 + dh
                    u = (n % 2) * 2
                    if n == 0:
                        if b == 0:
                            W(t, ("w_wo",))
                            W(t, ("w_bo",))
                        sch.w(t, sVN, b * QTB + QTB)
                        sch.w(t, sG, wp[("a2dup", b * QTB + QTB - 1)][1])
                    if n >= 2:
                        W(t, ("opev", b, n - 2))
                    sch.add(t, lambda e, dh=dh, u=u: e.matmul(
                        ps_sb[:, u * 512:u * 512 + 512], ones[0:1, :],
                        bo_row[0:1, dh * 512:dh * 512 + 512],
                        start=True, stop=False, tile_position=(0, 0)), None)
                    for m in range(8):
                        for par in range(2):
                            r = 2 * m + par
                            lo = par * 64 * 8192
                            tm = sch.add(t, lambda e, r=r, m=m, tc=tc, dh=dh, lo=lo, par=par, u=u: e.matmul(
                                ps_sb[:, (u + par) * 512:(u + par) * 512 + 512],
                                AP(A2, lo + 2048 * tc + r, [[8192, 64], [16, 128]]),
                                AP(Wo_sb, lo + m * 1024 + dh * 512, [[8192, 64], [1, 512]]),
                                start=(par == 1 and m == 0), stop=(m == 7),
                                tile_position=(64 * par, 0)), sTT if r == 15 else None)
                    wp[("opmm", b, tc, dh)] = (sTT, tm)
                    W(v, ("opmm", b, tc, dh))
                    sch.w(v, wp[("xload", b, tc)][0], wp[("xload", b, tc)][1])
                    sch.add_sync(v, lambda e, tc=tc, dh=dh, u=u: e.scalar_tensor_tensor(
                        s1[:, (tc % 2) * 1024 + dh * 512:(tc % 2) * 1024 + dh * 512 + 512],
                        ps_sb[:, u * 512:u * 512 + 512], 1.0,
                        x_tc[:, (tc % 2) * 1024 + dh * 512:(tc % 2) * 1024 + dh * 512 + 512],
                        ALU.mult, ALU.add), sVE)
                    ev = sch.add_sync(v, lambda e, tc=tc, dh=dh, u=u: e.scalar_tensor_tensor(
                        s1[:, (tc % 2) * 1024 + dh * 512:(tc % 2) * 1024 + dh * 512 + 512],
                        ps_sb[:, (u + 1) * 512:(u + 1) * 512 + 512], 1.0,
                        s1[:, (tc % 2) * 1024 + dh * 512:(tc % 2) * 1024 + dh * 512 + 512],
                        ALU.mult, ALU.add), sVE)
                    wp[("opev", b, n)] = (sVE, ev)
                    wp[("s1evac", b, tc)] = (sVE, ev)
                if tc == 3:
                    mark(("outproj", b), sTT)
                def emit_ln1(b=b, tc=tc):
                    sl = (tc % 2) * 1024
                    if b > 0 or tc >= 2:
                        bb, tt = (b, tc - 2) if tc >= 2 else (b - 1, tc + 2)
                        W(v, ("x1w", bb, tt))
                    sch.add_sync(v, lambda e, sl=sl: e.reduce_sum(
                        negsum[:, :], s1[:, sl:sl + 1024], AX.X, negate=True), sVE)
                    sch.add_sync(v, lambda e: e.tensor_scalar_mul(
                        negmean[:, :], negsum[:, :], 1.0 / 1024.0), sVE)
                    sch.add_sync(v, lambda e, sl=sl: e.tensor_scalar(
                        s1[:, sl:sl + 1024], s1[:, sl:sl + 1024],
                        negmean[:, :], None, ALU.add), sVE)
                    sch.add_sync(v, lambda e, sl=sl: e.scalar_tensor_tensor(
                        x1[:, sl:sl + 1024], s1[:, sl:sl + 1024], 1.0,
                        s1[:, sl:sl + 1024], ALU.mult, ALU.mult,
                        accum_out=varsum[:, :]), sVE)
                    lv = sch.add(v, lambda e: e.tensor_scalar(
                        vareps[:, :], varsum[:, :], 1.0 / 1024.0, LN_EPS,
                        ALU.mult, ALU.add), sVE)
                    sch.w(sc, sVE, lv)
                    sch.add_sync(sc, lambda e: e.activation(uln[:, :], vareps[:, :], AF.Ln), sSC)
                    sr = sch.add(sc, lambda e: e.activation(
                        rstd[:, :], uln[:, :], AF.Exp, scale=-0.5), sSC)
                    sch.w(v, sSC, sr)
                    x1v = sch.add(v, lambda e, sl=sl: e.tensor_scalar(
                        x1[:, sl:sl + 1024], s1[:, sl:sl + 1024],
                        rstd[:, :], None, ALU.mult), sVE)
                    gq = b == NB - 1 and tc < 2
                    weng = g if gq else s
                    xsem = sXW3 if gq else sX1W[tc % 2]
                    sch.w(weng, sVE, x1v)
                    xw = sch.add(weng, lambda e, b=b, tc=tc, sl=sl: e.dma_start(
                        AP(x1scrh, (b * 512 + tc * 128) * 1024, [[1024, 128], [1, 1024]]),
                        x1[:, sl:sl + 1024]), xsem, 16)
                    wp[("x1w", b, tc)] = (xsem, 32 if gq else xw)
                if b < NB - 1 or tc < 2:
                    emit_ln1()
                else:
                    deferred_ln.append(emit_ln1)
                emit_qstep()
                emit_qstep()
            if b < NB - 1:
                sch.w(g, sX1W[0], wp[("x1w", b, 2)][1])
                sch.w(g, sX1W[1], wp[("x1w", b, 3)][1])
                sch.add(g, lambda e, b=b: e.dma_start(
                    AP(x1bfh, b * 524288, [[1024, 512], [1, 1024]]),
                    AP(x1scrh, b * 524288, [[1024, 512], [1, 1024]])), sCB[b], 16)

        while qstate["i"] < len(qsteps):
            emit_qstep()

        ATT_T = sch.v(sTT); ATT_V = sch.v(sVE); ATT_G = sch.v(sG); ATT_S = sch.v(sS)


        def emit_x1t(b):
            sch.w(s, sCB[b], 16)
            if b == 0:
                W(s, ("outproj", NB - 1))
                W(s, ("s1evac", NB - 1, 3))
            if b > 0:
                W(s, ("w1done", b - 1))
            for dc in range(8):
                sch.add(s, lambda e, b=b, dc=dc: e.dma_start_transpose(
                    x1T[:, dc * 512:dc * 512 + 512],
                    AP(x1bfh, b * 524288 + dc * 128, [[1024, 512], [1, 128]])), sX1T, 16)
            wp[("x1T", b)] = (sX1T, sch.v(sX1T))

        for b in range(NB):
            if b == 0:
                emit_x1t(0)
            if b == 0:
                W(s, ("outproj", NB - 1))
                sch.w(s, sQS2, sch.v(sQS2))
                for oc in range(32):
                    sch.add(s, lambda e, oc=oc: e.dma_start_transpose(
                        WqT2[:, oc * 1024:oc * 1024 + 1024],
                        AP(wq2h, oc * 128, [[4096, 1024], [1, 128]])), sWQT, 16)
                wp[("wqt",)] = (sWQT, sch.v(sWQT))

            for oc in range(32):
                W(t, ("x1T", b))
                W(t, ("wqt1",))
                if oc >= 2:
                    W(t, ("hevac", b, oc - 2))
                elif b > 0:
                    W(t, ("hevac", b - 1, oc + 30))
                for dc in range(8):
                    tm = sch.add(t, lambda e, oc=oc, dc=dc: e.matmul(
                        ps_pv[oc % 2][:, :],
                        WqT1[:, dc * 4096 + oc * 128:dc * 4096 + oc * 128 + 128],
                        x1T[:, dc * 512:dc * 512 + 512],
                        start=(dc == 0), stop=(dc == 7)), sTT if dc == 7 else None)
                wp[("w1mm", b, oc)] = (sTT, tm)
                W(v, ("w1mm", b, oc))
                if b > 0 and oc < 2:
                    W(v, ("w2done", b - 1))
                if b == 0 and oc == 0:
                    sch.w(v, sWG[4], 36 * 16)
                ev = sch.add(v, lambda e, oc=oc: e.tensor_scalar(
                    hT[:, oc * 512:oc * 512 + 512], ps_pv[oc % 2][:, :],
                    b1c[:, oc:oc + 1], 0.0, ALU.add, ALU.max), sVE)
                wp[("hevac", b, oc)] = (sVE, ev)
            mark(("w1done", b), sTT)
            if b + 1 < NB:
                emit_x1t(b + 1)
            if b == 0:
                for f in deferred_ln:
                    f()
                bb = NB - 1
                sch.w(g, sX1W[0], wp[("x1w", bb, 2)][1])
                sch.w(g, sX1W[1], wp[("x1w", bb, 3)][1])
                sch.add(g, lambda e: e.dma_start(
                    AP(x1bfh, bb * 524288, [[1024, 512], [1, 1024]]),
                    AP(x1scrh, bb * 524288, [[1024, 512], [1, 1024]])), sCB[bb], 16)

            for tsub in range(4):
                if b > 0 or tsub >= 2:
                    bb, tt = (b, tsub - 2) if tsub >= 2 else (b - 1, tsub + 2)
                    W(s, ("s2done", bb, tt))
                else:
                    sch.w(s, sX1W[0], wp[("x1w", NB - 1, 2)][1])
                    sch.w(s, sX1W[1], wp[("x1w", NB - 1, 3)][1])
                ld = sch.add(s, lambda e, b=b, tsub=tsub: e.dma_start(
                    x1f[:, (tsub % 2) * 1024:(tsub % 2) * 1024 + 1024],
                    AP(x1scrh, (b * 512 + tsub * 128) * 1024, [[1024, 128], [1, 1024]])),
                    sX1F[tsub % 2], 16)
                wp[("x1fload", b, tsub)] = (sX1F[tsub % 2], ld)
                for dh in range(2):
                    W(t, ("hevac", b, 31))
                    if b == 0 and tsub == 0 and dh == 0:
                        W(t, ("wqt",))
                        W(t, ("wqt1",))
                    if b * 4 + tsub >= 2:
                        pb, pt = divmod(b * 4 + tsub - 2, 4)
                        W(t, ("s2evac", pb, pt))
                    sch.add(t, lambda e, tsub=tsub, dh=dh: e.matmul(
                        ps_sb[:, (tsub % 2) * 1024 + dh * 512:(tsub % 2) * 1024 + dh * 512 + 512],
                        ones[0:1, :], b2_row[0:1, dh * 512:dh * 512 + 512],
                        start=True, stop=False), None)
                    for oc in range(32):
                        tm = sch.add(t, lambda e, tsub=tsub, dh=dh, oc=oc: e.matmul(
                            ps_sb[:, (tsub % 2) * 1024 + dh * 512:(tsub % 2) * 1024 + dh * 512 + 512],
                            hT[:, oc * 512 + tsub * 128:oc * 512 + tsub * 128 + 128],
                            WqT2[:, oc * 1024 + dh * 512:oc * 1024 + dh * 512 + 512],
                            start=False, stop=(oc == 31)),
                            sTT if (dh == 1 and oc == 31) else None)
                wp[("w2mm", b, tsub)] = (sTT, tm)
                W(v, ("w2mm", b, tsub))
                sch.w(v, sX1F[tsub % 2], ld)
                if b * 4 + tsub >= 1:
                    pb, pt = divmod(b * 4 + tsub - 1, 4)
                    W(v, ("outdma", pb, pt))
                sl = 0
                xsl = (tsub % 2) * 1024
                if b == 0 and tsub == 0:
                    sch.w(v, sX1W[0], wp[("x1w", NB - 1, 2)][1])
                    sch.w(v, sX1W[1], wp[("x1w", NB - 1, 3)][1])
                ev = sch.add_sync(v, lambda e, tsub=tsub, xsl=xsl: e.scalar_tensor_tensor(
                    s2[:, 0:1024], ps_sb[:, (tsub % 2) * 1024:(tsub % 2) * 1024 + 1024], 1.0,
                    x1f[:, xsl:xsl + 1024], ALU.mult, ALU.add), sVE)
                wp[("s2evac", b, tsub)] = (sVE, ev)
                wp[("s2done", b, tsub)] = (sVE, ev)
                sch.add_sync(v, lambda e: e.reduce_sum(
                    negsum[:, :], s2[:, 0:1024], AX.X, negate=True), sVE)
                sch.add_sync(v, lambda e: e.tensor_scalar_mul(
                    negmean[:, :], negsum[:, :], 1.0 / 1024.0), sVE)
                sch.add_sync(v, lambda e: e.tensor_scalar(
                    s2[:, 0:1024], s2[:, 0:1024],
                    negmean[:, :], None, ALU.add), sVE)
                sch.add_sync(v, lambda e: e.scalar_tensor_tensor(
                    outb[:, 0:1024], s2[:, 0:1024], 1.0,
                    s2[:, 0:1024], ALU.mult, ALU.mult,
                    accum_out=varsum[:, :]), sVE)
                lv = sch.add(v, lambda e: e.tensor_scalar(
                    vareps[:, :], varsum[:, :], 1.0 / 1024.0, LN_EPS,
                    ALU.mult, ALU.add), sVE)
                sch.w(sc, sVE, lv)
                sch.add_sync(sc, lambda e: e.activation(uln[:, :], vareps[:, :], AF.Ln), sSC)
                sr = sch.add(sc, lambda e: e.activation(
                    rstd[:, :], uln[:, :], AF.Exp, scale=-0.5), sSC)
                sch.w(v, sSC, sr)
                ov = sch.add(v, lambda e: e.tensor_scalar(
                    outb[:, 0:1024], s2[:, 0:1024],
                    rstd[:, :], None, ALU.mult), sVE)
                wp[("outw", b, tsub)] = (sVE, ov)
                sch.w(s, sVE, ov)
                od = sch.add(s, lambda e, b=b, tsub=tsub: e.dma_start(
                    AP(yh, (b * 512 + tsub * 128) * 1024, [[1024, 128], [1, 1024]]),
                    outb[:, 0:1024]), sOUT, 16)
                wp[("outdma", b, tsub)] = (sOUT, od)
            mark(("w2done", b), sTT)

        sch.w(s, sOUT, sch.v(sOUT))
        sch.w(s, sX1T, sch.v(sX1T))
        for i in range(2):
            sch.w(s, sX1F[i], sch.v(sX1F[i]))
        sch.w(g, sG, sch.v(sG))
        for i in range(4):
            sch.w(g, sCB[i], 16)
        sch.w(g, sXW3, sch.v(sXW3))

        @blk.sync
        def _(e):
            for f in sch.L["sync"]:
                f(e)

        @blk.gpsimd
        def _(e):
            for f in sch.L["gpsimd"]:
                f(e)

        @blk.tensor
        def _(e):
            for f in sch.L["tensor"]:
                f(e)

        @blk.vector
        def _(e):
            for f in sch.L["vector"]:
                f(e)

        @blk.scalar
        def _(e):
            for f in sch.L["scalar"]:
                f(e)

    return nc


def kernel(L0_attn_x, L0_attn_Wq, L0_attn_bq, L0_attn_Wk, L0_attn_bk,
           L0_attn_Wv, L0_attn_bv, L0_attn_Wo, L0_attn_bo,
           L0_ffn_W1, L0_ffn_b1, L0_ffn_W2, L0_ffn_b2):
    x = np.ascontiguousarray(np.asarray(L0_attn_x, dtype=np.float32)).reshape(16384, 1024)
    com = dict(
        Wq=np.ascontiguousarray(np.asarray(L0_attn_Wq, np.float32)),
        bq=np.ascontiguousarray(np.asarray(L0_attn_bq, np.float32)),
        Wk=np.ascontiguousarray(np.asarray(L0_attn_Wk, np.float32)),
        bk=np.ascontiguousarray(np.asarray(L0_attn_bk, np.float32)),
        Wv=np.ascontiguousarray(np.asarray(L0_attn_Wv, np.float32)),
        bv=np.ascontiguousarray(np.asarray(L0_attn_bv, np.float32)),
        Wo=np.ascontiguousarray(np.asarray(L0_attn_Wo, np.float32)),
        bo=np.ascontiguousarray(np.asarray(L0_attn_bo, np.float32)),
        W1=np.ascontiguousarray(np.asarray(L0_ffn_W1, np.float32)),
        b1=np.ascontiguousarray(np.asarray(L0_ffn_b1, np.float32)),
        W2=np.ascontiguousarray(np.asarray(L0_ffn_W2, np.float32)),
        b2=np.ascontiguousarray(np.asarray(L0_ffn_b2, np.float32)),
        eye=np.eye(128, dtype=np.float32),
    )
    nc = build_nc()
    in_maps = [dict(com, x=np.ascontiguousarray(x[c * 2048:(c + 1) * 2048]))
               for c in range(8)]
    res = run_bass_kernel_spmd(nc, in_maps, core_ids=list(range(8)))
    out = np.concatenate([np.asarray(r["y"], np.float32) for r in res.results], axis=0)
    return out.reshape(2, 8192, 1024)


# version 13
# speedup vs baseline: 1.2426x; 1.0239x over previous
import numpy as np
import concourse.bass as bass
import concourse.mybir as mybir
from concourse.bass_utils import run_bass_kernel_spmd

F32 = mybir.dt.float32
BF16 = mybir.dt.bfloat16
AF = mybir.ActivationFunctionType
ALU = mybir.AluOpType
AX = mybir.AxisListType
AP = bass.AP

NB = 4
QTB = 16
NGQ = 32
SOFT_SCALE = float(2.0 ** -10)
LN_EPS = 1e-5
QN = 4096 * 1024


class Sch:

    def __init__(self):
        self.L = {k: [] for k in ("sync", "gpsimd", "tensor", "vector", "scalar")}
        self.c = {}

    def w(self, eng, sem, v):
        if v > 0:
            self.L[eng].append(lambda e, s=sem, vv=v: e.wait_ge(s, vv))

    def add(self, eng, f, inc=None, n=1):
        if inc is None:
            self.L[eng].append(f)
            return None
        tot = self.c[id(inc)] = self.c.get(id(inc), 0) + n
        self.L[eng].append(lambda e, ff=f, s=inc, nn=n: ff(e).then_inc(s, nn))
        return tot

    def v(self, sem):
        return self.c.get(id(sem), 0)

    def add_sync(self, eng, f, inc):
        tot = self.add(eng, f, inc)
        self.w(eng, inc, tot)
        return tot


def build_nc():
    nc = bass.Bass("TRN2", target_bir_lowering=False)

    xh = nc.dram_tensor("x", [2048, 1024], F32, kind="ExternalInput")
    wqh = nc.dram_tensor("Wq", [1024, 1024], F32, kind="ExternalInput")
    bqh = nc.dram_tensor("bq", [1024], F32, kind="ExternalInput")
    wkh = nc.dram_tensor("Wk", [1024, 1024], F32, kind="ExternalInput")
    bkh = nc.dram_tensor("bk", [1024], F32, kind="ExternalInput")
    wvh = nc.dram_tensor("Wv", [1024, 1024], F32, kind="ExternalInput")
    bvh = nc.dram_tensor("bv", [1024], F32, kind="ExternalInput")
    woh = nc.dram_tensor("Wo", [1024, 1024], F32, kind="ExternalInput")
    boh = nc.dram_tensor("bo", [1024], F32, kind="ExternalInput")
    w1h = nc.dram_tensor("W1", [4096, 1024], F32, kind="ExternalInput")
    b1h = nc.dram_tensor("b1", [4096], F32, kind="ExternalInput")
    w2h = nc.dram_tensor("W2", [1024, 4096], F32, kind="ExternalInput")
    b2h = nc.dram_tensor("b2", [1024], F32, kind="ExternalInput")
    eyeh = nc.dram_tensor("eye", [128, 128], F32, kind="ExternalInput")
    yh = nc.dram_tensor("y", [2048, 1024], F32, kind="ExternalOutput")

    wqt1h = nc.dram_tensor("wqt1", [4096, 1024], BF16, kind="Internal")
    wq2h = nc.dram_tensor("wq2", [1024, 4096], BF16, kind="Internal")
    x1scrh = nc.dram_tensor("x1scr", [2048, 1024], F32, kind="Internal")
    x1bfh = nc.dram_tensor("x1bfscr", [2048, 1024], BF16, kind="Internal")
    recscrh = nc.dram_tensor("recscr", [512], F32, kind="Internal")
    gscrh = nc.dram_tensor("gscr", [256], F32, kind="Internal")
    thrscrh = nc.dram_tensor("thrscr", [8], F32, kind="Internal")

    BASE = 16512
    off = [BASE]

    def alloc(name, shape, dtype, at=None):
        if at is None:
            at = off[0]
            nbytes = shape[-1] * (2 if dtype == BF16 else 4)
            for d in shape[1:-1]:
                nbytes *= d
            off[0] = at + ((nbytes + 31) // 32) * 32
        assert off[0] <= 229344, f"sbuf overflow at {name}: {off[0]}"
        return nc.alloc_sbuf_tensor_at(name, shape, dtype, offset=at)

    eye = alloc("eye", [128, 128], BF16)
    ones = alloc("ones", [1, 128], BF16)
    ones512 = alloc("ones512", [1, 512], BF16)
    bq_row = alloc("bq_row", [1, 1024], BF16)
    bk_row = alloc("bk_row", [1, 1024], BF16)
    bv_row = alloc("bv_row", [1, 1024], BF16)
    bo_row = alloc("bo_row", [1, 1024], BF16)
    b2_row = alloc("b2_row", [1, 1024], BF16)
    thrbc = alloc("thrbc", [128, 8], F32)
    gv = alloc("gv", [1, 16], F32)
    acc1 = alloc("acc1", [128, 1], F32)
    acc2 = alloc("acc2", [128, 1], F32)
    gtmp = alloc("gtmp", [128, 1], F32)
    negsum = alloc("negsum", [128, 1], F32)
    negmean = alloc("negmean", [128, 1], F32)
    varsum = alloc("varsum", [128, 1], F32)
    vareps = alloc("vareps", [128, 1], F32)
    uln = alloc("uln", [128, 1], F32)
    rstd = alloc("rstd", [128, 1], F32)
    warm = alloc("warm", [1, 256], BF16)
    b1c = alloc("b1c", [128, 32], F32)
    lnva = alloc("lnva", [128, 4], F32)

    A0 = off[0]

    Qd = alloc("Qd", [128, 8192], BF16)
    Kd = alloc("Kd", [128, 8192], BF16)
    Vs = alloc("Vs", [64, 8192], BF16)
    VsT = alloc("VsT", [128, 4160], BF16)
    PT = alloc("PT", [128, 4096], BF16)
    Wq_sb = alloc("Wq_sb", [128, 8192], BF16)
    Wk_sb = alloc("Wk_sb", [128, 8192], BF16)
    Wv_sb = alloc("Wv_sb", [128, 8192], BF16)
    Wo_sb = alloc("Wo_sb", [128, 8192], BF16)
    A2 = alloc("A2", [128, 8192], BF16)
    xT = alloc("xT", [128, 4096], BF16)
    x_tc = alloc("x_tc", [128, 2048], F32)
    x_bf = alloc("x_bf", [128, 1024], BF16)
    Qtm = alloc("Qtm", [128, 2048], BF16)
    rec_sb = alloc("rec_sb", [1, 512], F32)
    rec_bc = alloc("rec_bc", [64, 512], F32)
    s1 = alloc("s1", [128, 2048], F32)
    x1 = alloc("x1", [128, 2048], F32)
    ATTN_END = off[0]

    off[0] = A0
    WqT1 = alloc("WqT1", [128, 32768], BF16)
    WqT2 = alloc("WqT2", [128, 32768], BF16)
    hT = alloc("hT", [128, 16384], BF16)
    x1T = alloc("x1T", [128, 4096], BF16)
    x1f = alloc("x1f", [128, 2048], F32)
    s2 = alloc("s2", [128, 1024], F32)
    outb = alloc("outb", [128, 1024], F32)
    FFN_END = off[0]
    off[0] = ATTN_END
    grow = alloc("grow", [1, 256], F32)
    Wch = alloc("Wch", [128, 1024], F32)
    tq1 = alloc("tq1", [128, 512], BF16)
    tq2 = alloc("tq2", [128, 512], BF16)
    qout = alloc("qout", [128, 512], BF16)
    QNT_END = off[0]
    assert QNT_END <= 229344 and FFN_END <= 229344, (ATTN_END, FFN_END, QNT_END)
    off[0] = max(ATTN_END, FFN_END, QNT_END)

    sch = Sch()

    from contextlib import ExitStack
    with ExitStack() as es:
        E = es.enter_context
        ps_sb = E(nc.psum_tensor("ps_sb", [128, 2048], F32))
        ps_pv = [E(nc.psum_tensor(f"ps_pv{i}", [128, 512], F32)) for i in range(2)]
        ps_tr = E(nc.psum_tensor("ps_tr", [128, 2048], BF16))

        sS = E(nc.semaphore("sS"))
        sG = E(nc.semaphore("sG"))
        sQS = E(nc.semaphore("sQS"))
        sQS2 = E(nc.semaphore("sQS2"))
        sFoldP = [E(nc.semaphore(f"sFold{i}")) for i in range(2)]
        sFIN = E(nc.semaphore("sFIN"))
        sXL = [E(nc.semaphore(f"sXL{i}")) for i in range(2)]
        sX1W = [E(nc.semaphore(f"sX1W{i}")) for i in range(2)]
        sXBF = [E(nc.semaphore(f"sXBF{i}")) for i in range(2)]
        sWQT = E(nc.semaphore("sWQT"))
        sX1T = E(nc.semaphore("sX1T"))
        sX1F = [E(nc.semaphore(f"sX1F{i}")) for i in range(2)]
        sOUT = E(nc.semaphore("sOUT"))
        sXB1 = E(nc.semaphore("sXB1"))
        sCB = [E(nc.semaphore(f"sCB{i}")) for i in range(4)]
        sXW3 = E(nc.semaphore("sXW3"))
        sTS = E(nc.semaphore("sTS"))
        sEXP = E(nc.semaphore("sEXP"))
        sPVC = E(nc.semaphore("sPVC"))
        sREC = E(nc.semaphore("sREC"))
        sVN = E(nc.semaphore("sVN"))
        sVE = E(nc.semaphore("sVE"))
        sTT = E(nc.semaphore("sTT"))
        sSC = E(nc.semaphore("sSC"))
        sWG = [E(nc.semaphore(f"sWG{i}")) for i in range(5)]
        sWQ1 = E(nc.semaphore("sWQ1"))
        blk = E(nc.Block())

        wp = {}

        def W(eng, key):
            if key in wp:
                sem, v = wp[key]
                sch.w(eng, sem, v)

        def mark(key, sem):
            wp[key] = (sem, sch.v(sem))

        g = "gpsimd"; s = "sync"; t = "tensor"; v = "vector"; sc = "scalar"

        sch.add(g, lambda e: e.dma_start(eye[:, :], eyeh.ap()[:, :]), sWG[0], 16)
        wp[("w_eye",)] = (sWG[0], 16)
        for i, (nm, W_sb, wh, row, bh) in enumerate((
                ("q", Wq_sb, wqh, bq_row, bqh),
                ("k", Wk_sb, wkh, bk_row, bkh),
                ("v", Wv_sb, wvh, bv_row, bvh))):
            sch.add(g, lambda e, d=W_sb, sr=wh: e.dma_start(
                AP(d, 0, [[8192, 128], [1024, 8], [1, 1024]]),
                AP(sr, 0, [[1024, 128], [131072, 8], [1, 1024]])), sWG[1 + i], 16)
            sch.add(g, lambda e, d=row, sr=bh: e.dma_start(
                AP(d, 0, [[1024, 1], [1, 1024]]),
                AP(sr, 0, [[1024, 1], [1, 1024]])), sWG[1 + i], 16)
            wp[("w_w" + nm,)] = wp[("w_b" + nm,)] = (sWG[1 + i], 32)
        sch.add(g, lambda e: e.dma_start(
            AP(Wo_sb, 0, [[8192, 64], [1024, 8], [1, 1024]]),
            AP(woh, 0, [[1024, 64], [131072, 8], [1, 1024]])), sWG[4], 16)
        sch.add(g, lambda e: e.dma_start(
            AP(Wo_sb, 64 * 8192, [[8192, 64], [1024, 8], [1, 1024]]),
            AP(woh, 65536, [[1024, 64], [131072, 8], [1, 1024]])), sWG[4], 16)
        for row, bh in ((bo_row, boh), (b2_row, b2h)):
            sch.add(g, lambda e, d=row, sr=bh: e.dma_start(
                AP(d, 0, [[1024, 1], [1, 1024]]),
                AP(sr, 0, [[1024, 1], [1, 1024]])), sWG[4], 16)
        for oc in range(32):
            sch.add(g, lambda e, oc=oc: e.dma_start(
                b1c[:, oc:oc + 1],
                AP(b1h, oc * 128, [[1, 128], [1, 1]])), sWG[4], 16)
        wp[("w_wo",)] = wp[("w_bo",)] = wp[("w_b2",)] = wp[("w_b1c",)] = (sWG[4], 36 * 16)
        mark("prologue_g", sG)

        sch.add(v, lambda e: e.memset(ones[0:1, :], 1.0), sVE)
        sch.add(v, lambda e: e.memset(ones512[0:1, :], 1.0), sVE)
        sch.add(v, lambda e: e.memset(VsT[:, :], 1.0), sVE)
        sch.add(v, lambda e: e.memset(acc1[:, :], 0.0), sVE)
        sch.add(v, lambda e: e.memset(acc2[:, :], 0.0), sVE)
        mark("prologue_v", sVE)

        W(sc, "prologue_v")
        sch.add(sc, lambda e: e.activation(warm[0:1, 0:128], ones[0:1, :], AF.Exp), sSC)
        sch.add(sc, lambda e: e.activation(warm[0:1, 128:256], ones[0:1, :], AF.Ln), sSC)

        qsteps = []

        def gamma_step(widx, ch):
            def fs(e, widx=widx, ch=ch):
                if widx == 0:
                    return e.dma_start(Wch[:, :], AP(w1h, ch * 131072,
                                                    [[1024, 128], [1, 1024]]))
                a, cg = ch % 8, ch // 8
                return e.dma_start(Wch[:, :], AP(w2h, a * 524288 + cg * 1024,
                                                 [[4096, 128], [1, 1024]]))
            def fv(ops, widx=widx, ch=ch):
                acc = acc1 if widx == 0 else acc2
                ops.append((v, "sync", lambda e: e.reduce_sum(gtmp[:, :], Wch[:, :], AX.X,
                                                      apply_absolute_value=True), sVE))
                ops.append((v, "plain", lambda e, a=acc: e.tensor_add(a[:, :], a[:, :], gtmp[:, :]), sVE))
            return ("gamma", fs, fv)

        def finalize_step():
            return ("finalize", None, None)

        def quant_step(widx, ch):
            def fs(e, widx=widx, ch=ch):
                if widx == 0:
                    return e.dma_start(Wch[:, :], AP(w1h, ch * 131072,
                                                    [[1024, 128], [1, 1024]]))
                a, cg = ch % 8, ch // 8
                return e.dma_start(Wch[:, :], AP(w2h, a * 524288 + cg * 1024,
                                                 [[4096, 128], [1, 1024]]))
            def fo(e, hh, widx=widx, ch=ch):
                if widx == 0:
                    return e.dma_start(AP(wqt1h, ch * 131072 + hh * 512,
                                          [[1024, 128], [1, 512]]), qout[:, :])
                a, cg = ch % 8, ch // 8
                return e.dma_start(AP(wq2h, a * 524288 + cg * 1024 + hh * 512,
                                      [[4096, 128], [1, 512]]), qout[:, :])
            return ("quant", fs, fo, widx)

        for widx in (0, 1):
            for ch in range(32):
                qsteps.append(gamma_step(widx, ch))
        qsteps.append(finalize_step())
        for widx in (0, 1):
            for ch in range(32):
                qsteps.append(quant_step(widx, ch))
        qstate = {"i": 0, "thr_ready": None, "last_q_dma": None, "last_qv": None}

        def emit_qstep():
            i = qstate["i"]
            if i >= len(qsteps):
                return
            qstate["i"] = i + 1
            st = qsteps[i]
            if st[0] == "gamma":
                if qstate["last_qv"]:
                    W(s, qstate["last_qv"])
                ld = sch.add(s, st[1], sS, 16)
                sch.w(v, sS, ld)
                ops = []
                st[2](ops)
                for eng, kind, f, sem in ops:
                    (sch.add_sync if kind == "sync" else sch.add)(eng, f, sem)
                mark(f"qv_{i}", sVE)
                qstate["last_qv"] = f"qv_{i}" 
            elif st[0] == "finalize":
                sch.w(g, sVE, sch.v(sVE))
                sch.add(g, lambda e: e.dma_start(AP(gscrh, 0, [[1, 128], [1, 1]]),
                                                 acc1[:, :]), sFIN, 16)
                sch.add(g, lambda e: e.dma_start(AP(gscrh, 128, [[1, 128], [1, 1]]),
                                                 acc2[:, :]), sFIN, 16)
                sch.w(g, sFIN, sch.v(sFIN))
                gld = sch.add(g, lambda e: e.dma_start(
                    grow[0:1, :], AP(gscrh, 0, [[256, 1], [1, 256]])), sFIN, 16)
                sch.w(v, sFIN, gld)
                sch.add(v, lambda e: e.reduce_sum(gv[0:1, 0:1], grow[0:1, 0:128], AX.X), sVE)
                sch.add_sync(v, lambda e: e.reduce_sum(gv[0:1, 1:2], grow[0:1, 128:256], AX.X), sVE)
                c1, c2 = 0.5 / QN, 0.5 * 1e-5
                sch.add(v, lambda e: e.tensor_scalar(gv[0:1, 4:5], gv[0:1, 0:1], c1, c2, ALU.mult, ALU.add), sVE)
                sch.add(v, lambda e: e.tensor_scalar(gv[0:1, 5:6], gv[0:1, 0:1], -c1, -c2, ALU.mult, ALU.add), sVE)
                sch.add(v, lambda e: e.tensor_scalar(gv[0:1, 6:7], gv[0:1, 1:2], c1, c2, ALU.mult, ALU.add), sVE)
                thr_v = sch.add(v, lambda e: e.tensor_scalar(gv[0:1, 7:8], gv[0:1, 1:2], -c1, -c2, ALU.mult, ALU.add), sVE)
                sch.w(g, sVE, thr_v)
                sch.add(g, lambda e: e.dma_start(AP(thrscrh, 0, [[8, 1], [1, 4]]),
                                                 gv[0:1, 4:8]), sFIN, 16)
                sch.w(g, sFIN, sch.v(sFIN))
                thr_g = sch.add(g, lambda e: e.dma_start(
                    thrbc[:, 0:4], AP(thrscrh, 0, [[0, 128], [1, 4]])), sFIN, 16)
                qstate["thr_ready"] = thr_g
            else:
                widx = st[3]
                if qstate["last_qv"]:
                    W(s, qstate["last_qv"])
                ld = sch.add(s, st[1], sS, 16)
                sch.w(v, sS, ld)
                if qstate["thr_ready"] is not None:
                    sch.w(v, sFIN, qstate["thr_ready"])
                    qstate["thr_ready"] = None
                cp, cn = 2 * widx, 2 * widx + 1
                for hh in range(2):
                    if qstate["last_q_dma"] is not None:
                        sch.w(v, *qstate["last_q_dma"])
                    sch.add(v, lambda e, cp=cp, hh=hh: e.tensor_scalar(
                        tq1[:, :], Wch[:, hh * 512:hh * 512 + 512],
                        thrbc[:, cp:cp + 1], None, ALU.is_gt), sVE)
                    sch.add_sync(v, lambda e, cn=cn, hh=hh: e.tensor_scalar(
                        tq2[:, :], Wch[:, hh * 512:hh * 512 + 512],
                        thrbc[:, cn:cn + 1], None, ALU.is_lt), sVE)
                    qv = sch.add_sync(v, lambda e: e.tensor_sub(
                        qout[:, :], tq1[:, :], tq2[:, :]), sVE)
                    sch.w(s, sVE, qv)
                    qsem = sQS if widx == 0 else sQS2
                    qstate["last_q_dma"] = (qsem, sch.add(
                        s, lambda e, st=st, hh=hh: st[2](e, hh), qsem, 16))
                mark(f"qv_{i}", sVE)
                qstate["last_qv"] = f"qv_{i}"

        deferred_ln = []
        gidx_of = lambda b, qt, gg: (b * QTB + qt) * NGQ + gg

        for b in range(NB):
            for dc2 in range(2):
                if b > 0:
                    W(g, ("xtr", b - 1, 6 + dc2))
                ld = sch.add(g, lambda e, b=b, dc=dc2: e.dma_start(
                    AP(x_bf, (dc % 2) * 512, [[1024, 128], [128, 4], [1, 128]]),
                    AP(xh, b * 524288 + dc * 128, [[1024, 128], [131072, 4], [1, 128]])),
                    sXBF[dc2 % 2], 16)
                wp[("xbf", b, dc2)] = (sXBF[dc2 % 2], ld)

            trn = [0]
            wp.setdefault(("trch", 0, "prev"), None)
            wp.setdefault(("trch", 1, "prev"), None)

            def Wtr(eng, key):
                val = wp.get(key)
                if val:
                    sch.w(eng, val[0], val[1])

            for dc in range(8):
                for tc4 in range(4):
                    fw = []
                    if b == 0 and dc == 0 and tc4 == 0:
                        fw = [("w_eye",), "prologue_v"]
                    i = trn[0]; ch = i % 2
                    W(t, ("xbf", b, dc))
                    for wkey in fw:
                        W(t, wkey)
                    if b > 0 and dc == 0 and tc4 == 0:
                        W(t, ("qkv_done", b - 1))
                    Wtr(t, ("trch", ch, "prev"))
                    tm = sch.add(t, lambda e, dc=dc, tc=tc4, ch=ch: e.transpose(
                        ps_tr[:, ch * 1024:ch * 1024 + 128],
                        x_bf[:, (dc % 2) * 512 + tc * 128:(dc % 2) * 512 + tc * 128 + 128],
                        eye[:, :]), sTT)
                    trn[0] += 1
                    eng, sem = ((sc, sSC) if ch == 0 else (v, sVE))
                    sch.w(eng, sTT, tm)
                    ev = sch.add(eng, lambda e, dc=dc, tc=tc4, ch=ch: (
                        e.copy(xT[:, dc * 512 + tc * 128:dc * 512 + tc * 128 + 128],
                               ps_tr[:, ch * 1024:ch * 1024 + 128])
                        if ch == 0 else
                        e.tensor_copy(xT[:, dc * 512 + tc * 128:dc * 512 + tc * 128 + 128],
                                      ps_tr[:, ch * 1024:ch * 1024 + 128])), sem)
                    wp[("trch", ch, "prev")] = (sem, ev)
                wp[("xtr", b, dc)] = (sTT, sch.v(sTT))
                wp[("xtevacS", b, dc)] = (sSC, sch.v(sSC))
                wp[("xtevacV", b, dc)] = (sVE, sch.v(sVE))
                if dc + 2 < 8:
                    sch.w(g, sTT, sch.v(sTT))
                    ld = sch.add(g, lambda e, b=b, dc=dc + 2: e.dma_start(
                        AP(x_bf, (dc % 2) * 512, [[1024, 128], [128, 4], [1, 128]]),
                        AP(xh, b * 524288 + dc * 128, [[1024, 128], [131072, 4], [1, 128]])),
                        sXBF[dc % 2], 16)
                    wp[("xbf", b, dc + 2)] = (sXBF[dc % 2], ld)

            for pi, (W_sb, brow, dst, dup) in enumerate((
                    (Wq_sb, bq_row, Qd, True), (Wk_sb, bk_row, Kd, True),
                    (Wv_sb, bv_row, Vs, False))):
                for tc in range(4):
                    for dh in range(2):
                        n = pi * 8 + tc * 2 + dh
                        if b == 0 and tc == 0 and dh == 0:
                            W(t, ("w_" + ("wq", "wk", "wv")[pi],))
                            W(t, ("w_" + ("bq", "bk", "bv")[pi],))
                        W(t, ("xtevacS", b, 7))
                        W(t, ("xtevacV", b, 7))
                        if n >= 2:
                            W(t, ("qkvev", b, n - 2))
                        elif b > 0:
                            sch.w(t, sVN, b * QTB)
                        sch.add(t, lambda e, brow=brow, dh=dh, n=n: e.matmul(
                            ps_pv[n % 2][:, :], ones[0:1, :], brow[0:1, dh * 512:dh * 512 + 512],
                            start=True, stop=False), None)
                        for fc in range(8):
                            tm = sch.add(t, lambda e, W_sb=W_sb, fc=fc, tc=tc, dh=dh, n=n: e.matmul(
                                ps_pv[n % 2][:, :],
                                xT[:, fc * 512 + tc * 128:fc * 512 + tc * 128 + 128],
                                W_sb[:, fc * 1024 + dh * 512:fc * 1024 + dh * 512 + 512],
                                start=False, stop=(fc == 7)), sTT if fc == 7 else None)
                        wp[("qkvmm", b, pi, tc, dh)] = (sTT, tm)
                        W(v, ("qkvmm", b, pi, tc, dh))
                        prevu = (b, pi, tc - 2) if tc >= 2 else (
                            (b, pi - 1, tc + 2) if pi > 0 else
                            ((b - 1, 2, tc + 2) if b > 0 else None))
                        if prevu is not None and dh == 0:
                            W(v, ("folddma",) + prevu)
                        ev = sch.add(v, lambda e, tc=tc, dh=dh, n=n: e.tensor_copy(
                            Qtm[:, (tc % 2) * 1024 + dh * 512:(tc % 2) * 1024 + dh * 512 + 512],
                            ps_pv[n % 2][:, :]), sVE)
                        wp[("qkvev", b, n)] = (sVE, ev)
                        wp[("qkvevac", b, pi, tc, dh)] = (sVE, ev)
                    W(s, ("qkvevac", b, pi, tc, 1))
                    if pi == 2:
                        pass
                    sch.add(s, lambda e, dst=dst, tc=tc: e.dma_start(
                        AP(dst, 16 * tc * 8192, [[8192, 16], [1024, 8], [1, 1024]]),
                        AP(Qtm, (tc % 2) * 1024, [[2048, 128], [1, 1024]])), sFoldP[tc % 2], 16)
                    if dup:
                        fd = sch.add(s, lambda e, dst=dst, tc=tc: e.dma_start(
                            AP(dst, (64 + 16 * tc) * 8192, [[8192, 16], [1024, 8], [1, 1024]]),
                            AP(Qtm, (tc % 2) * 1024, [[2048, 128], [1, 1024]])), sFoldP[tc % 2], 16)
                    else:
                        fd = sch.v(sFoldP[tc % 2])
                    wp[("folddma", b, pi, tc)] = (sFoldP[tc % 2], fd)
            mark(("qkv_done", b), sTT)
            wp[("qkvevac_last", b)] = (sVE, sch.v(sVE))
            wp[("folds", b)] = (sFoldP[0], sch.v(sFoldP[0]))
            wp[("folds1", b)] = (sFoldP[1], sch.v(sFoldP[1]))

            for cc in range(64):
                ch = cc % 2
                if cc < 2:
                    W(t, ("folds", b))
                    W(t, ("folds1", b))
                Wtr(t, ("trch", ch, "prev"))
                tm = sch.add(t, lambda e, cc=cc, ch=ch: e.transpose(
                    ps_tr[:, ch * 1024:ch * 1024 + 64],
                    Vs[0:64, cc * 128:cc * 128 + 128], eye[0:64, 0:64]), sTT)
                eng, sem = ((sc, sSC) if ch == 0 else (v, sVE))
                sch.w(eng, sTT, tm)
                ev = sch.add(eng, lambda e, cc=cc, ch=ch: (
                    e.copy(VsT[:, 65 * cc:65 * cc + 64],
                           ps_tr[:, ch * 1024:ch * 1024 + 64])
                    if ch == 0 else
                    e.tensor_copy(VsT[:, 65 * cc:65 * cc + 64],
                                  ps_tr[:, ch * 1024:ch * 1024 + 64])), sem)
                wp[("trch", ch, "prev")] = (sem, ev)
            wp[("vst", b)] = (sVE, sch.v(sVE))
            wp[("vstS", b)] = (sSC, sch.v(sSC))

            PVLAG = 2
            dstate = {}
            for qt in range(QTB):
                qtg = b * QTB + qt
                if deferred_ln and b > 0:
                    if qt == 2:
                        dstate["a2"] = deferred_ln[0][0]()
                    elif qt == 4:
                        deferred_ln[0][1](dstate["a2"])
                        dstate["a3"] = deferred_ln[1][0]()
                    elif qt == 6:
                        deferred_ln[1][1](dstate["a3"])
                        deferred_ln.clear()
                        bb = b - 1
                        sch.w(g, sX1W[0], wp[("x1w", bb, 2)][1])
                        sch.w(g, sX1W[1], wp[("x1w", bb, 3)][1])
                        sch.add(g, lambda e, bb=bb: e.dma_start(
                            AP(x1bfh, bb * 524288, [[1024, 512], [1, 1024]]),
                            AP(x1scrh, bb * 524288, [[1024, 512], [1, 1024]])), sCB[bb], 16)
                for gg in range(NGQ + PVLAG):
                    gi = gidx_of(b, qt, gg)
                    if gg < NGQ:
                        if qt == 0 and gg == 0:
                            sch.w(t, sVE, wp[("vst", b)][1])
                            sch.w(t, sSC, wp[("vstS", b)][1])
                            W(t, ("folds", b))
                            W(t, ("folds1", b))
                            if b > 0:
                                W(t, ("opev", b - 1, 7))
                        if gi >= 2:
                            sch.w(t, sEXP, gi - 1)
                        hh = (gg % 2) * 1024
                        cc = 2 * gg
                        sch.add(t, lambda e, cc=cc, qt=qt, hh=hh: e.matmul(
                            ps_sb[:, hh:hh + 512],
                            Kd[0:64, cc * 128:cc * 128 + 128],
                            Qd[0:64, qt * 512:qt * 512 + 512],
                            start=True, stop=True, tile_position=(0, 0)), None)
                        ts = sch.add(t, lambda e, cc=cc, qt=qt, hh=hh: e.matmul(
                            ps_sb[:, hh + 512:hh + 1024],
                            Kd[64:128, (cc + 1) * 128:(cc + 1) * 128 + 128],
                            Qd[64:128, qt * 512:qt * 512 + 512],
                            start=True, stop=True, tile_position=(64, 0)), sTS)
                        sch.w(sc, sTS, ts)
                        if gi >= 3:
                            sch.w(sc, sPVC, gi - 2)
                        ex = sch.add(sc, lambda e, gg=gg: e.activation(
                            PT[:, (gg % 4) * 1024:(gg % 4) * 1024 + 1024],
                            ps_sb[:, (gg % 2) * 1024:(gg % 2) * 1024 + 1024],
                            AF.Exp, scale=SOFT_SCALE), sEXP)
                        wp[("exp", gi)] = (sEXP, ex)
                    pg = gg - PVLAG
                    if pg >= 0:
                        pgi = gidx_of(b, qt, pg)
                        W(t, ("exp", pgi))
                        if pg == 0 and qtg >= 2:
                            sch.w(t, sVN, qtg - 1)
                        if pg == 0 and qt < 2:
                            W(t, ("qkvev", b, 23))
                        sch.add(t, lambda e, pg=pg, qtg=qtg: e.matmul(
                            ps_pv[qtg % 2][0:65, :],
                            VsT[:, (2 * pg) * 65:(2 * pg) * 65 + 65],
                            PT[:, (pg % 4) * 1024:(pg % 4) * 1024 + 512],
                            start=(pg == 0), stop=False), None)
                        sch.add(t, lambda e, pg=pg, qtg=qtg: e.matmul(
                            ps_pv[qtg % 2][0:65, :],
                            VsT[:, (2 * pg + 1) * 65:(2 * pg + 1) * 65 + 65],
                            PT[:, (pg % 4) * 1024 + 512:(pg % 4) * 1024 + 1024],
                            start=False, stop=(pg == NGQ - 1)), sPVC)
                sch.w(v, sPVC, NGQ * (qtg + 1))
                if qtg >= 1:
                    W(v, ("recwr", qtg - 1))
                rc = sch.add(v, lambda e, qtg=qtg: e.reciprocal(
                    rec_sb[0:1, :], ps_pv[qtg % 2][64:65, :]), sREC)
                sch.w(g, sREC, rc)
                wr = sch.add(g, lambda e: e.dma_start(
                    AP(recscrh, 0, [[512, 1], [1, 512]]), rec_sb[0:1, :]), sG, 16)
                wp[("recwr", qtg)] = (sG, wr)
                sch.w(g, sG, wr)
                if qtg >= 1:
                    sch.w(g, sVN, qtg)
                rd = sch.add(g, lambda e, qtg=qtg: e.dma_start(
                    rec_bc[0:64, 0:512],
                    AP(recscrh, 0, [[0, 64], [1, 512]])), sG, 16)
                sch.w(v, sG, rd)
                if qt == 0 and b > 0:
                    W(v, ("outproj", b - 1))
                vn = sch.add(v, lambda e, qtg=qtg, qt=qt: e.scalar_tensor_tensor(
                    A2[0:64, qt * 512:qt * 512 + 512], ps_pv[qtg % 2][0:64, :], 1.0,
                    rec_bc[0:64, 0:512],
                    ALU.mult, ALU.mult), sVN)
                sch.w(g, sVN, vn)
                ad = sch.add(g, lambda e, qt=qt: e.dma_start(
                    A2[64:128, qt * 512:qt * 512 + 512],
                    A2[0:64, qt * 512:qt * 512 + 512]), sG, 16)
                wp[("a2dup", qtg)] = (sG, ad)
                emit_qstep()
                emit_qstep()

            for tc in range(4):
                if b == NB - 1 and tc == 2:
                    sch.w(s, sPVC, NB * QTB * NGQ)
                    sch.w(s, sQS, sch.v(sQS))
                    sch.w(s, sG, wp[("a2dup", NB * QTB - 1)][1])
                    for dc in range(8):
                        sch.add(s, lambda e, dc=dc: e.dma_start_transpose(
                            WqT1[:, dc * 4096:dc * 4096 + 4096],
                            AP(wqt1h, dc * 128, [[1024, 4096], [1, 128]])), sWQ1, 16)
                    wp[("wqt1",)] = (sWQ1, sch.v(sWQ1))
                if b > 0 or tc >= 2:
                    bb, tt = (b, tc - 2) if tc >= 2 else (b - 1, tc + 2)
                    W(s, ("s1evac", bb, tt))
                ld = sch.add(s, lambda e, b=b, tc=tc: e.dma_start(
                    x_tc[:, (tc % 2) * 1024:(tc % 2) * 1024 + 1024],
                    AP(xh, (b * 512 + tc * 128) * 1024, [[1024, 128], [1, 1024]])),
                    sXL[tc % 2], 16)
                wp[("xload", b, tc)] = (sXL[tc % 2], ld)
                for dh in range(2):
                    n = tc * 2 + dh
                    u = (n % 2) * 2
                    if n == 0:
                        if b == 0:
                            W(t, ("w_wo",))
                            W(t, ("w_bo",))
                        sch.w(t, sVN, b * QTB + QTB)
                        sch.w(t, sG, wp[("a2dup", b * QTB + QTB - 1)][1])
                    if n >= 2:
                        W(t, ("opev", b, n - 2))
                    sch.add(t, lambda e, dh=dh, u=u: e.matmul(
                        ps_sb[:, u * 512:u * 512 + 512], ones[0:1, :],
                        bo_row[0:1, dh * 512:dh * 512 + 512],
                        start=True, stop=False, tile_position=(0, 0)), None)
                    for m in range(8):
                        for par in range(2):
                            r = 2 * m + par
                            lo = par * 64 * 8192
                            tm = sch.add(t, lambda e, r=r, m=m, tc=tc, dh=dh, lo=lo, par=par, u=u: e.matmul(
                                ps_sb[:, (u + par) * 512:(u + par) * 512 + 512],
                                AP(A2, lo + 2048 * tc + r, [[8192, 64], [16, 128]]),
                                AP(Wo_sb, lo + m * 1024 + dh * 512, [[8192, 64], [1, 512]]),
                                start=(par == 1 and m == 0), stop=(m == 7),
                                tile_position=(64 * par, 0)), sTT if r == 15 else None)
                    wp[("opmm", b, tc, dh)] = (sTT, tm)
                    W(v, ("opmm", b, tc, dh))
                    sch.w(v, wp[("xload", b, tc)][0], wp[("xload", b, tc)][1])
                    sch.add_sync(v, lambda e, tc=tc, dh=dh, u=u: e.scalar_tensor_tensor(
                        s1[:, (tc % 2) * 1024 + dh * 512:(tc % 2) * 1024 + dh * 512 + 512],
                        ps_sb[:, u * 512:u * 512 + 512], 1.0,
                        x_tc[:, (tc % 2) * 1024 + dh * 512:(tc % 2) * 1024 + dh * 512 + 512],
                        ALU.mult, ALU.add), sVE)
                    ev = sch.add_sync(v, lambda e, tc=tc, dh=dh, u=u: e.scalar_tensor_tensor(
                        s1[:, (tc % 2) * 1024 + dh * 512:(tc % 2) * 1024 + dh * 512 + 512],
                        ps_sb[:, (u + 1) * 512:(u + 1) * 512 + 512], 1.0,
                        s1[:, (tc % 2) * 1024 + dh * 512:(tc % 2) * 1024 + dh * 512 + 512],
                        ALU.mult, ALU.add), sVE)
                    wp[("opev", b, n)] = (sVE, ev)
                    wp[("s1evac", b, tc)] = (sVE, ev)
                if tc == 3:
                    mark(("outproj", b), sTT)
                def emit_ln1_a(b=b, tc=tc):
                    sl = (tc % 2) * 1024
                    if b > 0 or tc >= 2:
                        bb, tt = (b, tc - 2) if tc >= 2 else (b - 1, tc + 2)
                        W(v, ("x1w", bb, tt))
                    sch.add_sync(v, lambda e, sl=sl: e.reduce_sum(
                        negsum[:, :], s1[:, sl:sl + 1024], AX.X, negate=True), sVE)
                    sch.add_sync(v, lambda e: e.tensor_scalar_mul(
                        negmean[:, :], negsum[:, :], 1.0 / 1024.0), sVE)
                    sch.add_sync(v, lambda e, sl=sl: e.tensor_scalar(
                        s1[:, sl:sl + 1024], s1[:, sl:sl + 1024],
                        negmean[:, :], None, ALU.add), sVE)
                    sch.add_sync(v, lambda e, sl=sl: e.scalar_tensor_tensor(
                        x1[:, sl:sl + 1024], s1[:, sl:sl + 1024], 1.0,
                        s1[:, sl:sl + 1024], ALU.mult, ALU.mult,
                        accum_out=varsum[:, :]), sVE)
                    lv = sch.add(v, lambda e, tc=tc: e.tensor_scalar(
                        lnva[:, tc:tc + 1], varsum[:, :], 1.0 / 1024.0, LN_EPS,
                        ALU.mult, ALU.add), sVE)
                    return lv

                def emit_ln1_b(lv, b=b, tc=tc):
                    sl = (tc % 2) * 1024
                    sch.w(sc, sVE, lv)
                    sch.add_sync(sc, lambda e, tc=tc: e.activation(
                        uln[:, :], lnva[:, tc:tc + 1], AF.Ln), sSC)
                    sr = sch.add(sc, lambda e: e.activation(
                        rstd[:, :], uln[:, :], AF.Exp, scale=-0.5), sSC)
                    sch.w(v, sSC, sr)
                    x1v = sch.add(v, lambda e, sl=sl: e.tensor_scalar(
                        x1[:, sl:sl + 1024], s1[:, sl:sl + 1024],
                        rstd[:, :], None, ALU.mult), sVE)
                    gq = b == NB - 1 and tc < 2
                    weng = g if gq else s
                    xsem = sXW3 if gq else sX1W[tc % 2]
                    sch.w(weng, sVE, x1v)
                    xw = sch.add(weng, lambda e, b=b, tc=tc, sl=sl: e.dma_start(
                        AP(x1scrh, (b * 512 + tc * 128) * 1024, [[1024, 128], [1, 1024]]),
                        x1[:, sl:sl + 1024]), xsem, 16)
                    wp[("x1w", b, tc)] = (xsem, 32 if gq else xw)
                if tc < 2:
                    emit_ln1_b(emit_ln1_a())
                else:
                    deferred_ln.append((emit_ln1_a, emit_ln1_b))
                emit_qstep()
                emit_qstep()

        while qstate["i"] < len(qsteps):
            emit_qstep()

        ATT_T = sch.v(sTT); ATT_V = sch.v(sVE); ATT_G = sch.v(sG); ATT_S = sch.v(sS)


        def emit_x1t(b):
            sch.w(s, sCB[b], 16)
            if b == 0:
                W(s, ("outproj", NB - 1))
                W(s, ("s1evac", NB - 1, 3))
            if b > 0:
                W(s, ("w1done", b - 1))
            for dc in range(8):
                sch.add(s, lambda e, b=b, dc=dc: e.dma_start_transpose(
                    x1T[:, dc * 512:dc * 512 + 512],
                    AP(x1bfh, b * 524288 + dc * 128, [[1024, 512], [1, 128]])), sX1T, 16)
            wp[("x1T", b)] = (sX1T, sch.v(sX1T))

        for b in range(NB):
            if b == 0:
                emit_x1t(0)
            if b == 0:
                W(s, ("outproj", NB - 1))
                sch.w(s, sQS2, sch.v(sQS2))
                for oc in range(32):
                    sch.add(s, lambda e, oc=oc: e.dma_start_transpose(
                        WqT2[:, oc * 1024:oc * 1024 + 1024],
                        AP(wq2h, oc * 128, [[4096, 1024], [1, 128]])), sWQT, 16)
                wp[("wqt",)] = (sWQT, sch.v(sWQT))

            for oc in range(32):
                W(t, ("x1T", b))
                W(t, ("wqt1",))
                if oc >= 2:
                    W(t, ("hevac", b, oc - 2))
                elif b > 0:
                    W(t, ("hevac", b - 1, oc + 30))
                for dc in range(8):
                    tm = sch.add(t, lambda e, oc=oc, dc=dc: e.matmul(
                        ps_pv[oc % 2][:, :],
                        WqT1[:, dc * 4096 + oc * 128:dc * 4096 + oc * 128 + 128],
                        x1T[:, dc * 512:dc * 512 + 512],
                        start=(dc == 0), stop=(dc == 7)), sTT if dc == 7 else None)
                wp[("w1mm", b, oc)] = (sTT, tm)
                W(v, ("w1mm", b, oc))
                if b > 0 and oc < 2:
                    W(v, ("w2done", b - 1))
                if b == 0 and oc == 0:
                    sch.w(v, sWG[4], 36 * 16)
                ev = sch.add(v, lambda e, oc=oc: e.tensor_scalar(
                    hT[:, oc * 512:oc * 512 + 512], ps_pv[oc % 2][:, :],
                    b1c[:, oc:oc + 1], 0.0, ALU.add, ALU.max), sVE)
                wp[("hevac", b, oc)] = (sVE, ev)
            mark(("w1done", b), sTT)
            if b + 1 < NB:
                emit_x1t(b + 1)
            if b == 0:
                for fa, fb in deferred_ln:
                    fb(fa())
                deferred_ln.clear()
                bb = NB - 1
                sch.w(g, sX1W[0], wp[("x1w", bb, 2)][1])
                sch.w(g, sX1W[1], wp[("x1w", bb, 3)][1])
                sch.add(g, lambda e: e.dma_start(
                    AP(x1bfh, bb * 524288, [[1024, 512], [1, 1024]]),
                    AP(x1scrh, bb * 524288, [[1024, 512], [1, 1024]])), sCB[bb], 16)

            for tsub in range(4):
                if b > 0 or tsub >= 2:
                    bb, tt = (b, tsub - 2) if tsub >= 2 else (b - 1, tsub + 2)
                    W(s, ("s2done", bb, tt))
                else:
                    sch.w(s, sX1W[0], wp[("x1w", NB - 1, 2)][1])
                    sch.w(s, sX1W[1], wp[("x1w", NB - 1, 3)][1])
                ld = sch.add(s, lambda e, b=b, tsub=tsub: e.dma_start(
                    x1f[:, (tsub % 2) * 1024:(tsub % 2) * 1024 + 1024],
                    AP(x1scrh, (b * 512 + tsub * 128) * 1024, [[1024, 128], [1, 1024]])),
                    sX1F[tsub % 2], 16)
                wp[("x1fload", b, tsub)] = (sX1F[tsub % 2], ld)
                for dh in range(2):
                    W(t, ("hevac", b, 31))
                    if b == 0 and tsub == 0 and dh == 0:
                        W(t, ("wqt",))
                        W(t, ("wqt1",))
                    if b * 4 + tsub >= 2:
                        pb, pt = divmod(b * 4 + tsub - 2, 4)
                        W(t, ("s2evac", pb, pt))
                    sch.add(t, lambda e, tsub=tsub, dh=dh: e.matmul(
                        ps_sb[:, (tsub % 2) * 1024 + dh * 512:(tsub % 2) * 1024 + dh * 512 + 512],
                        ones[0:1, :], b2_row[0:1, dh * 512:dh * 512 + 512],
                        start=True, stop=False), None)
                    for oc in range(32):
                        tm = sch.add(t, lambda e, tsub=tsub, dh=dh, oc=oc: e.matmul(
                            ps_sb[:, (tsub % 2) * 1024 + dh * 512:(tsub % 2) * 1024 + dh * 512 + 512],
                            hT[:, oc * 512 + tsub * 128:oc * 512 + tsub * 128 + 128],
                            WqT2[:, oc * 1024 + dh * 512:oc * 1024 + dh * 512 + 512],
                            start=False, stop=(oc == 31)),
                            sTT if (dh == 1 and oc == 31) else None)
                wp[("w2mm", b, tsub)] = (sTT, tm)
                W(v, ("w2mm", b, tsub))
                sch.w(v, sX1F[tsub % 2], ld)
                if b * 4 + tsub >= 1:
                    pb, pt = divmod(b * 4 + tsub - 1, 4)
                    W(v, ("outdma", pb, pt))
                sl = 0
                xsl = (tsub % 2) * 1024
                if b == 0 and tsub == 0:
                    sch.w(v, sX1W[0], wp[("x1w", NB - 1, 2)][1])
                    sch.w(v, sX1W[1], wp[("x1w", NB - 1, 3)][1])
                ev = sch.add_sync(v, lambda e, tsub=tsub, xsl=xsl: e.scalar_tensor_tensor(
                    s2[:, 0:1024], ps_sb[:, (tsub % 2) * 1024:(tsub % 2) * 1024 + 1024], 1.0,
                    x1f[:, xsl:xsl + 1024], ALU.mult, ALU.add), sVE)
                wp[("s2evac", b, tsub)] = (sVE, ev)
                wp[("s2done", b, tsub)] = (sVE, ev)
                sch.add_sync(v, lambda e: e.reduce_sum(
                    negsum[:, :], s2[:, 0:1024], AX.X, negate=True), sVE)
                sch.add_sync(v, lambda e: e.tensor_scalar_mul(
                    negmean[:, :], negsum[:, :], 1.0 / 1024.0), sVE)
                sch.add_sync(v, lambda e: e.tensor_scalar(
                    s2[:, 0:1024], s2[:, 0:1024],
                    negmean[:, :], None, ALU.add), sVE)
                sch.add_sync(v, lambda e: e.scalar_tensor_tensor(
                    outb[:, 0:1024], s2[:, 0:1024], 1.0,
                    s2[:, 0:1024], ALU.mult, ALU.mult,
                    accum_out=varsum[:, :]), sVE)
                lv = sch.add(v, lambda e: e.tensor_scalar(
                    vareps[:, :], varsum[:, :], 1.0 / 1024.0, LN_EPS,
                    ALU.mult, ALU.add), sVE)
                sch.w(sc, sVE, lv)
                sch.add_sync(sc, lambda e: e.activation(uln[:, :], vareps[:, :], AF.Ln), sSC)
                sr = sch.add(sc, lambda e: e.activation(
                    rstd[:, :], uln[:, :], AF.Exp, scale=-0.5), sSC)
                sch.w(v, sSC, sr)
                ov = sch.add(v, lambda e: e.tensor_scalar(
                    outb[:, 0:1024], s2[:, 0:1024],
                    rstd[:, :], None, ALU.mult), sVE)
                wp[("outw", b, tsub)] = (sVE, ov)
                sch.w(s, sVE, ov)
                od = sch.add(s, lambda e, b=b, tsub=tsub: e.dma_start(
                    AP(yh, (b * 512 + tsub * 128) * 1024, [[1024, 128], [1, 1024]]),
                    outb[:, 0:1024]), sOUT, 16)
                wp[("outdma", b, tsub)] = (sOUT, od)
            mark(("w2done", b), sTT)

        sch.w(s, sOUT, sch.v(sOUT))
        sch.w(s, sX1T, sch.v(sX1T))
        for i in range(2):
            sch.w(s, sX1F[i], sch.v(sX1F[i]))
        sch.w(g, sG, sch.v(sG))
        for i in range(4):
            sch.w(g, sCB[i], 16)
        sch.w(g, sXW3, sch.v(sXW3))

        @blk.sync
        def _(e):
            for f in sch.L["sync"]:
                f(e)

        @blk.gpsimd
        def _(e):
            for f in sch.L["gpsimd"]:
                f(e)

        @blk.tensor
        def _(e):
            for f in sch.L["tensor"]:
                f(e)

        @blk.vector
        def _(e):
            for f in sch.L["vector"]:
                f(e)

        @blk.scalar
        def _(e):
            for f in sch.L["scalar"]:
                f(e)

    return nc


def kernel(L0_attn_x, L0_attn_Wq, L0_attn_bq, L0_attn_Wk, L0_attn_bk,
           L0_attn_Wv, L0_attn_bv, L0_attn_Wo, L0_attn_bo,
           L0_ffn_W1, L0_ffn_b1, L0_ffn_W2, L0_ffn_b2):
    x = np.ascontiguousarray(np.asarray(L0_attn_x, dtype=np.float32)).reshape(16384, 1024)
    com = dict(
        Wq=np.ascontiguousarray(np.asarray(L0_attn_Wq, np.float32)),
        bq=np.ascontiguousarray(np.asarray(L0_attn_bq, np.float32)),
        Wk=np.ascontiguousarray(np.asarray(L0_attn_Wk, np.float32)),
        bk=np.ascontiguousarray(np.asarray(L0_attn_bk, np.float32)),
        Wv=np.ascontiguousarray(np.asarray(L0_attn_Wv, np.float32)),
        bv=np.ascontiguousarray(np.asarray(L0_attn_bv, np.float32)),
        Wo=np.ascontiguousarray(np.asarray(L0_attn_Wo, np.float32)),
        bo=np.ascontiguousarray(np.asarray(L0_attn_bo, np.float32)),
        W1=np.ascontiguousarray(np.asarray(L0_ffn_W1, np.float32)),
        b1=np.ascontiguousarray(np.asarray(L0_ffn_b1, np.float32)),
        W2=np.ascontiguousarray(np.asarray(L0_ffn_W2, np.float32)),
        b2=np.ascontiguousarray(np.asarray(L0_ffn_b2, np.float32)),
        eye=np.eye(128, dtype=np.float32),
    )
    nc = build_nc()
    in_maps = [dict(com, x=np.ascontiguousarray(x[c * 2048:(c + 1) * 2048]))
               for c in range(8)]
    res = run_bass_kernel_spmd(nc, in_maps, core_ids=list(range(8)))
    out = np.concatenate([np.asarray(r["y"], np.float32) for r in res.results], axis=0)
    return out.reshape(2, 8192, 1024)


# version 14 (final)
# speedup vs baseline: 1.2831x; 1.0326x over previous
import numpy as np
import concourse.bass as bass
import concourse.mybir as mybir
from concourse.bass_utils import run_bass_kernel_spmd

F32 = mybir.dt.float32
BF16 = mybir.dt.bfloat16
AF = mybir.ActivationFunctionType
ALU = mybir.AluOpType
AX = mybir.AxisListType
AP = bass.AP

NB = 4
QTB = 16
NGQ = 32
SOFT_SCALE = float(2.0 ** -10)
LN_EPS = 1e-5
QN = 4096 * 1024


class Sch:

    def __init__(self):
        self.L = {k: [] for k in ("sync", "gpsimd", "tensor", "vector", "scalar")}
        self.c = {}

    def w(self, eng, sem, v):
        if v > 0:
            self.L[eng].append(lambda e, s=sem, vv=v: e.wait_ge(s, vv))

    def add(self, eng, f, inc=None, n=1):
        if inc is None:
            self.L[eng].append(f)
            return None
        tot = self.c[id(inc)] = self.c.get(id(inc), 0) + n
        self.L[eng].append(lambda e, ff=f, s=inc, nn=n: ff(e).then_inc(s, nn))
        return tot

    def v(self, sem):
        return self.c.get(id(sem), 0)

    def add_sync(self, eng, f, inc):
        tot = self.add(eng, f, inc)
        self.w(eng, inc, tot)
        return tot


def build_nc():
    nc = bass.Bass("TRN2", target_bir_lowering=False)

    xh = nc.dram_tensor("x", [2048, 1024], F32, kind="ExternalInput")
    wqh = nc.dram_tensor("Wq", [1024, 1024], F32, kind="ExternalInput")
    bqh = nc.dram_tensor("bq", [1024], F32, kind="ExternalInput")
    wkh = nc.dram_tensor("Wk", [1024, 1024], F32, kind="ExternalInput")
    bkh = nc.dram_tensor("bk", [1024], F32, kind="ExternalInput")
    wvh = nc.dram_tensor("Wv", [1024, 1024], F32, kind="ExternalInput")
    bvh = nc.dram_tensor("bv", [1024], F32, kind="ExternalInput")
    woh = nc.dram_tensor("Wo", [1024, 1024], F32, kind="ExternalInput")
    boh = nc.dram_tensor("bo", [1024], F32, kind="ExternalInput")
    w1h = nc.dram_tensor("W1", [4096, 1024], F32, kind="ExternalInput")
    b1h = nc.dram_tensor("b1", [4096], F32, kind="ExternalInput")
    w2h = nc.dram_tensor("W2", [1024, 4096], F32, kind="ExternalInput")
    b2h = nc.dram_tensor("b2", [1024], F32, kind="ExternalInput")
    eyeh = nc.dram_tensor("eye", [128, 128], F32, kind="ExternalInput")
    yh = nc.dram_tensor("y", [2048, 1024], F32, kind="ExternalOutput")

    wqt1h = nc.dram_tensor("wqt1", [4096, 1024], BF16, kind="Internal")
    wq2h = nc.dram_tensor("wq2", [1024, 4096], BF16, kind="Internal")
    x1scrh = nc.dram_tensor("x1scr", [2048, 1024], F32, kind="Internal")
    x1bfh = nc.dram_tensor("x1bfscr", [2048, 1024], BF16, kind="Internal")
    recscrh = nc.dram_tensor("recscr", [512], F32, kind="Internal")
    gscrh = nc.dram_tensor("gscr", [256], F32, kind="Internal")
    thrscrh = nc.dram_tensor("thrscr", [8], F32, kind="Internal")

    BASE = 16512
    off = [BASE]

    def alloc(name, shape, dtype, at=None):
        if at is None:
            at = off[0]
            nbytes = shape[-1] * (2 if dtype == BF16 else 4)
            for d in shape[1:-1]:
                nbytes *= d
            off[0] = at + ((nbytes + 31) // 32) * 32
        assert off[0] <= 229344, f"sbuf overflow at {name}: {off[0]}"
        return nc.alloc_sbuf_tensor_at(name, shape, dtype, offset=at)

    eye = alloc("eye", [128, 128], BF16)
    ones = alloc("ones", [1, 128], BF16)
    ones512 = alloc("ones512", [1, 512], BF16)
    bq_row = alloc("bq_row", [1, 1024], BF16)
    bk_row = alloc("bk_row", [1, 1024], BF16)
    bv_row = alloc("bv_row", [1, 1024], BF16)
    bo_row = alloc("bo_row", [1, 1024], BF16)
    b2_row = alloc("b2_row", [1, 1024], BF16)
    thrbc = alloc("thrbc", [128, 8], F32)
    gv = alloc("gv", [1, 16], F32)
    acc1 = alloc("acc1", [128, 1], F32)
    acc2 = alloc("acc2", [128, 1], F32)
    gtmp = alloc("gtmp", [128, 1], F32)
    negsum = alloc("negsum", [128, 1], F32)
    negmean = alloc("negmean", [128, 1], F32)
    varsum = alloc("varsum", [128, 1], F32)
    vareps = alloc("vareps", [128, 1], F32)
    uln = alloc("uln", [128, 1], F32)
    rstd = alloc("rstd", [128, 1], F32)
    warm = alloc("warm", [1, 256], BF16)
    b1c = alloc("b1c", [128, 32], F32)
    lnva = alloc("lnva", [128, 4], F32)

    A0 = off[0]

    Qd = alloc("Qd", [128, 8192], BF16)
    Kd = alloc("Kd", [128, 8192], BF16)
    Vs = alloc("Vs", [64, 8192], BF16)
    VsT = alloc("VsT", [128, 4160], BF16)
    PT = alloc("PT", [128, 4096], BF16)
    Wq_sb = alloc("Wq_sb", [128, 8192], BF16)
    Wk_sb = alloc("Wk_sb", [128, 8192], BF16)
    Wv_sb = alloc("Wv_sb", [128, 8192], BF16)
    Wo_sb = alloc("Wo_sb", [128, 8192], BF16)
    A2 = alloc("A2", [128, 8192], BF16)
    xT = alloc("xT", [128, 4096], BF16)
    x_tc = alloc("x_tc", [128, 2048], F32)
    x_bf = alloc("x_bf", [128, 1024], BF16)
    Qtm = alloc("Qtm", [128, 2048], BF16)
    rec_sb = alloc("rec_sb", [1, 512], F32)
    rec_bc = alloc("rec_bc", [64, 512], F32)
    s1 = alloc("s1", [128, 2048], F32)
    x1 = alloc("x1", [128, 2048], F32)
    ATTN_END = off[0]

    off[0] = A0
    WqT1 = alloc("WqT1", [128, 32768], BF16)
    WqT2 = alloc("WqT2", [128, 32768], BF16)
    hT = alloc("hT", [128, 16384], BF16)
    x1T = alloc("x1T", [128, 4096], BF16)
    x1f = alloc("x1f", [128, 2048], F32)
    s2 = alloc("s2", [128, 1024], F32)
    outb = alloc("outb", [128, 1024], F32)
    FFN_END = off[0]
    off[0] = ATTN_END
    grow = alloc("grow", [1, 256], F32)
    Wch = alloc("Wch", [128, 1024], F32)
    tq1 = alloc("tq1", [128, 512], BF16)
    tq2 = alloc("tq2", [128, 512], BF16)
    qout = alloc("qout", [128, 512], BF16)
    QNT_END = off[0]
    assert QNT_END <= 229344 and FFN_END <= 229344, (ATTN_END, FFN_END, QNT_END)
    off[0] = max(ATTN_END, FFN_END, QNT_END)

    sch = Sch()

    from contextlib import ExitStack
    with ExitStack() as es:
        E = es.enter_context
        ps_sb = E(nc.psum_tensor("ps_sb", [128, 2048], F32))
        ps_pv = [E(nc.psum_tensor(f"ps_pv{i}", [128, 512], F32)) for i in range(2)]
        ps_tr = E(nc.psum_tensor("ps_tr", [128, 2048], BF16))

        sS = E(nc.semaphore("sS"))
        sG = E(nc.semaphore("sG"))
        sQS = E(nc.semaphore("sQS"))
        sQS2 = E(nc.semaphore("sQS2"))
        sFoldP = [E(nc.semaphore(f"sFold{i}")) for i in range(2)]
        sFIN = E(nc.semaphore("sFIN"))
        sXL = [E(nc.semaphore(f"sXL{i}")) for i in range(2)]
        sX1W = [E(nc.semaphore(f"sX1W{i}")) for i in range(2)]
        sXBF = [E(nc.semaphore(f"sXBF{i}")) for i in range(2)]
        sWQT = E(nc.semaphore("sWQT"))
        sX1T = E(nc.semaphore("sX1T"))
        sX1F = [E(nc.semaphore(f"sX1F{i}")) for i in range(2)]
        sOUT = E(nc.semaphore("sOUT"))
        sXB1 = E(nc.semaphore("sXB1"))
        sCB = [E(nc.semaphore(f"sCB{i}")) for i in range(4)]
        sXW3 = E(nc.semaphore("sXW3"))
        sTS = E(nc.semaphore("sTS"))
        sEXP = E(nc.semaphore("sEXP"))
        sPVC = E(nc.semaphore("sPVC"))
        sREC = E(nc.semaphore("sREC"))
        sVN = E(nc.semaphore("sVN"))
        sVE = E(nc.semaphore("sVE"))
        sTT = E(nc.semaphore("sTT"))
        sSC = E(nc.semaphore("sSC"))
        sWG = [E(nc.semaphore(f"sWG{i}")) for i in range(5)]
        sWQ1 = E(nc.semaphore("sWQ1"))
        blk = E(nc.Block())

        wp = {}

        def W(eng, key):
            if key in wp:
                sem, v = wp[key]
                sch.w(eng, sem, v)

        def mark(key, sem):
            wp[key] = (sem, sch.v(sem))

        g = "gpsimd"; s = "sync"; t = "tensor"; v = "vector"; sc = "scalar"

        sch.add(g, lambda e: e.dma_start(eye[:, :], eyeh.ap()[:, :]), sWG[0], 16)
        wp[("w_eye",)] = (sWG[0], 16)
        for i, (nm, W_sb, wh, row, bh) in enumerate((
                ("q", Wq_sb, wqh, bq_row, bqh),
                ("k", Wk_sb, wkh, bk_row, bkh),
                ("v", Wv_sb, wvh, bv_row, bvh))):
            sch.add(g, lambda e, d=W_sb, sr=wh: e.dma_start(
                AP(d, 0, [[8192, 128], [1024, 8], [1, 1024]]),
                AP(sr, 0, [[1024, 128], [131072, 8], [1, 1024]])), sWG[1 + i], 16)
            sch.add(g, lambda e, d=row, sr=bh: e.dma_start(
                AP(d, 0, [[1024, 1], [1, 1024]]),
                AP(sr, 0, [[1024, 1], [1, 1024]])), sWG[1 + i], 16)
            wp[("w_w" + nm,)] = wp[("w_b" + nm,)] = (sWG[1 + i], 32)
        sch.add(g, lambda e: e.dma_start(
            AP(Wo_sb, 0, [[8192, 64], [1024, 8], [1, 1024]]),
            AP(woh, 0, [[1024, 64], [131072, 8], [1, 1024]])), sWG[4], 16)
        sch.add(g, lambda e: e.dma_start(
            AP(Wo_sb, 64 * 8192, [[8192, 64], [1024, 8], [1, 1024]]),
            AP(woh, 65536, [[1024, 64], [131072, 8], [1, 1024]])), sWG[4], 16)
        for row, bh in ((bo_row, boh), (b2_row, b2h)):
            sch.add(g, lambda e, d=row, sr=bh: e.dma_start(
                AP(d, 0, [[1024, 1], [1, 1024]]),
                AP(sr, 0, [[1024, 1], [1, 1024]])), sWG[4], 16)
        for oc in range(32):
            sch.add(g, lambda e, oc=oc: e.dma_start(
                b1c[:, oc:oc + 1],
                AP(b1h, oc * 128, [[1, 128], [1, 1]])), sWG[4], 16)
        wp[("w_wo",)] = wp[("w_bo",)] = wp[("w_b2",)] = wp[("w_b1c",)] = (sWG[4], 36 * 16)
        mark("prologue_g", sG)

        sch.add(v, lambda e: e.memset(ones[0:1, :], 1.0), sVE)
        sch.add(v, lambda e: e.memset(ones512[0:1, :], 1.0), sVE)
        sch.add(v, lambda e: e.memset(VsT[:, :], 1.0), sVE)
        sch.add(v, lambda e: e.memset(acc1[:, :], 0.0), sVE)
        sch.add(v, lambda e: e.memset(acc2[:, :], 0.0), sVE)
        mark("prologue_v", sVE)

        W(sc, "prologue_v")
        sch.add(sc, lambda e: e.activation(warm[0:1, 0:128], ones[0:1, :], AF.Exp), sSC)
        sch.add(sc, lambda e: e.activation(warm[0:1, 128:256], ones[0:1, :], AF.Ln), sSC)

        qsteps = []

        def gamma_step(widx, ch):
            def fs(e, widx=widx, ch=ch):
                if widx == 0:
                    return e.dma_start(Wch[:, :], AP(w1h, ch * 131072,
                                                    [[1024, 128], [1, 1024]]))
                a, cg = ch % 8, ch // 8
                return e.dma_start(Wch[:, :], AP(w2h, a * 524288 + cg * 1024,
                                                 [[4096, 128], [1, 1024]]))
            def fv(ops, widx=widx, ch=ch):
                acc = acc1 if widx == 0 else acc2
                ops.append((v, "sync", lambda e: e.reduce_sum(gtmp[:, :], Wch[:, :], AX.X,
                                                      apply_absolute_value=True), sVE))
                ops.append((v, "plain", lambda e, a=acc: e.tensor_add(a[:, :], a[:, :], gtmp[:, :]), sVE))
            return ("gamma", fs, fv)

        def finalize_step():
            return ("finalize", None, None)

        def quant_step(widx, ch):
            def fs(e, widx=widx, ch=ch):
                if widx == 0:
                    return e.dma_start(Wch[:, :], AP(w1h, ch * 131072,
                                                    [[1024, 128], [1, 1024]]))
                a, cg = ch % 8, ch // 8
                return e.dma_start(Wch[:, :], AP(w2h, a * 524288 + cg * 1024,
                                                 [[4096, 128], [1, 1024]]))
            def fo(e, hh, widx=widx, ch=ch):
                if widx == 0:
                    return e.dma_start(AP(wqt1h, ch * 131072 + hh * 512,
                                          [[1024, 128], [1, 512]]), qout[:, :])
                a, cg = ch % 8, ch // 8
                return e.dma_start(AP(wq2h, a * 524288 + cg * 1024 + hh * 512,
                                      [[4096, 128], [1, 512]]), qout[:, :])
            return ("quant", fs, fo, widx)

        for widx in (0, 1):
            for ch in range(32):
                qsteps.append(gamma_step(widx, ch))
        qsteps.append(finalize_step())
        for widx in (0, 1):
            for ch in range(32):
                qsteps.append(quant_step(widx, ch))
        qstate = {"i": 0, "thr_ready": None, "last_q_dma": None, "last_qv": None}

        def emit_qstep():
            i = qstate["i"]
            if i >= len(qsteps):
                return
            qstate["i"] = i + 1
            st = qsteps[i]
            if st[0] == "gamma":
                if qstate["last_qv"]:
                    W(s, qstate["last_qv"])
                ld = sch.add(s, st[1], sS, 16)
                sch.w(v, sS, ld)
                ops = []
                st[2](ops)
                for eng, kind, f, sem in ops:
                    (sch.add_sync if kind == "sync" else sch.add)(eng, f, sem)
                mark(f"qv_{i}", sVE)
                qstate["last_qv"] = f"qv_{i}" 
            elif st[0] == "finalize":
                sch.w(g, sVE, sch.v(sVE))
                sch.add(g, lambda e: e.dma_start(AP(gscrh, 0, [[1, 128], [1, 1]]),
                                                 acc1[:, :]), sFIN, 16)
                sch.add(g, lambda e: e.dma_start(AP(gscrh, 128, [[1, 128], [1, 1]]),
                                                 acc2[:, :]), sFIN, 16)
                sch.w(g, sFIN, sch.v(sFIN))
                gld = sch.add(g, lambda e: e.dma_start(
                    grow[0:1, :], AP(gscrh, 0, [[256, 1], [1, 256]])), sFIN, 16)
                sch.w(v, sFIN, gld)
                sch.add(v, lambda e: e.reduce_sum(gv[0:1, 0:1], grow[0:1, 0:128], AX.X), sVE)
                sch.add_sync(v, lambda e: e.reduce_sum(gv[0:1, 1:2], grow[0:1, 128:256], AX.X), sVE)
                c1, c2 = 0.5 / QN, 0.5 * 1e-5
                sch.add(v, lambda e: e.tensor_scalar(gv[0:1, 4:5], gv[0:1, 0:1], c1, c2, ALU.mult, ALU.add), sVE)
                sch.add(v, lambda e: e.tensor_scalar(gv[0:1, 5:6], gv[0:1, 0:1], -c1, -c2, ALU.mult, ALU.add), sVE)
                sch.add(v, lambda e: e.tensor_scalar(gv[0:1, 6:7], gv[0:1, 1:2], c1, c2, ALU.mult, ALU.add), sVE)
                thr_v = sch.add(v, lambda e: e.tensor_scalar(gv[0:1, 7:8], gv[0:1, 1:2], -c1, -c2, ALU.mult, ALU.add), sVE)
                sch.w(g, sVE, thr_v)
                sch.add(g, lambda e: e.dma_start(AP(thrscrh, 0, [[8, 1], [1, 4]]),
                                                 gv[0:1, 4:8]), sFIN, 16)
                sch.w(g, sFIN, sch.v(sFIN))
                thr_g = sch.add(g, lambda e: e.dma_start(
                    thrbc[:, 0:4], AP(thrscrh, 0, [[0, 128], [1, 4]])), sFIN, 16)
                qstate["thr_ready"] = thr_g
            else:
                widx = st[3]
                if qstate["last_qv"]:
                    W(s, qstate["last_qv"])
                ld = sch.add(s, st[1], sS, 16)
                sch.w(v, sS, ld)
                if qstate["thr_ready"] is not None:
                    sch.w(v, sFIN, qstate["thr_ready"])
                    qstate["thr_ready"] = None
                cp, cn = 2 * widx, 2 * widx + 1
                for hh in range(2):
                    if qstate["last_q_dma"] is not None:
                        sch.w(v, *qstate["last_q_dma"])
                    sch.add(v, lambda e, cp=cp, hh=hh: e.tensor_scalar(
                        tq1[:, :], Wch[:, hh * 512:hh * 512 + 512],
                        thrbc[:, cp:cp + 1], None, ALU.is_gt), sVE)
                    sch.add_sync(v, lambda e, cn=cn, hh=hh: e.tensor_scalar(
                        tq2[:, :], Wch[:, hh * 512:hh * 512 + 512],
                        thrbc[:, cn:cn + 1], None, ALU.is_lt), sVE)
                    qv = sch.add_sync(v, lambda e: e.tensor_sub(
                        qout[:, :], tq1[:, :], tq2[:, :]), sVE)
                    sch.w(s, sVE, qv)
                    qsem = sQS if widx == 0 else sQS2
                    qstate["last_q_dma"] = (qsem, sch.add(
                        s, lambda e, st=st, hh=hh: st[2](e, hh), qsem, 16))
                mark(f"qv_{i}", sVE)
                qstate["last_qv"] = f"qv_{i}"

        deferred_ln = []
        gidx_of = lambda b, qt, gg: (b * QTB + qt) * NGQ + gg

        for b in range(NB):
            for dc2 in range(2):
                if b > 0:
                    W(g, ("xtr", b - 1, 6 + dc2))
                ld = sch.add(g, lambda e, b=b, dc=dc2: e.dma_start(
                    AP(x_bf, (dc % 2) * 512, [[1024, 128], [128, 4], [1, 128]]),
                    AP(xh, b * 524288 + dc * 128, [[1024, 128], [131072, 4], [1, 128]])),
                    sXBF[dc2 % 2], 16)
                wp[("xbf", b, dc2)] = (sXBF[dc2 % 2], ld)

            trn = [0]
            wp.setdefault(("trch", 0, "prev"), None)
            wp.setdefault(("trch", 1, "prev"), None)

            def Wtr(eng, key):
                val = wp.get(key)
                if val:
                    sch.w(eng, val[0], val[1])

            for dc in range(8):
                for tc4 in range(4):
                    fw = []
                    if b == 0 and dc == 0 and tc4 == 0:
                        fw = [("w_eye",), "prologue_v"]
                    i = trn[0]; ch = i % 2
                    W(t, ("xbf", b, dc))
                    for wkey in fw:
                        W(t, wkey)
                    if b > 0 and dc == 0 and tc4 == 0:
                        W(t, ("qkv_done", b - 1))
                    Wtr(t, ("trch", ch, "prev"))
                    tm = sch.add(t, lambda e, dc=dc, tc=tc4, ch=ch: e.transpose(
                        ps_tr[:, ch * 1024:ch * 1024 + 128],
                        x_bf[:, (dc % 2) * 512 + tc * 128:(dc % 2) * 512 + tc * 128 + 128],
                        eye[:, :]), sTT)
                    trn[0] += 1
                    eng, sem = ((sc, sSC) if ch == 0 else (v, sVE))
                    sch.w(eng, sTT, tm)
                    ev = sch.add(eng, lambda e, dc=dc, tc=tc4, ch=ch: (
                        e.copy(xT[:, dc * 512 + tc * 128:dc * 512 + tc * 128 + 128],
                               ps_tr[:, ch * 1024:ch * 1024 + 128])
                        if ch == 0 else
                        e.tensor_copy(xT[:, dc * 512 + tc * 128:dc * 512 + tc * 128 + 128],
                                      ps_tr[:, ch * 1024:ch * 1024 + 128])), sem)
                    wp[("trch", ch, "prev")] = (sem, ev)
                wp[("xtr", b, dc)] = (sTT, sch.v(sTT))
                wp[("xtevacS", b, dc)] = (sSC, sch.v(sSC))
                wp[("xtevacV", b, dc)] = (sVE, sch.v(sVE))
                if dc + 2 < 8:
                    sch.w(g, sTT, sch.v(sTT))
                    ld = sch.add(g, lambda e, b=b, dc=dc + 2: e.dma_start(
                        AP(x_bf, (dc % 2) * 512, [[1024, 128], [128, 4], [1, 128]]),
                        AP(xh, b * 524288 + dc * 128, [[1024, 128], [131072, 4], [1, 128]])),
                        sXBF[dc % 2], 16)
                    wp[("xbf", b, dc + 2)] = (sXBF[dc % 2], ld)

            for pi, (W_sb, brow, dst, dup) in enumerate((
                    (Wq_sb, bq_row, Qd, True), (Wk_sb, bk_row, Kd, True),
                    (Wv_sb, bv_row, Vs, False))):
                for tc in range(4):
                    for dh in range(2):
                        n = pi * 8 + tc * 2 + dh
                        if b == 0 and tc == 0 and dh == 0:
                            W(t, ("w_" + ("wq", "wk", "wv")[pi],))
                            W(t, ("w_" + ("bq", "bk", "bv")[pi],))
                        W(t, ("xtevacS", b, 7))
                        W(t, ("xtevacV", b, 7))
                        if n >= 2:
                            W(t, ("qkvev", b, n - 2))
                        elif b > 0:
                            sch.w(t, sVN, b * QTB)
                        sch.add(t, lambda e, brow=brow, dh=dh, n=n: e.matmul(
                            ps_pv[n % 2][:, :], ones[0:1, :], brow[0:1, dh * 512:dh * 512 + 512],
                            start=True, stop=False), None)
                        for fc in range(8):
                            tm = sch.add(t, lambda e, W_sb=W_sb, fc=fc, tc=tc, dh=dh, n=n: e.matmul(
                                ps_pv[n % 2][:, :],
                                xT[:, fc * 512 + tc * 128:fc * 512 + tc * 128 + 128],
                                W_sb[:, fc * 1024 + dh * 512:fc * 1024 + dh * 512 + 512],
                                start=False, stop=(fc == 7)), sTT if fc == 7 else None)
                        wp[("qkvmm", b, pi, tc, dh)] = (sTT, tm)
                        W(v, ("qkvmm", b, pi, tc, dh))
                        prevu = (b, pi, tc - 2) if tc >= 2 else (
                            (b, pi - 1, tc + 2) if pi > 0 else
                            ((b - 1, 2, tc + 2) if b > 0 else None))
                        if prevu is not None and dh == 0:
                            W(v, ("folddma",) + prevu)
                        ev = sch.add(v, lambda e, tc=tc, dh=dh, n=n: e.tensor_copy(
                            Qtm[:, (tc % 2) * 1024 + dh * 512:(tc % 2) * 1024 + dh * 512 + 512],
                            ps_pv[n % 2][:, :]), sVE)
                        wp[("qkvev", b, n)] = (sVE, ev)
                        wp[("qkvevac", b, pi, tc, dh)] = (sVE, ev)
                    W(s, ("qkvevac", b, pi, tc, 1))
                    if pi == 2:
                        pass
                    sch.add(s, lambda e, dst=dst, tc=tc: e.dma_start(
                        AP(dst, 16 * tc * 8192, [[8192, 16], [1024, 8], [1, 1024]]),
                        AP(Qtm, (tc % 2) * 1024, [[2048, 128], [1, 1024]])), sFoldP[tc % 2], 16)
                    if dup:
                        fd = sch.add(s, lambda e, dst=dst, tc=tc: e.dma_start(
                            AP(dst, (64 + 16 * tc) * 8192, [[8192, 16], [1024, 8], [1, 1024]]),
                            AP(Qtm, (tc % 2) * 1024, [[2048, 128], [1, 1024]])), sFoldP[tc % 2], 16)
                    else:
                        fd = sch.v(sFoldP[tc % 2])
                    wp[("folddma", b, pi, tc)] = (sFoldP[tc % 2], fd)
            mark(("qkv_done", b), sTT)
            wp[("qkvevac_last", b)] = (sVE, sch.v(sVE))
            wp[("folds", b)] = (sFoldP[0], sch.v(sFoldP[0]))
            wp[("folds1", b)] = (sFoldP[1], sch.v(sFoldP[1]))

            for cc in range(64):
                ch = cc % 2
                if cc < 2:
                    W(t, ("folds", b))
                    W(t, ("folds1", b))
                Wtr(t, ("trch", ch, "prev"))
                tm = sch.add(t, lambda e, cc=cc, ch=ch: e.transpose(
                    ps_tr[:, ch * 1024:ch * 1024 + 64],
                    Vs[0:64, cc * 128:cc * 128 + 128], eye[0:64, 0:64]), sTT)
                eng, sem = ((sc, sSC) if ch == 0 else (v, sVE))
                sch.w(eng, sTT, tm)
                ev = sch.add(eng, lambda e, cc=cc, ch=ch: (
                    e.copy(VsT[:, 65 * cc:65 * cc + 64],
                           ps_tr[:, ch * 1024:ch * 1024 + 64])
                    if ch == 0 else
                    e.tensor_copy(VsT[:, 65 * cc:65 * cc + 64],
                                  ps_tr[:, ch * 1024:ch * 1024 + 64])), sem)
                wp[("trch", ch, "prev")] = (sem, ev)
            wp[("vst", b)] = (sVE, sch.v(sVE))
            wp[("vstS", b)] = (sSC, sch.v(sSC))

            PVLAG = 2
            dstate = {}
            for qt in range(QTB):
                qtg = b * QTB + qt
                if deferred_ln and b > 0:
                    if qt == 2:
                        dstate["a2"] = deferred_ln[0][0]()
                    elif qt == 4:
                        deferred_ln[0][1](dstate["a2"])
                        dstate["a3"] = deferred_ln[1][0]()
                    elif qt == 6:
                        deferred_ln[1][1](dstate["a3"])
                        deferred_ln.clear()
                        bb = b - 1
                        sch.w(g, sX1W[0], wp[("x1w", bb, 2)][1])
                        sch.w(g, sX1W[1], wp[("x1w", bb, 3)][1])
                        sch.add(g, lambda e, bb=bb: e.dma_start(
                            AP(x1bfh, bb * 524288, [[1024, 512], [1, 1024]]),
                            AP(x1scrh, bb * 524288, [[1024, 512], [1, 1024]])), sCB[bb], 16)
                for gg in range(NGQ + PVLAG):
                    gi = gidx_of(b, qt, gg)
                    if gg < NGQ:
                        if qt == 0 and gg == 0:
                            sch.w(t, sVE, wp[("vst", b)][1])
                            sch.w(t, sSC, wp[("vstS", b)][1])
                            W(t, ("folds", b))
                            W(t, ("folds1", b))
                            if b > 0:
                                W(t, ("opev", b - 1, 7))
                        if gi >= 2:
                            sch.w(t, sEXP, gi - 1)
                        hh = (gg % 2) * 1024
                        cc = 2 * gg
                        sch.add(t, lambda e, cc=cc, qt=qt, hh=hh: e.matmul(
                            ps_sb[:, hh:hh + 512],
                            Kd[0:64, cc * 128:cc * 128 + 128],
                            Qd[0:64, qt * 512:qt * 512 + 512],
                            start=True, stop=True, tile_position=(0, 0)), None)
                        ts = sch.add(t, lambda e, cc=cc, qt=qt, hh=hh: e.matmul(
                            ps_sb[:, hh + 512:hh + 1024],
                            Kd[64:128, (cc + 1) * 128:(cc + 1) * 128 + 128],
                            Qd[64:128, qt * 512:qt * 512 + 512],
                            start=True, stop=True, tile_position=(64, 0)), sTS)
                        sch.w(sc, sTS, ts)
                        if gi >= 3:
                            sch.w(sc, sPVC, gi - 2)
                        ex = sch.add(sc, lambda e, gg=gg: e.activation(
                            PT[:, (gg % 4) * 1024:(gg % 4) * 1024 + 1024],
                            ps_sb[:, (gg % 2) * 1024:(gg % 2) * 1024 + 1024],
                            AF.Exp, scale=SOFT_SCALE), sEXP)
                        wp[("exp", gi)] = (sEXP, ex)
                    pg = gg - PVLAG
                    if pg >= 0:
                        pgi = gidx_of(b, qt, pg)
                        W(t, ("exp", pgi))
                        if pg == 0 and qtg >= 2:
                            sch.w(t, sVN, qtg - 1)
                        if pg == 0 and qt < 2:
                            W(t, ("qkvev", b, 23))
                        sch.add(t, lambda e, pg=pg, qtg=qtg: e.matmul(
                            ps_pv[qtg % 2][0:65, :],
                            VsT[:, (2 * pg) * 65:(2 * pg) * 65 + 65],
                            PT[:, (pg % 4) * 1024:(pg % 4) * 1024 + 512],
                            start=(pg == 0), stop=False), None)
                        sch.add(t, lambda e, pg=pg, qtg=qtg: e.matmul(
                            ps_pv[qtg % 2][0:65, :],
                            VsT[:, (2 * pg + 1) * 65:(2 * pg + 1) * 65 + 65],
                            PT[:, (pg % 4) * 1024 + 512:(pg % 4) * 1024 + 1024],
                            start=False, stop=(pg == NGQ - 1)), sPVC)
                sch.w(v, sPVC, NGQ * (qtg + 1))
                if qtg >= 1:
                    W(v, ("recwr", qtg - 1))
                rc = sch.add(v, lambda e, qtg=qtg: e.reciprocal(
                    rec_sb[0:1, :], ps_pv[qtg % 2][64:65, :]), sREC)
                sch.w(g, sREC, rc)
                wr = sch.add(g, lambda e: e.dma_start(
                    AP(recscrh, 0, [[512, 1], [1, 512]]), rec_sb[0:1, :]), sG, 16)
                wp[("recwr", qtg)] = (sG, wr)
                sch.w(g, sG, wr)
                if qtg >= 1:
                    sch.w(g, sVN, qtg)
                rd = sch.add(g, lambda e, qtg=qtg: e.dma_start(
                    rec_bc[0:64, 0:512],
                    AP(recscrh, 0, [[0, 64], [1, 512]])), sG, 16)
                sch.w(v, sG, rd)
                if qt == 0 and b > 0:
                    W(v, ("outproj", b - 1))
                vn = sch.add(v, lambda e, qtg=qtg, qt=qt: e.scalar_tensor_tensor(
                    A2[0:64, qt * 512:qt * 512 + 512], ps_pv[qtg % 2][0:64, :], 1.0,
                    rec_bc[0:64, 0:512],
                    ALU.mult, ALU.mult), sVN)
                sch.w(g, sVN, vn)
                ad = sch.add(g, lambda e, qt=qt: e.dma_start(
                    A2[64:128, qt * 512:qt * 512 + 512],
                    A2[0:64, qt * 512:qt * 512 + 512]), sG, 16)
                wp[("a2dup", qtg)] = (sG, ad)
                emit_qstep()
                emit_qstep()
                if 8 <= qt < 12:
                    emit_qstep()
                    emit_qstep()

            for tc in range(4):
                if b == NB - 1 and tc == 2:
                    sch.w(s, sPVC, NB * QTB * NGQ)
                    sch.w(s, sQS, sch.v(sQS))
                    sch.w(s, sG, wp[("a2dup", NB * QTB - 1)][1])
                    for dc in range(8):
                        sch.add(s, lambda e, dc=dc: e.dma_start_transpose(
                            WqT1[:, dc * 4096:dc * 4096 + 4096],
                            AP(wqt1h, dc * 128, [[1024, 4096], [1, 128]])), sWQ1, 16)
                    wp[("wqt1",)] = (sWQ1, sch.v(sWQ1))
                if b > 0 or tc >= 2:
                    bb, tt = (b, tc - 2) if tc >= 2 else (b - 1, tc + 2)
                    W(s, ("s1evac", bb, tt))
                ld = sch.add(s, lambda e, b=b, tc=tc: e.dma_start(
                    x_tc[:, (tc % 2) * 1024:(tc % 2) * 1024 + 1024],
                    AP(xh, (b * 512 + tc * 128) * 1024, [[1024, 128], [1, 1024]])),
                    sXL[tc % 2], 16)
                wp[("xload", b, tc)] = (sXL[tc % 2], ld)
                for dh in range(2):
                    n = tc * 2 + dh
                    u = (n % 2) * 2
                    if n == 0:
                        if b == 0:
                            W(t, ("w_wo",))
                            W(t, ("w_bo",))
                        sch.w(t, sVN, b * QTB + QTB)
                        sch.w(t, sG, wp[("a2dup", b * QTB + QTB - 1)][1])
                    if n >= 2:
                        W(t, ("opev", b, n - 2))
                    sch.add(t, lambda e, dh=dh, u=u: e.matmul(
                        ps_sb[:, u * 512:u * 512 + 512], ones[0:1, :],
                        bo_row[0:1, dh * 512:dh * 512 + 512],
                        start=True, stop=False, tile_position=(0, 0)), None)
                    for m in range(8):
                        for par in range(2):
                            r = 2 * m + par
                            lo = par * 64 * 8192
                            tm = sch.add(t, lambda e, r=r, m=m, tc=tc, dh=dh, lo=lo, par=par, u=u: e.matmul(
                                ps_sb[:, (u + par) * 512:(u + par) * 512 + 512],
                                AP(A2, lo + 2048 * tc + r, [[8192, 64], [16, 128]]),
                                AP(Wo_sb, lo + m * 1024 + dh * 512, [[8192, 64], [1, 512]]),
                                start=(par == 1 and m == 0), stop=(m == 7),
                                tile_position=(64 * par, 0)), sTT if r == 15 else None)
                    wp[("opmm", b, tc, dh)] = (sTT, tm)
                    W(v, ("opmm", b, tc, dh))
                    sch.w(v, wp[("xload", b, tc)][0], wp[("xload", b, tc)][1])
                    sch.add_sync(v, lambda e, tc=tc, dh=dh, u=u: e.scalar_tensor_tensor(
                        s1[:, (tc % 2) * 1024 + dh * 512:(tc % 2) * 1024 + dh * 512 + 512],
                        ps_sb[:, u * 512:u * 512 + 512], 1.0,
                        x_tc[:, (tc % 2) * 1024 + dh * 512:(tc % 2) * 1024 + dh * 512 + 512],
                        ALU.mult, ALU.add), sVE)
                    ev = sch.add_sync(v, lambda e, tc=tc, dh=dh, u=u: e.scalar_tensor_tensor(
                        s1[:, (tc % 2) * 1024 + dh * 512:(tc % 2) * 1024 + dh * 512 + 512],
                        ps_sb[:, (u + 1) * 512:(u + 1) * 512 + 512], 1.0,
                        s1[:, (tc % 2) * 1024 + dh * 512:(tc % 2) * 1024 + dh * 512 + 512],
                        ALU.mult, ALU.add), sVE)
                    wp[("opev", b, n)] = (sVE, ev)
                    wp[("s1evac", b, tc)] = (sVE, ev)
                if tc == 3:
                    mark(("outproj", b), sTT)
                def emit_ln1_a(b=b, tc=tc):
                    sl = (tc % 2) * 1024
                    if b > 0 or tc >= 2:
                        bb, tt = (b, tc - 2) if tc >= 2 else (b - 1, tc + 2)
                        W(v, ("x1w", bb, tt))
                    sch.add_sync(v, lambda e, sl=sl: e.reduce_sum(
                        negsum[:, :], s1[:, sl:sl + 1024], AX.X, negate=True), sVE)
                    sch.add_sync(v, lambda e: e.tensor_scalar_mul(
                        negmean[:, :], negsum[:, :], 1.0 / 1024.0), sVE)
                    sch.add_sync(v, lambda e, sl=sl: e.tensor_scalar(
                        s1[:, sl:sl + 1024], s1[:, sl:sl + 1024],
                        negmean[:, :], None, ALU.add), sVE)
                    sch.add_sync(v, lambda e, sl=sl: e.scalar_tensor_tensor(
                        x1[:, sl:sl + 1024], s1[:, sl:sl + 1024], 1.0,
                        s1[:, sl:sl + 1024], ALU.mult, ALU.mult,
                        accum_out=varsum[:, :]), sVE)
                    lv = sch.add(v, lambda e, tc=tc: e.tensor_scalar(
                        lnva[:, tc:tc + 1], varsum[:, :], 1.0 / 1024.0, LN_EPS,
                        ALU.mult, ALU.add), sVE)
                    return lv

                def emit_ln1_b(lv, b=b, tc=tc):
                    sl = (tc % 2) * 1024
                    sch.w(sc, sVE, lv)
                    sch.add_sync(sc, lambda e, tc=tc: e.activation(
                        uln[:, :], lnva[:, tc:tc + 1], AF.Ln), sSC)
                    sr = sch.add(sc, lambda e: e.activation(
                        rstd[:, :], uln[:, :], AF.Exp, scale=-0.5), sSC)
                    sch.w(v, sSC, sr)
                    x1v = sch.add(v, lambda e, sl=sl: e.tensor_scalar(
                        x1[:, sl:sl + 1024], s1[:, sl:sl + 1024],
                        rstd[:, :], None, ALU.mult), sVE)
                    gq = b == NB - 1 and tc < 2
                    weng = g if gq else s
                    xsem = sXW3 if gq else sX1W[tc % 2]
                    sch.w(weng, sVE, x1v)
                    xw = sch.add(weng, lambda e, b=b, tc=tc, sl=sl: e.dma_start(
                        AP(x1scrh, (b * 512 + tc * 128) * 1024, [[1024, 128], [1, 1024]]),
                        x1[:, sl:sl + 1024]), xsem, 16)
                    wp[("x1w", b, tc)] = (xsem, 32 if gq else xw)
                if tc < 2:
                    emit_ln1_b(emit_ln1_a())
                else:
                    deferred_ln.append((emit_ln1_a, emit_ln1_b))

        while qstate["i"] < len(qsteps):
            emit_qstep()

        ATT_T = sch.v(sTT); ATT_V = sch.v(sVE); ATT_G = sch.v(sG); ATT_S = sch.v(sS)


        def emit_x1t(b):
            sch.w(s, sCB[b], 16)
            if b == 0:
                W(s, ("outproj", NB - 1))
                W(s, ("s1evac", NB - 1, 3))
            if b > 0:
                W(s, ("w1done", b - 1))
            for dc in range(8):
                sch.add(s, lambda e, b=b, dc=dc: e.dma_start_transpose(
                    x1T[:, dc * 512:dc * 512 + 512],
                    AP(x1bfh, b * 524288 + dc * 128, [[1024, 512], [1, 128]])), sX1T, 16)
            wp[("x1T", b)] = (sX1T, sch.v(sX1T))

        for b in range(NB):
            if b == 0:
                emit_x1t(0)
            if b == 0:
                W(s, ("outproj", NB - 1))
                sch.w(s, sQS2, sch.v(sQS2))
                for oc in range(32):
                    sch.add(s, lambda e, oc=oc: e.dma_start_transpose(
                        WqT2[:, oc * 1024:oc * 1024 + 1024],
                        AP(wq2h, oc * 128, [[4096, 1024], [1, 128]])), sWQT, 16)
                wp[("wqt",)] = (sWQT, sch.v(sWQT))

            for oc in range(32):
                W(t, ("x1T", b))
                W(t, ("wqt1",))
                if oc >= 2:
                    W(t, ("hevac", b, oc - 2))
                elif b > 0:
                    W(t, ("hevac", b - 1, oc + 30))
                for dc in range(8):
                    tm = sch.add(t, lambda e, oc=oc, dc=dc: e.matmul(
                        ps_pv[oc % 2][:, :],
                        WqT1[:, dc * 4096 + oc * 128:dc * 4096 + oc * 128 + 128],
                        x1T[:, dc * 512:dc * 512 + 512],
                        start=(dc == 0), stop=(dc == 7)), sTT if dc == 7 else None)
                wp[("w1mm", b, oc)] = (sTT, tm)
                W(v, ("w1mm", b, oc))
                if b > 0 and oc < 2:
                    W(v, ("w2done", b - 1))
                if b == 0 and oc == 0:
                    sch.w(v, sWG[4], 36 * 16)
                ev = sch.add(v, lambda e, oc=oc: e.tensor_scalar(
                    hT[:, oc * 512:oc * 512 + 512], ps_pv[oc % 2][:, :],
                    b1c[:, oc:oc + 1], 0.0, ALU.add, ALU.max), sVE)
                wp[("hevac", b, oc)] = (sVE, ev)
            mark(("w1done", b), sTT)
            if b + 1 < NB:
                emit_x1t(b + 1)
            if b == 0:
                for fa, fb in deferred_ln:
                    fb(fa())
                deferred_ln.clear()
                bb = NB - 1
                sch.w(g, sX1W[0], wp[("x1w", bb, 2)][1])
                sch.w(g, sX1W[1], wp[("x1w", bb, 3)][1])
                sch.add(g, lambda e: e.dma_start(
                    AP(x1bfh, bb * 524288, [[1024, 512], [1, 1024]]),
                    AP(x1scrh, bb * 524288, [[1024, 512], [1, 1024]])), sCB[bb], 16)

            for tsub in range(4):
                if b > 0 or tsub >= 2:
                    bb, tt = (b, tsub - 2) if tsub >= 2 else (b - 1, tsub + 2)
                    W(s, ("s2done", bb, tt))
                else:
                    sch.w(s, sX1W[0], wp[("x1w", NB - 1, 2)][1])
                    sch.w(s, sX1W[1], wp[("x1w", NB - 1, 3)][1])
                ld = sch.add(s, lambda e, b=b, tsub=tsub: e.dma_start(
                    x1f[:, (tsub % 2) * 1024:(tsub % 2) * 1024 + 1024],
                    AP(x1scrh, (b * 512 + tsub * 128) * 1024, [[1024, 128], [1, 1024]])),
                    sX1F[tsub % 2], 16)
                wp[("x1fload", b, tsub)] = (sX1F[tsub % 2], ld)
                for dh in range(2):
                    W(t, ("hevac", b, 31))
                    if b == 0 and tsub == 0 and dh == 0:
                        W(t, ("wqt",))
                        W(t, ("wqt1",))
                    if b * 4 + tsub >= 2:
                        pb, pt = divmod(b * 4 + tsub - 2, 4)
                        W(t, ("s2evac", pb, pt))
                    sch.add(t, lambda e, tsub=tsub, dh=dh: e.matmul(
                        ps_sb[:, (tsub % 2) * 1024 + dh * 512:(tsub % 2) * 1024 + dh * 512 + 512],
                        ones[0:1, :], b2_row[0:1, dh * 512:dh * 512 + 512],
                        start=True, stop=False), None)
                    for oc in range(32):
                        tm = sch.add(t, lambda e, tsub=tsub, dh=dh, oc=oc: e.matmul(
                            ps_sb[:, (tsub % 2) * 1024 + dh * 512:(tsub % 2) * 1024 + dh * 512 + 512],
                            hT[:, oc * 512 + tsub * 128:oc * 512 + tsub * 128 + 128],
                            WqT2[:, oc * 1024 + dh * 512:oc * 1024 + dh * 512 + 512],
                            start=False, stop=(oc == 31)),
                            sTT if (dh == 1 and oc == 31) else None)
                wp[("w2mm", b, tsub)] = (sTT, tm)
                W(v, ("w2mm", b, tsub))
                sch.w(v, sX1F[tsub % 2], ld)
                if b * 4 + tsub >= 1:
                    pb, pt = divmod(b * 4 + tsub - 1, 4)
                    W(v, ("outdma", pb, pt))
                sl = 0
                xsl = (tsub % 2) * 1024
                if b == 0 and tsub == 0:
                    sch.w(v, sX1W[0], wp[("x1w", NB - 1, 2)][1])
                    sch.w(v, sX1W[1], wp[("x1w", NB - 1, 3)][1])
                ev = sch.add_sync(v, lambda e, tsub=tsub, xsl=xsl: e.scalar_tensor_tensor(
                    s2[:, 0:1024], ps_sb[:, (tsub % 2) * 1024:(tsub % 2) * 1024 + 1024], 1.0,
                    x1f[:, xsl:xsl + 1024], ALU.mult, ALU.add), sVE)
                wp[("s2evac", b, tsub)] = (sVE, ev)
                wp[("s2done", b, tsub)] = (sVE, ev)
                sch.add_sync(v, lambda e: e.reduce_sum(
                    negsum[:, :], s2[:, 0:1024], AX.X, negate=True), sVE)
                sch.add_sync(v, lambda e: e.tensor_scalar_mul(
                    negmean[:, :], negsum[:, :], 1.0 / 1024.0), sVE)
                sch.add_sync(v, lambda e: e.tensor_scalar(
                    s2[:, 0:1024], s2[:, 0:1024],
                    negmean[:, :], None, ALU.add), sVE)
                sch.add_sync(v, lambda e: e.scalar_tensor_tensor(
                    outb[:, 0:1024], s2[:, 0:1024], 1.0,
                    s2[:, 0:1024], ALU.mult, ALU.mult,
                    accum_out=varsum[:, :]), sVE)
                lv = sch.add(v, lambda e: e.tensor_scalar(
                    vareps[:, :], varsum[:, :], 1.0 / 1024.0, LN_EPS,
                    ALU.mult, ALU.add), sVE)
                sch.w(sc, sVE, lv)
                sch.add_sync(sc, lambda e: e.activation(uln[:, :], vareps[:, :], AF.Ln), sSC)
                sr = sch.add(sc, lambda e: e.activation(
                    rstd[:, :], uln[:, :], AF.Exp, scale=-0.5), sSC)
                sch.w(v, sSC, sr)
                ov = sch.add(v, lambda e: e.tensor_scalar(
                    outb[:, 0:1024], s2[:, 0:1024],
                    rstd[:, :], None, ALU.mult), sVE)
                wp[("outw", b, tsub)] = (sVE, ov)
                sch.w(s, sVE, ov)
                od = sch.add(s, lambda e, b=b, tsub=tsub: e.dma_start(
                    AP(yh, (b * 512 + tsub * 128) * 1024, [[1024, 128], [1, 1024]]),
                    outb[:, 0:1024]), sOUT, 16)
                wp[("outdma", b, tsub)] = (sOUT, od)
            mark(("w2done", b), sTT)

        sch.w(s, sOUT, sch.v(sOUT))
        sch.w(s, sX1T, sch.v(sX1T))
        for i in range(2):
            sch.w(s, sX1F[i], sch.v(sX1F[i]))
        sch.w(g, sG, sch.v(sG))
        for i in range(4):
            sch.w(g, sCB[i], 16)
        sch.w(g, sXW3, sch.v(sXW3))

        @blk.sync
        def _(e):
            for f in sch.L["sync"]:
                f(e)

        @blk.gpsimd
        def _(e):
            for f in sch.L["gpsimd"]:
                f(e)

        @blk.tensor
        def _(e):
            for f in sch.L["tensor"]:
                f(e)

        @blk.vector
        def _(e):
            for f in sch.L["vector"]:
                f(e)

        @blk.scalar
        def _(e):
            for f in sch.L["scalar"]:
                f(e)

    return nc


def kernel(L0_attn_x, L0_attn_Wq, L0_attn_bq, L0_attn_Wk, L0_attn_bk,
           L0_attn_Wv, L0_attn_bv, L0_attn_Wo, L0_attn_bo,
           L0_ffn_W1, L0_ffn_b1, L0_ffn_W2, L0_ffn_b2):
    x = np.ascontiguousarray(np.asarray(L0_attn_x, dtype=np.float32)).reshape(16384, 1024)
    com = dict(
        Wq=np.ascontiguousarray(np.asarray(L0_attn_Wq, np.float32)),
        bq=np.ascontiguousarray(np.asarray(L0_attn_bq, np.float32)),
        Wk=np.ascontiguousarray(np.asarray(L0_attn_Wk, np.float32)),
        bk=np.ascontiguousarray(np.asarray(L0_attn_bk, np.float32)),
        Wv=np.ascontiguousarray(np.asarray(L0_attn_Wv, np.float32)),
        bv=np.ascontiguousarray(np.asarray(L0_attn_bv, np.float32)),
        Wo=np.ascontiguousarray(np.asarray(L0_attn_Wo, np.float32)),
        bo=np.ascontiguousarray(np.asarray(L0_attn_bo, np.float32)),
        W1=np.ascontiguousarray(np.asarray(L0_ffn_W1, np.float32)),
        b1=np.ascontiguousarray(np.asarray(L0_ffn_b1, np.float32)),
        W2=np.ascontiguousarray(np.asarray(L0_ffn_W2, np.float32)),
        b2=np.ascontiguousarray(np.asarray(L0_ffn_b2, np.float32)),
        eye=np.eye(128, dtype=np.float32),
    )
    nc = build_nc()
    in_maps = [dict(com, x=np.ascontiguousarray(x[c * 2048:(c + 1) * 2048]))
               for c in range(8)]
    res = run_bass_kernel_spmd(nc, in_maps, core_ids=list(range(8)))
    out = np.concatenate([np.asarray(r["y"], np.float32) for r in res.results], axis=0)
    return out.reshape(2, 8192, 1024)
